# version 03
# baseline (speedup 1.0000x reference)
import contextlib
import numpy as np
import ml_dtypes
import concourse.bass as bass
import concourse.mybir as mybir
from concourse.bass_utils import run_bass_kernel_spmd

F32 = mybir.dt.float32
BF16 = mybir.dt.bfloat16
AF = mybir.ActivationFunctionType
ALU = mybir.AluOpType
AX = mybir.AxisListType.X
BF = ml_dtypes.bfloat16

TH, NT = 4352, 34
TOWN, NTO, NG = 4096, 32, 8
D, NDC = 1024, 8
H, E = 8, 4
LANE, NLANE = 256, 17
EXTC, NCH = 4224, 33
DFF = 4096


def build_nc(debug=False):
    nc = bass.Bass("TRN2", target_bir_lowering=False)

    xs = nc.dram_tensor("xs", [TH, D], F32, kind="ExternalInput").ap()
    wkvg = nc.dram_tensor("wkvg", [D, 1056], F32, kind="ExternalInput").ap()
    wqd = nc.dram_tensor("wqd", [D, 512], F32, kind="ExternalInput").ap()
    wod = nc.dram_tensor("wod", [512, D], F32, kind="ExternalInput").ap()
    w1d = nc.dram_tensor("w1d", [D, DFF], F32, kind="ExternalInput").ap()
    w2d = nc.dram_tensor("w2d", [DFF, D], F32, kind="ExternalInput").ap()
    b1d = nc.dram_tensor("b1d", [128, 32], F32, kind="ExternalInput").ap()
    b2d = nc.dram_tensor("b2d", [8, D], F32, kind="ExternalInput").ap()
    rwd = nc.dram_tensor("rwd", [D, 8], F32, kind="ExternalInput").ap()
    rbd = nc.dram_tensor("rbd", [1, 8], F32, kind="ExternalInput").ap()
    injd = nc.dram_tensor("injd", [128, 68], F32, kind="ExternalInput").ap()
    extd = nc.dram_tensor("extd", [128, 128], F32, kind="ExternalInput").ap()
    fhd = nc.dram_tensor("fhd", [128, 1], F32, kind="ExternalInput").ap()
    bgd = nc.dram_tensor("bgd", [128, 32], F32, kind="ExternalInput").ap()
    mskd = nc.dram_tensor("mskd", [128, 160], F32, kind="ExternalInput").ap()
    i4d = nc.dram_tensor("i4d", [128, 512], F32, kind="ExternalInput").ap()
    idfd = nc.dram_tensor("idfd", [128, 128], F32, kind="ExternalInput").ap()
    on1d = nc.dram_tensor("on1d", [1, 128], F32, kind="ExternalInput").ap()
    epsd = nc.dram_tensor("epsd", [128, 1], F32, kind="ExternalInput").ap()
    ond2 = nc.dram_tensor("ond2", [128, 128], F32, kind="ExternalInput").ap()

    KI = "ExternalOutput" if debug else "Internal"
    scr_q = nc.dram_tensor("scr_q", [4, 128, TOWN], BF16, kind=KI).ap()
    scr_y = nc.dram_tensor("scr_y", [4, 128, TOWN], BF16, kind=KI).ap()
    scr_x1 = nc.dram_tensor("scr_x1", [TOWN, D], F32, kind=KI).ap()
    scr_xf = nc.dram_tensor("scr_xf", [NDC, 128, TOWN], BF16, kind=KI).ap()
    scr_o1 = nc.dram_tensor("scr_o1", [NDC, 128, TOWN], BF16, kind=KI).ap()
    if debug:
        dbg_g = nc.dram_tensor("dbg_g", [128, NT * 32], F32, kind="ExternalOutput").ap()
        dbg_a = nc.dram_tensor("dbg_a", [128, NT * 4], F32, kind="ExternalOutput").ap()
        dbg_ext = nc.dram_tensor("dbg_ext", [128, E * EXTC], F32, kind="ExternalOutput").ap()
        dbg_pt = nc.dram_tensor("dbg_pt", [128, E * NCH * 160], F32, kind="ExternalOutput").ap()
        dbg_r = nc.dram_tensor("dbg_r", [128, NTO * 8], F32, kind="ExternalOutput").ap()
        dbg_gkv = nc.dram_tensor("dbg_gkv", [128, E * TH], F32, kind="ExternalOutput").ap()
        dbg_abc = nc.dram_tensor("dbg_abc", [128, TH], F32, kind="ExternalOutput").ap()
        dbg_nkv = nc.dram_tensor("dbg_nkv", [128, TH], F32, kind="ExternalOutput").ap()
    y = nc.dram_tensor("y", [TOWN, D], F32, kind="ExternalOutput").ap()

    es = contextlib.ExitStack()
    semnames = [
        "dW", "dX", "dXT", "dQo", "dQi", "dVT", "dYo", "dYi", "dXR", "dX1s",
        "dX1T", "dXFs", "dXG", "dO1o", "dO1i", "dY", "dWF", "dXF", "dFT",
        "sKVG", "sPre", "sG", "sDiag", "sGKV", "sEvGKV", "sEvKV",
        "sA", "sQ", "sEvQ", "sDgA", "sBcA", "sEvA", "sScan", "sExt",
        "sSc", "sExp", "sMkV", "sMkG", "sPV", "sRcp", "sBcD", "sEvD", "sY",
        "sOP", "sEvOP", "sRes1", "sBN1", "sSq1", "sRst1", "sNrm1", "sX1b",
        "sRt", "sEvRt", "sRtd", "sSig", "sTop", "sRT", "sEvRT",
        "sDgR", "sBcR", "sEvR", "sH1", "sGel", "sScl", "sW2", "sEvW2",
        "sAdd2", "sBN2", "sSq2", "sRst2", "sNrm2", "dDbg", "sVs", "sAr", "sAh",
    ]
    S = {n: es.enter_context(nc.semaphore(n)) for n in semnames}
    _vs = {"n": 0}
    def VS(v, ins):
        _vs["n"] += 1
        ins.then_inc(S["sVs"], 1)
        v.wait_ge(S["sVs"], _vs["n"])

    i4 = es.enter_context(nc.sbuf_tensor("i4", [128, 512], BF16))
    idf = es.enter_context(nc.sbuf_tensor("idf", [128, 128], F32))
    on1 = es.enter_context(nc.sbuf_tensor("on1", [1, 128], BF16))
    onf = es.enter_context(nc.sbuf_tensor("onf", [128, 128], F32))
    onb = es.enter_context(nc.sbuf_tensor("onb", [128, 128], BF16))
    msk = es.enter_context(nc.sbuf_tensor("msk", [128, 160], BF16))
    fh = es.enter_context(nc.sbuf_tensor("fh", [128, 1], F32))
    epsb = es.enter_context(nc.sbuf_tensor("epsb", [128, 1], F32))
    bg = es.enter_context(nc.sbuf_tensor("bg", [128, 32], F32))
    inj = es.enter_context(nc.sbuf_tensor("inj", [128, 68], F32))
    extov = es.enter_context(nc.sbuf_tensor("extov", [128, 128], F32))
    rw_sb = es.enter_context(nc.sbuf_tensor("rw_sb", [128, 64], BF16))
    rb_sb = es.enter_context(nc.sbuf_tensor("rb_sb", [1, 8], BF16))
    b1_sb = es.enter_context(nc.sbuf_tensor("b1_sb", [128, 32], F32))
    b2_sb = es.enter_context(nc.sbuf_tensor("b2_sb", [8, 1024], BF16))
    g_tm = es.enter_context(nc.sbuf_tensor("g_tm", [128, NT * 32], F32))
    a_tm = es.enter_context(nc.sbuf_tensor("a_tm", [128, NT * 4], F32))
    a_hi = es.enter_context(nc.sbuf_tensor("a_hi", [128, NT * 4], BF16))
    a_lo = es.enter_context(nc.sbuf_tensor("a_lo", [128, NT * 4], F32))
    a_hif = es.enter_context(nc.sbuf_tensor("a_hif", [128, NT * 4], F32))
    stat = es.enter_context(nc.sbuf_tensor("stat", [128, 96], F32))
    r_tm = es.enter_context(nc.sbuf_tensor("r_tm", [128, NTO * 8], F32))
    rstat = es.enter_context(nc.sbuf_tensor("rstat", [128, 128], F32))
    lbuf = es.enter_context(nc.sbuf_tensor("lbuf", [128, 16], F32))
    rcast = es.enter_context(nc.sbuf_tensor("rcast", [128, 16], BF16))

    pall = es.enter_context(nc.psum_tensor("pall", [128, 3584], F32))
    pbf = es.enter_context(nc.psum_tensor("pbf", [128, 192], BF16))

    def PB(k, off=0, w=512, p0=0, pn=128):
        return pall[p0:p0 + pn, k * 512 + off:k * 512 + off + w]

    blk = es.enter_context(nc.Block())

    esA2 = contextlib.ExitStack()
    gkv_fm = esA2.enter_context(nc.sbuf_tensor("gkv_fm", [128, E * TH], BF16, side="right"))
    esA1 = contextlib.ExitStack()
    x_tm = esA1.enter_context(nc.sbuf_tensor("x_tm", [128, 2 * 1024], BF16))
    x_fm = esA1.enter_context(nc.sbuf_tensor("x_fm", [128, NDC * TH], BF16,
                                             is_transpose_dest=True))
    wkvg_sb = esA1.enter_context(nc.sbuf_tensor("wkvg_sb", [128, NDC * 1056], BF16))
    wq_sb = esA1.enter_context(nc.sbuf_tensor("wq_sb", [128, NDC * 512], BF16))
    kv_tm = esA1.enter_context(nc.sbuf_tensor("kv_tm", [128, 2 * 1024], BF16))
    dg = esA1.enter_context(nc.sbuf_tensor("dg", [128, 4096], BF16))
    gpre = esA1.enter_context(nc.sbuf_tensor("gpre", [128, 64], F32))
    qev = esA1.enter_context(nc.sbuf_tensor("qev", [128, 2 * 512], BF16))

    @blk.gpsimd
    def _(gp):
        for dst, src in [(i4[:], i4d), (idf[:], idfd), (on1[0:1, :], on1d),
                         (onf[:], ond2), (onb[:], ond2), (msk[:], mskd),
                         (fh[:], fhd), (epsb[:], epsd), (bg[:], bgd), (inj[:], injd),
                         (extov[:], extd), (rb_sb[0:1, :], rbd),
                         (b1_sb[:], b1d), (b2_sb[0:8, :], b2d)]:
            gp.dma_start(out=dst, in_=src).then_inc(S["dW"], 16)
        for dc in range(NDC):
            gp.dma_start(out=rw_sb[:, dc * 8:dc * 8 + 8],
                         in_=rwd[dc * 128:dc * 128 + 128, :]).then_inc(S["dW"], 16)
            gp.dma_start(out=wkvg_sb[:, dc * 1056:dc * 1056 + 1056],
                         in_=wkvg[dc * 128:dc * 128 + 128, :]).then_inc(S["dW"], 16)
            gp.dma_start(out=wq_sb[:, dc * 512:dc * 512 + 512],
                         in_=wqd[dc * 128:dc * 128 + 128, :]).then_inc(S["dW"], 16)
        for i in range(NT):
            if i >= 2:
                gp.wait_ge(S["dXT"], (i - 1) * 128)
            gp.dma_start(out=x_tm[:, (i % 2) * 1024:(i % 2) * 1024 + 1024],
                         in_=xs[i * 128:i * 128 + 128, :]).then_inc(S["dX"], 16)
        for i in range(NT):
            gp.wait_ge(S["sG"], i + 1)
            if i >= 1:
                gp.wait_ge(S["sGKV"], i)
            for hh in range(4, 8):
                for e in range(E):
                    ins = gp.tensor_scalar(
                        dg[:, hh * 512 + e * 128:hh * 512 + e * 128 + 128],
                        i4[:, 0:128],
                        g_tm[:, i * 32 + e * 8 + hh:i * 32 + e * 8 + hh + 1],
                        None, ALU.mult)
            ins.then_inc(S["sDiag"], 1)
    NW0 = 38

    @blk.scalar
    def _(sc):
        for i in range(NT):
            sc.wait_ge(S["dX"], (i + 1) * 16)
            for dc in range(NDC):
                sc.dma_start(out=x_fm[:, dc * TH + i * 128:dc * TH + i * 128 + 128],
                             in_=x_tm[:, (i % 2) * 1024 + dc * 128:(i % 2) * 1024 + dc * 128 + 128],
                             transpose=True).then_inc(S["dXT"], 16)

    @blk.tensor
    def _(t):
        t.wait_ge(S["dW"], NW0 * 16)
        nq = 0
        for i in range(NT):
            t.wait_ge(S["dXT"], (i + 1) * 128)
            t.wait_ge(S["sPre"], i)
            if i >= 2:
                t.wait_ge(S["sEvKV"], (i - 1) * 2)
            for dc in range(NDC):
                lhs = x_fm[:, dc * TH + i * 128:dc * TH + i * 128 + 128]
                t.matmul(PB(0 + i % 2), lhs, wkvg_sb[:, dc * 1056:dc * 1056 + 512],
                         start=(dc == 0), stop=(dc == NDC - 1))
                t.matmul(PB(2 + i % 2), lhs, wkvg_sb[:, dc * 1056 + 512:dc * 1056 + 1024],
                         start=(dc == 0), stop=(dc == NDC - 1))
                mm = t.matmul(PB(5, 0, 32), lhs,
                              wkvg_sb[:, dc * 1056 + 1024:dc * 1056 + 1056],
                              start=(dc == 0), stop=(dc == NDC - 1))
            mm.then_inc(S["sKVG"], 1)
            for ii in ([i - 1] if i >= 1 else []) + ([NT - 1] if i == NT - 1 else []):
                t.wait_ge(S["sDiag"], 2 * ii + 2)
                t.wait_ge(S["sEvKV"], 2 * ii + 2)
                if ii >= 1:
                    t.wait_ge(S["sEvGKV"], ii)
                for hh in range(H):
                    mm = t.matmul(PB(4),
                                  kv_tm[:, (ii % 2) * 1024 + hh * 128:(ii % 2) * 1024 + hh * 128 + 128],
                                  dg[:, hh * 512:hh * 512 + 512],
                                  start=(hh == 0), stop=(hh == H - 1))
                mm.then_inc(S["sGKV"], 1)
            if i >= 5 and (i - 5) % 4 == 0:
                gg = (i - 5) // 4
                for qc in range(4):
                    if nq >= 1:
                        t.wait_ge(S["sEvQ"], nq)
                    for dc in range(NDC):
                        mm = t.matmul(PB(6), wq_sb[:, dc * 512 + qc * 128:dc * 512 + qc * 128 + 128],
                                      x_fm[:, dc * TH + 256 + gg * 512:dc * TH + 256 + gg * 512 + 512],
                                      start=(dc == 0), stop=(dc == NDC - 1))
                    mm.then_inc(S["sQ"], 1)
                    nq += 1

    @blk.vector
    def _(v):
        nq = 0
        for i in range(NT):
            v.wait_ge(S["sKVG"], i + 1)
            v.tensor_tensor(gpre[:, (i % 2) * 32:(i % 2) * 32 + 32],
                            PB(5, 0, 32), bg[:], ALU.add).then_inc(S["sPre"], 1)
            if i >= 2:
                v.wait_ge(S["sGKV"], i - 1)
            v.tensor_copy(kv_tm[:, (i % 2) * 1024:(i % 2) * 1024 + 512],
                          PB(0 + i % 2)).then_inc(S["sEvKV"], 1)
            v.tensor_copy(kv_tm[:, (i % 2) * 1024 + 512:(i % 2) * 1024 + 1024],
                          PB(2 + i % 2)).then_inc(S["sEvKV"], 1)
            v.wait_ge(S["sG"], i + 1)
            if i >= 1:
                v.wait_ge(S["sGKV"], i)
            for hh in range(4):
                for e in range(E):
                    ins = v.tensor_scalar(
                        dg[:, hh * 512 + e * 128:hh * 512 + e * 128 + 128],
                        i4[:, 0:128],
                        g_tm[:, i * 32 + e * 8 + hh:i * 32 + e * 8 + hh + 1],
                        None, ALU.mult)
            ins.then_inc(S["sDiag"], 1)
            if i >= 2:
                v.wait_ge(S["sA"], i - 1)
            ab = 64 + (i % 2) * 8
            for e in range(E):
                ins = v.reduce_sum(stat[:, ab + e:ab + e + 1],
                                   g_tm[:, i * 32 + e * 8:i * 32 + e * 8 + 8],
                                   axis=AX, negate=True)
            ins.then_inc(S["sAr"], 1)
            for ii0 in ([i - 1] if i >= 1 else []) + ([NT - 1] if i == NT - 1 else []):
                v.wait_ge(S["sA"], ii0 + 1)
                ins = v.tensor_copy(a_hi[:, ii0 * 4:ii0 * 4 + 4], a_tm[:, ii0 * 4:ii0 * 4 + 4])
                VS(v, ins)
                ins = v.tensor_copy(a_hif[:, ii0 * 4:ii0 * 4 + 4], a_hi[:, ii0 * 4:ii0 * 4 + 4])
                VS(v, ins)
                v.tensor_sub(a_lo[:, ii0 * 4:ii0 * 4 + 4], a_tm[:, ii0 * 4:ii0 * 4 + 4],
                             a_hif[:, ii0 * 4:ii0 * 4 + 4]).then_inc(S["sAh"], 1)
            for ii in ([i - 1] if i >= 1 else []) + ([NT - 1] if i == NT - 1 else []):
                v.wait_ge(S["sGKV"], ii + 1)
                dst = gkv_fm[:].rearrange("p (e t) -> p e t", e=E)[:, :, ii * 128:ii * 128 + 128]
                v.tensor_copy(dst, PB(4)).then_inc(S["sEvGKV"], 1)
            if i >= 5 and (i - 5) % 4 == 0:
                for qc in range(4):
                    v.wait_ge(S["sQ"], nq + 1)
                    if nq >= 2:
                        v.wait_ge(S["dQo"], (nq - 1) * 16)
                    v.tensor_copy(qev[:, (nq % 2) * 512:(nq % 2) * 512 + 512],
                                  PB(6)).then_inc(S["sEvQ"], 1)
                    nq += 1

    @blk.scalar
    def _(sc):
        for i in range(NT):
            sc.wait_ge(S["sPre"], i + 1)
            sc.activation(g_tm[:, i * 32:i * 32 + 32],
                          gpre[:, (i % 2) * 32:(i % 2) * 32 + 32],
                          AF.Sigmoid, bias=0.0, scale=1.0).then_inc(S["sG"], 1)
            sc.wait_ge(S["sAr"], i + 1)
            sc.activation(a_tm[:, i * 4:i * 4 + 4],
                          stat[:, 64 + (i % 2) * 8:64 + (i % 2) * 8 + 4],
                          AF.Relu, bias=1.0, scale=1.0).then_inc(S["sA"], 1)

    @blk.sync
    def _(s):
        for k in range(32):
            gg, qc = divmod(k, 4)
            s.wait_ge(S["sEvQ"], k + 1)
            s.dma_start(out=scr_q[qc][0:128, gg * 512:gg * 512 + 512],
                        in_=qev[:, (k % 2) * 512:(k % 2) * 512 + 512]).then_inc(S["dQo"], 16)

    esA1.close()
    esB2 = contextlib.ExitStack()
    kvext = esB2.enter_context(nc.sbuf_tensor("kvext", [128, E * EXTC], BF16))
    vtm = esB2.enter_context(nc.sbuf_tensor("vtm", [128, E * NCH * 80], BF16,
                                            is_transpose_dest=True))
    esB1 = contextlib.ExitStack()
    a_bc = esB1.enter_context(nc.sbuf_tensor("a_bc", [128, TH], F32))
    dga = esB1.enter_context(nc.sbuf_tensor("dga", [128, 512], BF16))
    nkv = esB1.enter_context(nc.sbuf_tensor("nkv", [128, TH], F32))

    @blk.tensor
    def _(t):
        t.wait_ge(S["sEvGKV"], NT)
        for e in range(E):
            for i in range(NT):
                k = e * NT + i
                t.wait_ge(S["sDgA"], k + 1)
                if k >= 2:
                    t.wait_ge(S["sEvA"], k - 1)
                t.matmul(PB(4 + k % 2, 0, 128), onb[:],
                         dga[:, (k % 2) * 256:(k % 2) * 256 + 128],
                         start=True, stop=False)
                t.matmul(PB(4 + k % 2, 0, 128), onb[:],
                         dga[:, (k % 2) * 256 + 128:(k % 2) * 256 + 256],
                         start=False, stop=True).then_inc(S["sBcA"], 1)

    @blk.vector
    def _(v):
        for e in range(E):
            if debug and e == 1:
                v.wait_ge(S["dDbg"], 4 * 16)
            if e == 0:
                v.wait_ge(S["sQ"], 32)
            for i in range(NT):
                k = e * NT + i
                v.wait_ge(S["sAh"], i + 1)
                if k >= 2:
                    v.wait_ge(S["sBcA"], k - 1)
                v.tensor_scalar(dga[:, (k % 2) * 256:(k % 2) * 256 + 128],
                                i4[:, 0:128], a_hif[:, i * 4 + e:i * 4 + e + 1],
                                None, ALU.mult)
                v.tensor_scalar(dga[:, (k % 2) * 256 + 128:(k % 2) * 256 + 256],
                                i4[:, 0:128], a_lo[:, i * 4 + e:i * 4 + e + 1],
                                None, ALU.mult).then_inc(S["sDgA"], 1)
            v.wait_ge(S["sEvA"], (e + 1) * NT)
            acols = a_bc[:, 0:TH:LANE]
            ins = v.tensor_tensor(stat[:, 0:17], inj[:, e * 17:e * 17 + 17], acols, ALU.mult)
            VS(v, ins)
            gcols = gkv_fm[:, e * TH:e * TH + TH:LANE]
            ins = v.tensor_tensor(gcols, gcols, stat[:, 0:17], ALU.add)
            VS(v, ins)
            ins = v.memset(acols, 0.0)
            VS(v, ins)
            v.tensor_tensor_scan(nkv[:], a_bc[:], gkv_fm[:, e * TH:e * TH + TH],
                                 0.0, ALU.mult, ALU.add).then_inc(S["sScan"], 1)
            v.wait_ge(S["sScan"], e + 1)
            v.memset(kvext[:, e * EXTC + 4128:e * EXTC + 4224], 0.0)
            ins = v.tensor_sub(stat[:, 0:32], extov[:, e * 32:e * 32 + 32], nkv[:, 224:256])
            VS(v, ins)
            v.scalar_tensor_tensor(kvext[:, e * EXTC:e * EXTC + 32],
                                   stat[:, 0:32], fh[:, 0:1], nkv[:, 224:256],
                                   ALU.mult, ALU.add)
            v.tensor_copy(kvext[:, e * EXTC + 32:e * EXTC + 4128], nkv[:, 256:TH])
            ins = v.memset(vtm[:, e * NCH * 80 + 64:(e + 1) * NCH * 80:80], 1.0)
            ins.then_inc(S["sExt"], 1)

    @blk.scalar
    def _(sc):
        for e in range(E):
            if e >= 1:
                sc.wait_ge(S["sScan"], e)
            for i in range(NT):
                k = e * NT + i
                sc.wait_ge(S["sBcA"], k + 1)
                sc.activation(a_bc[:, i * 128:i * 128 + 128],
                              PB(4 + k % 2, 0, 128), AF.Copy,
                              bias=0.0, scale=1.0).then_inc(S["sEvA"], 1)

    @blk.scalar
    def _(sc):
        for e in range(E):
            sc.wait_ge(S["sExt"], e + 1)
            for c in range(NCH):
                sc.dma_start(out=vtm[:, (e * NCH + c) * 80:(e * NCH + c) * 80 + 64],
                             in_=kvext[64:128, e * EXTC + c * 128:e * EXTC + c * 128 + 128],
                             transpose=True).then_inc(S["dVT"], 16)

    if debug:
        @blk.gpsimd
        def _(gp):
            gp.wait_ge(S["sG"], NT)
            gp.dma_start(out=dbg_g, in_=g_tm[:]).then_inc(S["dDbg"], 16)
            gp.wait_ge(S["sA"], NT)
            gp.dma_start(out=dbg_a, in_=a_tm[:]).then_inc(S["dDbg"], 16)
            gp.wait_ge(S["sScan"], 1)
            gp.dma_start(out=dbg_abc, in_=a_bc[:]).then_inc(S["dDbg"], 16)
            gp.dma_start(out=dbg_nkv, in_=nkv[:]).then_inc(S["dDbg"], 16)
            gp.wait_ge(S["sScan"], E)
            gp.dma_start(out=dbg_gkv, in_=gkv_fm[:]).then_inc(S["dDbg"], 16)
            gp.wait_ge(S["sExt"], E)
            gp.dma_start(out=dbg_ext, in_=kvext[:]).then_inc(S["dDbg"], 16)

    esB1.close()
    esA2.close()
    esB3 = contextlib.ExitStack()
    qh = esB3.enter_context(nc.sbuf_tensor("qh", [64, 2 * TOWN], BF16))
    pt = esB3.enter_context(nc.sbuf_tensor("pt", [128, E * NCH * 160], BF16))
    dinv = esB3.enter_context(nc.sbuf_tensor("dinv", [1, 1024], F32))
    dinvb = esB3.enter_context(nc.sbuf_tensor("dinvb", [1, 1024], BF16))
    dbc = esB3.enter_context(nc.sbuf_tensor("dbc", [64, 1024], F32))
    ysc = esB3.enter_context(nc.sbuf_tensor("ysc", [64, 1024], BF16))

    def chunk_cols(c):
        return max(c * 128 - 32, 0), min(c * 128 + 126, TOWN - 1)

    @blk.sync
    def _(s):
        s.wait_ge(S["dQo"], 32 * 16)
        s.wait_ge(S["sExt"], E)
        for hh in range(2):
            s.dma_start(out=qh[0:64, (hh % 2) * TOWN:(hh % 2) * TOWN + TOWN],
                        in_=scr_q[hh // 2][(hh % 2) * 64:(hh % 2) * 64 + 64, :]).then_inc(S["dQi"], 16)

    @blk.tensor
    def _(t):
        t.wait_ge(S["dVT"], E * NCH * 16)
        for hh in range(H):
            t.wait_ge(S["dQi"], (hh + 1) * 16)
            gnext = 0
            for c in range(NCH):
                b0 = (c % 2) * 2
                if c >= 2:
                    t.wait_ge(S["sExp"], hh * NCH + c - 1)
                elif hh >= 1:
                    t.wait_ge(S["sExp"], hh * NCH - (1 if c == 1 else 0))
                lo, hi = chunk_cols(c)
                n = hi - lo + 1
                joff = lo - (c * 128 - 32)
                for e in range(E):
                    off = [0, 160, 320, 0][e]
                    bank = b0 if e < 3 else b0 + 1
                    mm = t.matmul(PB(bank, off + joff, n),
                                  kvext[0:64, e * EXTC + c * 128:e * EXTC + c * 128 + 128],
                                  qh[0:64, (hh % 2) * TOWN + lo:(hh % 2) * TOWN + lo + n],
                                  start=True, stop=True)
                mm.then_inc(S["sSc"], 1)
                while gnext < NG and c == min(4 * gnext + 4, NCH - 1):
                    gg = gnext
                    k = hh * NG + gg
                    t.wait_ge(S["sMkV"], hh * NCH + c + 1)
                    t.wait_ge(S["sMkG"], hh * NCH + c + 1)
                    if k >= 2:
                        t.wait_ge(S["sY"], k - 1)
                    plan = []
                    for e in range(E):
                        for cc in range(4 * gg, min(4 * gg + 5, NCH)):
                            tlo, thi = chunk_cols(cc)
                            lo2, hi2 = max(tlo, gg * 512), min(thi, gg * 512 + 511)
                            if lo2 <= hi2:
                                plan.append((e, cc, lo2, hi2))
                    for idx, (e, cc, lo2, hi2) in enumerate(plan):
                        j0 = lo2 - (cc * 128 - 32)
                        n2 = hi2 - lo2 + 1
                        mm = t.matmul(PB(4 + k % 2, lo2 - gg * 512, n2, 0, 65),
                                      vtm[:, (e * NCH + cc) * 80:(e * NCH + cc) * 80 + 65],
                                      pt[:, (e * NCH + cc) * 160 + j0:(e * NCH + cc) * 160 + j0 + n2],
                                      start=(idx == 0), stop=(idx == len(plan) - 1))
                    mm.then_inc(S["sPV"], 1)
                    gnext += 1
                    if gg >= 1:
                        kk = k - 1
                        t.wait_ge(S["sRcp"], kk + 1)
                        if kk >= 1:
                            t.wait_ge(S["sEvD"], kk)
                        t.matmul(PB(6, 0, 512, 0, 64), on1[0:1, 0:64],
                                 dinvb[0:1, (kk % 2) * 512:(kk % 2) * 512 + 512],
                                 start=True, stop=True).then_inc(S["sBcD"], 1)
            kk = hh * NG + NG - 1
            t.wait_ge(S["sRcp"], kk + 1)
            t.wait_ge(S["sEvD"], kk)
            t.matmul(PB(6, 0, 512, 0, 64), on1[0:1, 0:64],
                     dinvb[0:1, (kk % 2) * 512:(kk % 2) * 512 + 512],
                     start=True, stop=True).then_inc(S["sBcD"], 1)

    @blk.scalar
    def _(sc):
        for hh in range(H):
            for c in range(NCH):
                b0 = (c % 2) * 2
                sc.wait_ge(S["sSc"], hh * NCH + c + 1)
                dst3 = pt[:].rearrange("p (e cj) -> p e cj", e=E)[:, 0:3,
                                                                 c * 160:c * 160 + 160]
                src3 = pall[:, b0 * 512:b0 * 512 + 480].rearrange("p (e j) -> p e j", e=3)
                sc.activation(dst3, src3, AF.Exp, bias=0.0, scale=0.125)
                sc.activation(pt[:, (3 * NCH + c) * 160:(3 * NCH + c) * 160 + 160],
                              PB(b0 + 1, 0, 160), AF.Exp,
                              bias=0.0, scale=0.125).then_inc(S["sExp"], 1)
                if c >= 8 and (c - 8) % 4 == 0 and (c - 8) // 4 < NG - 1:
                    gg = (c - 8) // 4
                    k = hh * NG + gg
                    sc.wait_ge(S["sBcD"], k + 1)
                    sc.activation(dbc[0:64, (k % 2) * 512:(k % 2) * 512 + 512],
                                  PB(6, 0, 512, 0, 64), AF.Copy,
                                  bias=0.0, scale=1.0).then_inc(S["sEvD"], 1)
            for gg in [NG - 2, NG - 1]:
                k = hh * NG + gg
                sc.wait_ge(S["sBcD"], k + 1)
                sc.activation(dbc[0:64, (k % 2) * 512:(k % 2) * 512 + 512],
                              PB(6, 0, 512, 0, 64), AF.Copy,
                              bias=0.0, scale=1.0).then_inc(S["sEvD"], 1)

    @blk.vector
    def _(v):
        for hh in range(H):
            gnext = 0
            for c in range(NCH):
                v.wait_ge(S["sExp"], hh * NCH + c + 1)
                for e in range(2):
                    pc = pt[:, (e * NCH + c) * 160:(e * NCH + c) * 160 + 160]
                    ins = v.tensor_tensor(pc, pc, msk[:], ALU.mult)
                ins.then_inc(S["sMkV"], 1)
                while gnext < NG and c == min(4 * gnext + 4, NCH - 1):
                    gg = gnext
                    k = hh * NG + gg
                    v.wait_ge(S["sPV"], k + 1)
                    ins = v.reciprocal(dinv[0:1, (k % 2) * 512:(k % 2) * 512 + 512],
                                       PB(4 + k % 2, 0, 512, 64, 1))
                    VS(v, ins)
                    v.tensor_copy(dinvb[0:1, (k % 2) * 512:(k % 2) * 512 + 512],
                                  dinv[0:1, (k % 2) * 512:(k % 2) * 512 + 512]).then_inc(S["sRcp"], 1)
                    if gg >= 1:
                        kk = k - 1
                        v.wait_ge(S["sEvD"], kk + 1)
                        if kk >= 2:
                            v.wait_ge(S["dYo"], (kk - 1) * 16)
                        v.scalar_tensor_tensor(
                            ysc[0:64, (kk % 2) * 512:(kk % 2) * 512 + 512],
                            PB(4 + kk % 2, 0, 512, 0, 64), 1.0,
                            dbc[0:64, (kk % 2) * 512:(kk % 2) * 512 + 512],
                            ALU.mult, ALU.mult).then_inc(S["sY"], 1)
                    gnext += 1
            kk = hh * NG + NG - 1
            v.wait_ge(S["sEvD"], kk + 1)
            v.wait_ge(S["dYo"], (kk - 1) * 16)
            v.scalar_tensor_tensor(ysc[0:64, (kk % 2) * 512:(kk % 2) * 512 + 512],
                                   PB(4 + kk % 2, 0, 512, 0, 64), 1.0,
                                   dbc[0:64, (kk % 2) * 512:(kk % 2) * 512 + 512],
                                   ALU.mult, ALU.mult).then_inc(S["sY"], 1)

    @blk.gpsimd
    def _(gp):
        for hh in range(H):
            for c in range(NCH):
                gp.wait_ge(S["sExp"], hh * NCH + c + 1)
                for e in range(2, 4):
                    pc = pt[:, (e * NCH + c) * 160:(e * NCH + c) * 160 + 160]
                    ins = gp.tensor_tensor(pc, pc, msk[:], ALU.mult)
                ins.then_inc(S["sMkG"], 1)

    @blk.sync
    def _(s):
        for hh in range(H):
            for gg in range(NG):
                k = hh * NG + gg
                s.wait_ge(S["sY"], k + 1)
                s.dma_start(out=scr_y[hh // 2][(hh % 2) * 64:(hh % 2) * 64 + 64,
                                               gg * 512:gg * 512 + 512],
                            in_=ysc[0:64, (k % 2) * 512:(k % 2) * 512 + 512]).then_inc(S["dYo"], 16)
            if hh + 2 < H:
                h2 = hh + 2
                s.wait_ge(S["sSc"], (h2 - 1) * NCH)
                s.dma_start(out=qh[0:64, (h2 % 2) * TOWN:(h2 % 2) * TOWN + TOWN],
                            in_=scr_q[h2 // 2][(h2 % 2) * 64:(h2 % 2) * 64 + 64, :]).then_inc(S["dQi"], 16)

    if debug:
        @blk.gpsimd
        def _(gp):
            gp.wait_ge(S["sMkV"], H * NCH)
            gp.wait_ge(S["sMkG"], H * NCH)
            gp.dma_start(out=dbg_pt, in_=pt[:]).then_inc(S["dDbg"], 16)
            gp.wait_ge(S["dDbg"], 7 * 16)

    esB3.close()
    esB2.close()
    esC = contextlib.ExitStack()
    wo_sb = esC.enter_context(nc.sbuf_tensor("wo_sb", [128, 4 * 1024], BF16))
    ybuf = esC.enter_context(nc.sbuf_tensor("ybuf", [128, 2048], BF16))
    attn_fm = esC.enter_context(nc.sbuf_tensor("attn_fm", [128, 8 * 512], BF16))
    attn_tm = esC.enter_context(nc.sbuf_tensor("attn_tm", [128, 6 * 1024], BF16,
                                               is_transpose_dest=True))
    xres = esC.enter_context(nc.sbuf_tensor("xres", [128, 2 * 1024], F32))
    rres = esC.enter_context(nc.sbuf_tensor("rres", [128, 2 * 1024], F32))
    x1_tm = esC.enter_context(nc.sbuf_tensor("x1_tm", [128, 2 * 1024], F32))
    x1b = esC.enter_context(nc.sbuf_tensor("x1b", [128, 2 * 1024], BF16))
    x1t = esC.enter_context(nc.sbuf_tensor("x1t", [128, 2 * 1024], BF16,
                                           is_transpose_dest=True))
    r_fm = esC.enter_context(nc.sbuf_tensor("r_fm", [8, TOWN], BF16))
    x1g = esC.enter_context(nc.sbuf_tensor("x1g", [128, 2 * 4096], BF16))
    h1g = esC.enter_context(nc.sbuf_tensor("h1g", [128, 16 * 512], BF16))
    h1p = esC.enter_context(nc.sbuf_tensor("h1p", [128, 2 * 512], BF16))
    r_bc = esC.enter_context(nc.sbuf_tensor("r_bc", [128, 2 * 2048], BF16))
    dgr = esC.enter_context(nc.sbuf_tensor("dgr", [128, 256], BF16))
    o1ev = esC.enter_context(nc.sbuf_tensor("o1ev", [128, 2 * 512], BF16))
    o1in = esC.enter_context(nc.sbuf_tensor("o1in", [128, 2 * 512], BF16))
    ffn_fm = esC.enter_context(nc.sbuf_tensor("ffn_fm", [128, 2 * 512], BF16))
    ffn_tm = esC.enter_context(nc.sbuf_tensor("ffn_tm", [128, 4 * 1024], BF16,
                                              is_transpose_dest=True))
    esF = contextlib.ExitStack()
    w1h = esF.enter_context(nc.sbuf_tensor("w1h", [128, NDC * 2048], BF16))
    w2h = esF.enter_context(nc.sbuf_tensor("w2h", [128, 16 * 1024], BF16))

    @blk.gpsimd
    def _(gp):
        gp.wait_ge(S["sY"], 64)
        for qc in range(4):
            gp.dma_start(out=wo_sb[:, qc * 1024:qc * 1024 + 1024],
                         in_=wod[qc * 128:qc * 128 + 128, :]).then_inc(S["dW"], 16)
        for half in range(2):
            if half == 1:
                gp.wait_ge(S["sW2"], 64)
            for dc in range(NDC):
                gp.dma_start(out=w1h[:, dc * 2048:dc * 2048 + 2048],
                             in_=w1d[dc * 128:dc * 128 + 128,
                                     half * 2048:half * 2048 + 2048]).then_inc(S["dWF"], 16)
            for fl in range(16):
                gp.dma_start(out=w2h[:, fl * 1024:fl * 1024 + 1024],
                             in_=w2d[(half * 16 + fl) * 128:(half * 16 + fl) * 128 + 128, :]).then_inc(S["dWF"], 16)

    scr_y_r = scr_y.rearrange("q p t -> p q t")
    scr_xf_r = scr_xf.rearrange("c p t -> p c t")

    @blk.sync
    def _(s):
        s.wait_ge(S["dYo"], 64 * 16)
        s.dma_start(out=ybuf[:, 0:2048],
                    in_=scr_y_r[:, :, 0:512]).then_inc(S["dYi"], 16)
        def _p5store(s, tt):
            s.wait_ge(S["sNrm1"], tt + 1)
            s.dma_start(out=scr_x1[tt * 128:tt * 128 + 128, :],
                        in_=x1_tm[:, (tt % 2) * 1024:(tt % 2) * 1024 + 1024]).then_inc(S["dX1s"], 16)
            s.wait_ge(S["dX1T"], (tt + 1) * 128)
            s.dma_start(out=scr_xf_r[:, :, tt * 128:tt * 128 + 128],
                        in_=x1t[:, (tt % 2) * 1024:(tt % 2) * 1024 + 1024]).then_inc(S["dXFs"], 16)
        for gg in range(NG):
            if gg >= 1:
                for s4 in range(4):
                    _p5store(s, (gg - 1) * 4 + s4)
            for s4 in range(4):
                tt = gg * 4 + s4
                if tt >= 2:
                    s.wait_ge(S["sRes1"], tt - 1)
                s.dma_start(out=xres[:, (tt % 2) * 1024:(tt % 2) * 1024 + 1024],
                            in_=xs[256 + tt * 128:256 + tt * 128 + 128, :]).then_inc(S["dXR"], 16)
            if gg + 1 < NG:
                s.wait_ge(S["sOP"], (gg + 1) * NDC)
                s.dma_start(out=ybuf[:, 0:2048],
                            in_=scr_y_r[:, :, (gg + 1) * 512:(gg + 1) * 512 + 512]).then_inc(S["dYi"], 16)
        for s4 in range(4):
            _p5store(s, (NG - 1) * 4 + s4)
        no1 = 0
        for half in range(2):
            for gg in range(NG):
                hg = half * NG + gg
                s.wait_ge(S["dXFs"], (4 * gg + 4) * 16)
                if hg >= 2:
                    s.wait_ge(S["sH1"], (hg - 1) * 16)
                s.dma_start(out=x1g[:, (gg % 2) * 4096:(gg % 2) * 4096 + 4096],
                            in_=scr_xf_r[:, :, gg * 512:gg * 512 + 512]).then_inc(S["dXG"], 16)
                if half == 1 and gg >= 1:
                    for s4 in range(4):
                        tt = (gg - 1) * 4 + s4
                        if tt >= 2:
                            s.wait_ge(S["sAdd2"], tt - 1)
                        else:
                            s.wait_ge(S["dX1s"], NTO * 16)
                        s.dma_start(out=x1_tm[:, (tt % 2) * 1024:(tt % 2) * 1024 + 1024],
                                    in_=scr_x1[tt * 128:tt * 128 + 128, :]).then_inc(S["dXR"], 16)
                    for s4 in range(4):
                        tt = (gg - 1) * 4 + s4
                        s.wait_ge(S["sNrm2"], tt + 1)
                        s.dma_start(out=y[tt * 128:tt * 128 + 128, :],
                                    in_=xres[:, (tt % 2) * 1024:(tt % 2) * 1024 + 1024]).then_inc(S["dY"], 16)
                for dc in range(NDC):
                    k2 = gg * NDC + dc
                    if half == 0:
                        s.wait_ge(S["sEvW2"], k2 + 1)
                        s.dma_start(out=scr_o1[dc][0:128, gg * 512:gg * 512 + 512],
                                    in_=o1ev[:, (k2 % 2) * 512:(k2 % 2) * 512 + 512]).then_inc(S["dO1o"], 16)
                    else:
                        if no1 >= 2:
                            s.wait_ge(S["sEvW2"], 64 + no1 - 1)
                        s.dma_start(out=o1in[:, (k2 % 2) * 512:(k2 % 2) * 512 + 512],
                                    in_=scr_o1[dc][0:128, gg * 512:gg * 512 + 512]).then_inc(S["dO1i"], 16)
                        no1 += 1
        for tt in range((NG - 1) * 4, NTO):
            s.wait_ge(S["sAdd2"], tt - 1)
            s.dma_start(out=x1_tm[:, (tt % 2) * 1024:(tt % 2) * 1024 + 1024],
                        in_=scr_x1[tt * 128:tt * 128 + 128, :]).then_inc(S["dXR"], 16)
        for tt in range((NG - 1) * 4, NTO):
            s.wait_ge(S["sNrm2"], tt + 1)
            s.dma_start(out=y[tt * 128:tt * 128 + 128, :],
                        in_=xres[:, (tt % 2) * 1024:(tt % 2) * 1024 + 1024]).then_inc(S["dY"], 16)
        s.wait_ge(S["dY"], NTO * 16)
        if debug:
            s.dma_start(out=dbg_r, in_=r_tm[:]).then_inc(S["dDbg"], 16)
            s.wait_ge(S["dDbg"], 8 * 16)

    @blk.tensor
    def _(t):
        if debug:
            t.wait_ge(S["dDbg"], 7 * 16)
        t.wait_ge(S["dW"], (NW0 + 4) * 16)
        t.wait_ge(S["sExp"], H * NCH)
        for gg in range(NG):
            t.wait_ge(S["dYi"], (gg + 1) * 16)
            for dc in range(NDC):
                k = gg * NDC + dc
                if k >= 2:
                    t.wait_ge(S["sEvOP"], k - 1)
                for qc in range(4):
                    mm = t.matmul(PB(k % 2), wo_sb[:, qc * 1024 + dc * 128:qc * 1024 + dc * 128 + 128],
                                  ybuf[:, qc * 512:qc * 512 + 512],
                                  start=(qc == 0), stop=(qc == 3))
                mm.then_inc(S["sOP"], 1)
        t.wait_ge(S["sEvD"], 64)
        t.wait_ge(S["sEvOP"], 64)
        for half in range(2):
            for gg in range(NG):
                hg = half * NG + gg
                t.wait_ge(S["dXG"], (hg + 1) * 16)
                if half == 0:
                    for s4 in range(4):
                        tt = gg * 4 + s4
                        if tt >= 1:
                            t.wait_ge(S["sEvRt"], tt)
                        for dc in range(NDC):
                            t.matmul(PB(6, 0, 8),
                                     x1g[:, (gg % 2) * 4096 + dc * 512 + s4 * 128:(gg % 2) * 4096 + dc * 512 + s4 * 128 + 128],
                                     rw_sb[:, dc * 8:dc * 8 + 8],
                                     start=(dc == 0), stop=False)
                        t.matmul(PB(6, 0, 8), on1[0:1, 0:128], rb_sb[0:1, :],
                                 start=False, stop=True).then_inc(S["sRt"], 1)
                    for s4 in range(4):
                        tt = gg * 4 + s4
                        t.wait_ge(S["sTop"], tt + 1)
                        if tt >= 1:
                            t.wait_ge(S["sEvRT"], tt)
                        t.transpose(pbf[0:8, 0:128], rcast[:, (tt % 2) * 8:(tt % 2) * 8 + 8],
                                    i4[:, 0:128]).then_inc(S["sRT"], 1)
                for el in range(4):
                    m = hg * 4 + el
                    if m >= 2:
                        t.wait_ge(S["sEvR"], m - 1)
                    for s4 in range(4):
                        j = m * 4 + s4
                        t.wait_ge(S["sDgR"], j + 1)
                        t.matmul(PB(0 + m % 2, s4 * 128, 128), onb[:],
                                 dgr[:, (j % 2) * 128:(j % 2) * 128 + 128],
                                 start=True, stop=True).then_inc(S["sBcR"], 1)
                for fl in range(16):
                    kf = hg * 16 + fl
                    t.wait_ge(S["dWF"], (half * 24 + 8) * 16)
                    if kf >= 2:
                        t.wait_ge(S["sGel"], kf - 1)
                    for dc in range(NDC):
                        mm = t.matmul(PB(2 + fl % 2),
                                      w1h[:, dc * 2048 + fl * 128:dc * 2048 + fl * 128 + 128],
                                      x1g[:, (gg % 2) * 4096 + dc * 512:(gg % 2) * 4096 + dc * 512 + 512],
                                      start=(dc == 0), stop=(dc == NDC - 1))
                    mm.then_inc(S["sH1"], 1)
                t.wait_ge(S["dWF"], (half + 1) * 24 * 16)
                t.wait_ge(S["sScl"], hg * 16 + 16)
                for dc in range(NDC):
                    k2 = hg * NDC + dc
                    if k2 >= 2:
                        t.wait_ge(S["sEvW2"], k2 - 1)
                    for fl in range(16):
                        mm = t.matmul(PB(4 + k2 % 2),
                                      w2h[:, fl * 1024 + dc * 128:fl * 1024 + dc * 128 + 128],
                                      h1g[:, fl * 512:fl * 512 + 512],
                                      start=(fl == 0), stop=(half == 0 and fl == 15))
                    if half == 1:
                        t.wait_ge(S["sEvRT"], NTO)
                        mm = t.matmul(PB(4 + k2 % 2), b2_sb[0:8, dc * 128:dc * 128 + 128],
                                      r_fm[0:8, gg * 512:gg * 512 + 512],
                                      start=False, stop=True)
                    mm.then_inc(S["sW2"], 1)

    @blk.scalar
    def _(sc):
        def _p5act(sc, tt):
            sb0 = (tt % 2) * 24
            sc.wait_ge(S["sBN1"], tt + 1)
            sc.activation(stat[:, sb0 + 14:sb0 + 15], stat[:, sb0 + 1:sb0 + 2],
                          AF.Sqrt, bias=epsb[:, 0:1], scale=1.0).then_inc(S["sSq1"], 1)
            sc.wait_ge(S["sRst1"], tt + 1)
            if tt >= 2:
                sc.wait_ge(S["dX1s"], (tt - 1) * 16)
            sc.activation(x1_tm[:, (tt % 2) * 1024:(tt % 2) * 1024 + 1024],
                          rres[:, (tt % 2) * 1024:(tt % 2) * 1024 + 1024],
                          AF.Identity, bias=stat[:, sb0 + 16:sb0 + 17],
                          scale=stat[:, sb0 + 15:sb0 + 16]).then_inc(S["sNrm1"], 1)
            sc.wait_ge(S["sX1b"], tt + 1)
            if tt >= 2:
                sc.wait_ge(S["dXFs"], (tt - 1) * 16)
            for dc in range(NDC):
                sc.dma_start(out=x1t[:, (tt % 2) * 1024 + dc * 128:(tt % 2) * 1024 + dc * 128 + 128],
                             in_=x1b[:, (tt % 2) * 1024 + dc * 128:(tt % 2) * 1024 + dc * 128 + 128],
                             transpose=True).then_inc(S["dX1T"], 16)
        for gg in range(NG):
            if gg >= 1:
                for s4 in range(4):
                    _p5act(sc, (gg - 1) * 4 + s4)
            if gg >= 1:
                sc.wait_ge(S["sRes1"], max(0, 4 * gg - 2))
            for dc in range(NDC):
                k = gg * NDC + dc
                sc.wait_ge(S["sEvOP"], k + 1)
                for s4 in range(4):
                    tt = gg * 4 + s4
                    sc.dma_start(out=attn_tm[:, (tt % 6) * 1024 + dc * 128:(tt % 6) * 1024 + dc * 128 + 128],
                                 in_=attn_fm[:, (k % 8) * 512 + s4 * 128:(k % 8) * 512 + s4 * 128 + 128],
                                 transpose=True).then_inc(S["dXF"], 16)
        for s4 in range(4):
            _p5act(sc, (NG - 1) * 4 + s4)
        def _sqrt2(sc, tt):
            sb0 = 48 + (tt % 2) * 24
            sc.wait_ge(S["sBN2"], tt + 1)
            sc.activation(stat[:, sb0 + 14:sb0 + 15], stat[:, sb0 + 1:sb0 + 2],
                          AF.Sqrt, bias=epsb[:, 0:1], scale=1.0).then_inc(S["sSq2"], 1)
        def _nrm2(sc, tt):
            sb0 = 48 + (tt % 2) * 24
            sc.wait_ge(S["sRst2"], tt + 1)
            if tt < 2:
                sc.wait_ge(S["sRes1"], NTO)
            else:
                sc.wait_ge(S["dY"], (tt - 1) * 16)
            sc.activation(xres[:, (tt % 2) * 1024:(tt % 2) * 1024 + 1024],
                          rres[:, (tt % 2) * 1024:(tt % 2) * 1024 + 1024],
                          AF.Identity, bias=stat[:, sb0 + 16:sb0 + 17],
                          scale=stat[:, sb0 + 15:sb0 + 16]).then_inc(S["sNrm2"], 1)
        def _ffnT(sc, gg):
            if gg >= 1:
                sc.wait_ge(S["sAdd2"], gg * 4)
            for dc in range(NDC):
                ko = gg * NDC + dc
                sc.wait_ge(S["sEvW2"], 64 + ko + 1)
                for s4 in range(4):
                    tt = gg * 4 + s4
                    sc.dma_start(out=ffn_tm[:, (tt % 4) * 1024 + dc * 128:(tt % 4) * 1024 + dc * 128 + 128],
                                 in_=ffn_fm[:, (ko % 2) * 512 + s4 * 128:(ko % 2) * 512 + s4 * 128 + 128],
                                 transpose=True).then_inc(S["dFT"], 16)
        for half in range(2):
            for gg in range(NG):
                hg = half * NG + gg
                if half == 1 and gg >= 1:
                    for s4 in range(4):
                        _sqrt2(sc, (gg - 1) * 4 + s4)
                        _nrm2(sc, (gg - 1) * 4 + s4)
                if half == 0:
                    for s4 in range(4):
                        tt = gg * 4 + s4
                        sc.wait_ge(S["sRtd"], tt + 1)
                        sc.activation(rstat[:, (tt % 4) * 32 + 3:(tt % 4) * 32 + 4],
                                      rstat[:, (tt % 4) * 32 + 2:(tt % 4) * 32 + 3],
                                      AF.Sigmoid, bias=0.0, scale=1.0).then_inc(S["sSig"], 1)
                for el in range(4):
                    m = hg * 4 + el
                    sc.wait_ge(S["sBcR"], (m + 1) * 4)
                    if hg >= 2:
                        sc.wait_ge(S["sScl"], (hg - 1) * 16)
                    sc.activation(r_bc[:, (hg % 2) * 2048 + el * 512:(hg % 2) * 2048 + el * 512 + 512],
                                  PB(0 + m % 2), AF.Copy, bias=0.0,
                                  scale=1.0).then_inc(S["sEvR"], 1)
                for fl in range(16):
                    kf = hg * 16 + fl
                    sc.wait_ge(S["sH1"], kf + 1)
                    if kf >= 2:
                        sc.wait_ge(S["sScl"], kf - 1)
                    sc.activation(h1p[:, (kf % 2) * 512:(kf % 2) * 512 + 512],
                                  PB(2 + fl % 2), AF.Gelu,
                                  bias=b1_sb[:, half * 16 + fl:half * 16 + fl + 1],
                                  scale=1.0).then_inc(S["sGel"], 1)
                if half == 1:
                    _ffnT(sc, gg)
        for s4 in range(4):
            _sqrt2(sc, (NG - 1) * 4 + s4)
            _nrm2(sc, (NG - 1) * 4 + s4)

    @blk.vector
    def _(v):
        def _p5tile(v, tt):
            gg = tt // 4
            if True:
                sb0 = (tt % 2) * 24
                v.wait_ge(S["dXF"], (gg + 1) * 32 * 16)
                v.wait_ge(S["dXR"], (tt + 1) * 16)
                if tt >= 2:
                    v.wait_ge(S["sNrm1"], tt - 1)
                v.tensor_tensor(rres[:, (tt % 2) * 1024:(tt % 2) * 1024 + 1024],
                                xres[:, (tt % 2) * 1024:(tt % 2) * 1024 + 1024],
                                attn_tm[:, (tt % 6) * 1024:(tt % 6) * 1024 + 1024],
                                ALU.add).then_inc(S["sRes1"], 1)
                v.wait_ge(S["sRes1"], tt + 1)
                v.bn_stats(stat[:, sb0 + 2:sb0 + 8],
                           rres[:, (tt % 2) * 1024:(tt % 2) * 1024 + 512])
                ins = v.bn_stats(stat[:, sb0 + 8:sb0 + 14],
                                 rres[:, (tt % 2) * 1024 + 512:(tt % 2) * 1024 + 1024])
                VS(v, ins)
                v.bn_aggr(stat[:, sb0:sb0 + 2], stat[:, sb0 + 2:sb0 + 14]).then_inc(S["sBN1"], 1)
                v.wait_ge(S["sSq1"], tt + 1)
                ins = v.reciprocal(stat[:, sb0 + 15:sb0 + 16], stat[:, sb0 + 14:sb0 + 15])
                VS(v, ins)
                v.tensor_scalar(stat[:, sb0 + 16:sb0 + 17], stat[:, sb0:sb0 + 1],
                                stat[:, sb0 + 15:sb0 + 16], -1.0,
                                ALU.mult, ALU.mult).then_inc(S["sRst1"], 1)
                v.wait_ge(S["sNrm1"], tt + 1)
                if tt >= 2:
                    v.wait_ge(S["dX1T"], (tt - 1) * 128)
                v.tensor_copy(x1b[:, (tt % 2) * 1024:(tt % 2) * 1024 + 1024],
                              x1_tm[:, (tt % 2) * 1024:(tt % 2) * 1024 + 1024]).then_inc(S["sX1b"], 1)
        for gg in range(NG):
            if gg >= 1:
                for s4 in range(4):
                    _p5tile(v, (gg - 1) * 4 + s4)
            for dc in range(NDC):
                k = gg * NDC + dc
                v.wait_ge(S["sOP"], k + 1)
                if k >= 8:
                    v.wait_ge(S["dXF"], (k - 7) * 64)
                v.tensor_copy(attn_fm[:, (k % 8) * 512:(k % 8) * 512 + 512],
                              PB(k % 2)).then_inc(S["sEvOP"], 1)
        for s4 in range(4):
            _p5tile(v, (NG - 1) * 4 + s4)
        def _p7tile(v, tt):
            sb0 = 48 + (tt % 2) * 24
            gg7 = tt // 4
            v.wait_ge(S["dFT"], (gg7 + 1) * 32 * 16)
            v.wait_ge(S["dXR"], (NTO + tt + 1) * 16)
            if tt >= 2:
                v.wait_ge(S["sNrm2"], tt - 1)
            else:
                v.wait_ge(S["sNrm1"], NTO)
            v.tensor_tensor(rres[:, (tt % 2) * 1024:(tt % 2) * 1024 + 1024],
                            x1_tm[:, (tt % 2) * 1024:(tt % 2) * 1024 + 1024],
                            ffn_tm[:, (tt % 4) * 1024:(tt % 4) * 1024 + 1024],
                            ALU.add).then_inc(S["sAdd2"], 1)
            v.wait_ge(S["sAdd2"], tt + 1)
            v.bn_stats(stat[:, sb0 + 2:sb0 + 8],
                       rres[:, (tt % 2) * 1024:(tt % 2) * 1024 + 512])
            ins = v.bn_stats(stat[:, sb0 + 8:sb0 + 14],
                             rres[:, (tt % 2) * 1024 + 512:(tt % 2) * 1024 + 1024])
            VS(v, ins)
            v.bn_aggr(stat[:, sb0:sb0 + 2], stat[:, sb0 + 2:sb0 + 14]).then_inc(S["sBN2"], 1)
            v.wait_ge(S["sSq2"], tt + 1)
            ins = v.reciprocal(stat[:, sb0 + 15:sb0 + 16], stat[:, sb0 + 14:sb0 + 15])
            VS(v, ins)
            v.tensor_scalar(stat[:, sb0 + 16:sb0 + 17], stat[:, sb0:sb0 + 1],
                            stat[:, sb0 + 15:sb0 + 16], -1.0,
                            ALU.mult, ALU.mult).then_inc(S["sRst2"], 1)
        for half in range(2):
            for gg in range(NG):
                hg = half * NG + gg
                if half == 1 and gg >= 1:
                    for s4 in range(4):
                        _p7tile(v, (gg - 1) * 4 + s4)
                if half == 0:
                    for s4 in range(4):
                        tt = gg * 4 + s4
                        rs = (tt % 4) * 32
                        v.wait_ge(S["sRt"], tt + 1)
                        if tt >= 4:
                            v.wait_ge(S["sTop"], tt - 3)
                        v.tensor_copy(lbuf[:, (tt % 2) * 8:(tt % 2) * 8 + 8],
                                      PB(6, 0, 8)).then_inc(S["sEvRt"], 1)
                        v.wait_ge(S["sEvRt"], tt + 1)
                        L = lbuf[:, (tt % 2) * 8:(tt % 2) * 8 + 8]
                        ins = v.reduce_max(rstat[:, rs:rs + 1], L, axis=AX)
                        VS(v, ins)
                        ins = v.tensor_scalar(rstat[:, rs + 8:rs + 16], L,
                                              rstat[:, rs:rs + 1], None, ALU.is_ge)
                        VS(v, ins)
                        ins = v.scalar_tensor_tensor(rstat[:, rs + 16:rs + 24],
                                                     rstat[:, rs + 8:rs + 16], -1e30, L,
                                                     ALU.mult, ALU.add)
                        VS(v, ins)
                        ins = v.reduce_max(rstat[:, rs + 1:rs + 2],
                                           rstat[:, rs + 16:rs + 24], axis=AX)
                        VS(v, ins)
                        v.tensor_sub(rstat[:, rs + 2:rs + 3], rstat[:, rs:rs + 1],
                                     rstat[:, rs + 1:rs + 2]).then_inc(S["sRtd"], 1)
                    for s4 in range(4):
                        tt = gg * 4 + s4
                        rs = (tt % 4) * 32
                        v.wait_ge(S["sSig"], tt + 1)
                        v.tensor_scalar(rstat[:, rs + 4:rs + 5], rstat[:, rs + 3:rs + 4],
                                        -1.0, 1.0, ALU.mult, ALU.add)
                        ins = v.tensor_scalar(rstat[:, rs + 24:rs + 32],
                                              rstat[:, rs + 16:rs + 24],
                                              rstat[:, rs + 1:rs + 2], None, ALU.is_ge)
                        VS(v, ins)
                        ins = v.tensor_scalar(rstat[:, rs + 16:rs + 24],
                                              rstat[:, rs + 8:rs + 16],
                                              rstat[:, rs + 3:rs + 4], None, ALU.mult)
                        VS(v, ins)
                        ins = v.scalar_tensor_tensor(r_tm[:, tt * 8:tt * 8 + 8],
                                                     rstat[:, rs + 24:rs + 32],
                                                     rstat[:, rs + 4:rs + 5],
                                                     rstat[:, rs + 16:rs + 24],
                                                     ALU.mult, ALU.add)
                        VS(v, ins)
                        v.tensor_copy(rcast[:, (tt % 2) * 8:(tt % 2) * 8 + 8],
                                      r_tm[:, tt * 8:tt * 8 + 8]).then_inc(S["sTop"], 1)
                        v.wait_ge(S["sRT"], tt + 1)
                        v.tensor_copy(r_fm[0:8, tt * 128:tt * 128 + 128],
                                      pbf[0:8, 0:128]).then_inc(S["sEvRT"], 1)
                for el in range(4):
                    m = hg * 4 + el
                    for s4 in range(4):
                        j = m * 4 + s4
                        tt = gg * 4 + s4
                        if j >= 2:
                            v.wait_ge(S["sBcR"], j - 1)
                        v.tensor_scalar(dgr[:, (j % 2) * 128:(j % 2) * 128 + 128],
                                        i4[:, 0:128],
                                        r_tm[:, tt * 8 + half * 4 + el:tt * 8 + half * 4 + el + 1],
                                        None, ALU.mult).then_inc(S["sDgR"], 1)
                for fl in range(16):
                    kf = hg * 16 + fl
                    el = fl // 4
                    v.wait_ge(S["sGel"], kf + 1)
                    v.wait_ge(S["sEvR"], hg * 4 + el + 1)
                    v.tensor_tensor(h1g[:, fl * 512:fl * 512 + 512],
                                    h1p[:, (kf % 2) * 512:(kf % 2) * 512 + 512],
                                    r_bc[:, (hg % 2) * 2048 + el * 512:(hg % 2) * 2048 + el * 512 + 512],
                                    ALU.mult).then_inc(S["sScl"], 1)
                for dc in range(NDC):
                    k2 = hg * NDC + dc
                    v.wait_ge(S["sW2"], k2 + 1)
                    if half == 0:
                        if k2 >= 2:
                            v.wait_ge(S["dO1o"], (k2 - 1) * 16)
                        v.tensor_copy(o1ev[:, (k2 % 2) * 512:(k2 % 2) * 512 + 512],
                                      PB(4 + k2 % 2)).then_inc(S["sEvW2"], 1)
                    else:
                        ko = gg * NDC + dc
                        v.wait_ge(S["dO1i"], (ko + 1) * 16)
                        if ko >= 2:
                            v.wait_ge(S["dFT"], (ko - 1) * 64)
                        v.tensor_tensor(ffn_fm[:, (ko % 2) * 512:(ko % 2) * 512 + 512],
                                        PB(4 + k2 % 2),
                                        o1in[:, (ko % 2) * 512:(ko % 2) * 512 + 512],
                                        ALU.add).then_inc(S["sEvW2"], 1)
        for tt in range((NG - 1) * 4, NTO):
            _p7tile(v, tt)

    esF.close()
    esC.close()
    return nc, es


_CACHE = {}


def _host_arrays(inputs):
    X = np.asarray(inputs["L0_attn_X"])
    w_G = np.asarray(inputs["L0_attn_w_G"])
    b_G = np.asarray(inputs["L0_attn_b_G"])
    w_K = np.asarray(inputs["L0_attn_w_K"])
    w_V = np.asarray(inputs["L0_attn_w_V"])
    w_Q = np.asarray(inputs["L0_attn_w_Q"])
    w_O = np.asarray(inputs["L0_attn_w_O"])
    iK = np.asarray(inputs["L0_attn_init_K_rec"])
    iV = np.asarray(inputs["L0_attn_init_V_rec"])
    rw = np.asarray(inputs["L0_ffn_router_w"])
    rb = np.asarray(inputs["L0_ffn_router_b"])
    W1 = np.asarray(inputs["L0_ffn_W1"])
    b1 = np.asarray(inputs["L0_ffn_b1"])
    W2 = np.asarray(inputs["L0_ffn_W2"])
    b2 = np.asarray(inputs["L0_ffn_b2"])

    f = np.float32
    wkvg = np.empty((1024, 1056), f)
    for h in range(H):
        wkvg[:, h * 128:h * 128 + 64] = w_K[h].T
        wkvg[:, h * 128 + 64:h * 128 + 128] = w_V[h].T
    for e in range(E):
        for h in range(H):
            wkvg[:, 1024 + e * 8 + h] = w_G[h, e]
    wq = w_Q.transpose(2, 0, 1).reshape(1024, 512).astype(f)
    w1c = np.ascontiguousarray(W1.transpose(1, 0, 2).reshape(1024, 4096), f)
    w2c = np.ascontiguousarray(W2.reshape(4096, 1024), f)
    b1h = np.ascontiguousarray(b1.reshape(4096).reshape(32, 128).T, f)
    bgrow = np.empty(32, f)
    for e in range(E):
        for h in range(H):
            bgrow[e * 8 + h] = b_G[h, e]
    bgd = np.ascontiguousarray(np.broadcast_to(bgrow, (128, 32)), f)
    mskd = np.zeros((128, 160), f)
    for p in range(128):
        mskd[p, p:p + 32] = 1.0
    i4d = np.ascontiguousarray(np.tile(np.eye(128, dtype=f), (1, 4)))
    idfd = np.eye(128, dtype=f)
    on1d = np.ones((1, 128), f)
    ond2 = np.ones((128, 128), f)
    extd = np.empty((128, 128), f)
    for e in range(E):
        extd[0:64, e * 32:e * 32 + 32] = iK[e].T
        extd[64:128, e * 32:e * 32 + 32] = iV[e].T

    common = dict(wkvg=np.ascontiguousarray(wkvg), wqd=np.ascontiguousarray(wq),
                  wod=np.ascontiguousarray(w_O.astype(f)), w1d=w1c, w2d=w2c,
                  b1d=b1h, b2d=np.ascontiguousarray(b2.astype(f)),
                  rwd=np.ascontiguousarray(rw.astype(f)),
                  rbd=np.ascontiguousarray(rb.reshape(1, 8).astype(f)),
                  extd=extd, bgd=bgd, mskd=mskd, i4d=i4d, idfd=idfd,
                  on1d=on1d, ond2=ond2, epsd=np.full((128, 1), 1e-5, f))

    in_maps = []
    for c in range(8):
        n, half = divmod(c, 2)
        t0 = half * TOWN
        halo = X[n, t0 - 256:t0] if half == 1 else X[n, 0:256]
        xs_c = np.ascontiguousarray(np.concatenate([halo, X[n, t0:t0 + TOWN]], 0), f)
        lanes = ([15] + list(range(16, 32))) if half == 1 else ([0] + list(range(0, 16)))
        injd = np.empty((128, 68), f)
        for e in range(E):
            for li, lg in enumerate(lanes):
                injd[0:64, e * 17 + li] = iK[e, lg]
                injd[64:128, e * 17 + li] = iV[e, lg]
        fhd = np.full((128, 1), 1.0 if half == 0 else 0.0, f)
        m = dict(common)
        m.update(xs=xs_c, injd=injd, fhd=fhd)
        in_maps.append(m)
    return in_maps


def kernel(**inputs):
    if "nc" not in _CACHE:
        _CACHE["nc"] = build_nc()
    nc = _CACHE["nc"][0]
    in_maps = _host_arrays(inputs)
    res = run_bass_kernel_spmd(nc, in_maps, core_ids=list(range(8)))
    out = np.empty((4, 8192, 1024), np.float32)
    for c in range(8):
        n, half = divmod(c, 2)
        out[n, half * TOWN:(half + 1) * TOWN] = res.results[c]["y"]
    return out


# version 04
# speedup vs baseline: 1.1319x; 1.1319x over previous
import contextlib
import numpy as np
import ml_dtypes
import concourse.bass as bass
import concourse.mybir as mybir
from concourse.bass_utils import run_bass_kernel_spmd

F32 = mybir.dt.float32
BF16 = mybir.dt.bfloat16
AF = mybir.ActivationFunctionType
ALU = mybir.AluOpType
AX = mybir.AxisListType.X
BF = ml_dtypes.bfloat16

TH, NT = 4352, 34
TOWN, NTO, NG = 4096, 32, 8
D, NDC = 1024, 8
H, E = 8, 4
LANE, NLANE = 256, 17
EXTC, NCH = 4224, 33
DFF = 4096


def build_nc(debug=False):
    nc = bass.Bass("TRN2", target_bir_lowering=False)

    xs = nc.dram_tensor("xs", [TH, D], F32, kind="ExternalInput").ap()
    wkvg = nc.dram_tensor("wkvg", [D, 1056], F32, kind="ExternalInput").ap()
    wqd = nc.dram_tensor("wqd", [D, 512], F32, kind="ExternalInput").ap()
    wod = nc.dram_tensor("wod", [512, D], F32, kind="ExternalInput").ap()
    w1d = nc.dram_tensor("w1d", [D, DFF], F32, kind="ExternalInput").ap()
    w2d = nc.dram_tensor("w2d", [DFF, D], F32, kind="ExternalInput").ap()
    b1d = nc.dram_tensor("b1d", [128, 32], F32, kind="ExternalInput").ap()
    b2d = nc.dram_tensor("b2d", [8, D], F32, kind="ExternalInput").ap()
    rwd = nc.dram_tensor("rwd", [D, 8], F32, kind="ExternalInput").ap()
    rbd = nc.dram_tensor("rbd", [1, 8], F32, kind="ExternalInput").ap()
    injd = nc.dram_tensor("injd", [128, 68], F32, kind="ExternalInput").ap()
    extd = nc.dram_tensor("extd", [128, 128], F32, kind="ExternalInput").ap()
    fhd = nc.dram_tensor("fhd", [128, 1], F32, kind="ExternalInput").ap()
    bgd = nc.dram_tensor("bgd", [128, 32], F32, kind="ExternalInput").ap()
    mskd = nc.dram_tensor("mskd", [128, 160], F32, kind="ExternalInput").ap()
    i4d = nc.dram_tensor("i4d", [128, 512], F32, kind="ExternalInput").ap()
    idfd = nc.dram_tensor("idfd", [128, 128], F32, kind="ExternalInput").ap()
    on1d = nc.dram_tensor("on1d", [1, 128], F32, kind="ExternalInput").ap()
    epsd = nc.dram_tensor("epsd", [128, 1], F32, kind="ExternalInput").ap()
    ond2 = nc.dram_tensor("ond2", [128, 128], F32, kind="ExternalInput").ap()

    KI = "ExternalOutput" if debug else "Internal"
    scr_q = nc.dram_tensor("scr_q", [4, 128, TOWN], BF16, kind=KI).ap()
    scr_y = nc.dram_tensor("scr_y", [4, 128, TOWN], BF16, kind=KI).ap()
    scr_x1 = nc.dram_tensor("scr_x1", [TOWN, D], F32, kind=KI).ap()
    scr_xf = nc.dram_tensor("scr_xf", [NDC, 128, TOWN], BF16, kind=KI).ap()
    scr_o1 = nc.dram_tensor("scr_o1", [NDC, 128, TOWN], BF16, kind=KI).ap()
    if debug:
        dbg_g = nc.dram_tensor("dbg_g", [128, NT * 32], F32, kind="ExternalOutput").ap()
        dbg_a = nc.dram_tensor("dbg_a", [128, NT * 4], F32, kind="ExternalOutput").ap()
        dbg_ext = nc.dram_tensor("dbg_ext", [128, E * EXTC], F32, kind="ExternalOutput").ap()
        dbg_pt = nc.dram_tensor("dbg_pt", [128, E * NCH * 160], F32, kind="ExternalOutput").ap()
        dbg_r = nc.dram_tensor("dbg_r", [128, NTO * 8], F32, kind="ExternalOutput").ap()
        dbg_gkv = nc.dram_tensor("dbg_gkv", [128, E * TH], F32, kind="ExternalOutput").ap()
        dbg_abc = nc.dram_tensor("dbg_abc", [128, TH], F32, kind="ExternalOutput").ap()
        dbg_nkv = nc.dram_tensor("dbg_nkv", [128, TH], F32, kind="ExternalOutput").ap()
    y = nc.dram_tensor("y", [TOWN, D], F32, kind="ExternalOutput").ap()

    es = contextlib.ExitStack()
    semnames = [
        "dW", "dX", "dXT", "dQo", "dQi", "dVT", "dYo", "dYi", "dXR", "dX1s",
        "dX1T", "dXFs", "dXG", "dO1o", "dO1i", "dY", "dWF", "dXF", "dFT",
        "sKVG", "sPre", "sG", "sDiag", "sGKV", "sEvGKV", "sEvKV",
        "sA", "sQ", "sEvQ", "sDgA", "sBcA", "sEvA", "sScan", "sExt",
        "sSc", "sExp", "sMkV", "sMkG", "sPV", "sRcp", "sBcD", "sEvD", "sY",
        "sOP", "sEvOP", "sRes1", "sBN1", "sSq1", "sRst1", "sNrm1", "sX1b",
        "sRt", "sEvRt", "sRtd", "sSig", "sTop", "sRT", "sEvRT",
        "sDgR", "sBcR", "sEvR", "sH1", "sGel", "sScl", "sW2", "sEvW2",
        "sAdd2", "sBN2", "sSq2", "sRst2", "sNrm2", "dDbg", "sVs", "sAr", "sAh",
    ]
    S = {n: es.enter_context(nc.semaphore(n)) for n in semnames}
    _vs = {"n": 0}
    def VS(v, ins):
        _vs["n"] += 1
        ins.then_inc(S["sVs"], 1)
        v.wait_ge(S["sVs"], _vs["n"])

    i4 = es.enter_context(nc.sbuf_tensor("i4", [128, 512], BF16))
    idf = es.enter_context(nc.sbuf_tensor("idf", [128, 128], F32))
    on1 = es.enter_context(nc.sbuf_tensor("on1", [1, 128], BF16))
    onf = es.enter_context(nc.sbuf_tensor("onf", [128, 128], F32))
    onb = es.enter_context(nc.sbuf_tensor("onb", [128, 128], BF16))
    msk = es.enter_context(nc.sbuf_tensor("msk", [128, 160], BF16))
    fh = es.enter_context(nc.sbuf_tensor("fh", [128, 1], F32))
    epsb = es.enter_context(nc.sbuf_tensor("epsb", [128, 1], F32))
    bg = es.enter_context(nc.sbuf_tensor("bg", [128, 32], F32))
    inj = es.enter_context(nc.sbuf_tensor("inj", [128, 68], F32))
    extov = es.enter_context(nc.sbuf_tensor("extov", [128, 128], F32))
    rw_sb = es.enter_context(nc.sbuf_tensor("rw_sb", [128, 64], BF16))
    rb_sb = es.enter_context(nc.sbuf_tensor("rb_sb", [1, 8], BF16))
    b1_sb = es.enter_context(nc.sbuf_tensor("b1_sb", [128, 32], F32))
    b2_sb = es.enter_context(nc.sbuf_tensor("b2_sb", [8, 1024], BF16))
    g_tm = es.enter_context(nc.sbuf_tensor("g_tm", [128, NT * 32], F32))
    a_tm = es.enter_context(nc.sbuf_tensor("a_tm", [128, NT * 4], F32))
    a_hi = es.enter_context(nc.sbuf_tensor("a_hi", [128, NT * 4], BF16))
    a_lo = es.enter_context(nc.sbuf_tensor("a_lo", [128, NT * 4], F32))
    a_hif = es.enter_context(nc.sbuf_tensor("a_hif", [128, NT * 4], F32))
    stat = es.enter_context(nc.sbuf_tensor("stat", [128, 96], F32))
    r_tm = es.enter_context(nc.sbuf_tensor("r_tm", [128, NTO * 8], F32))
    rstat = es.enter_context(nc.sbuf_tensor("rstat", [128, 128], F32))
    lbuf = es.enter_context(nc.sbuf_tensor("lbuf", [128, 16], F32))
    rcast = es.enter_context(nc.sbuf_tensor("rcast", [128, 16], BF16))

    pall = es.enter_context(nc.psum_tensor("pall", [128, 3584], F32))
    pbf = es.enter_context(nc.psum_tensor("pbf", [128, 192], BF16))

    def PB(k, off=0, w=512, p0=0, pn=128):
        return pall[p0:p0 + pn, k * 512 + off:k * 512 + off + w]

    blk = es.enter_context(nc.Block())

    esA2 = contextlib.ExitStack()
    gkv_fm = esA2.enter_context(nc.sbuf_tensor("gkv_fm", [128, E * TH], BF16, side="right"))
    esA1 = contextlib.ExitStack()
    x_tm = esA1.enter_context(nc.sbuf_tensor("x_tm", [128, 2 * 1024], BF16))
    x_fm = esA1.enter_context(nc.sbuf_tensor("x_fm", [128, NDC * TH], BF16,
                                             is_transpose_dest=True))
    wkvg_sb = esA1.enter_context(nc.sbuf_tensor("wkvg_sb", [128, NDC * 1056], BF16))
    wq_sb = esA1.enter_context(nc.sbuf_tensor("wq_sb", [128, NDC * 512], BF16))
    kv_tm = esA1.enter_context(nc.sbuf_tensor("kv_tm", [128, 2 * 1024], BF16))
    dg = esA1.enter_context(nc.sbuf_tensor("dg", [128, 4096], BF16))
    gpre = esA1.enter_context(nc.sbuf_tensor("gpre", [128, 64], F32))
    qev = esA1.enter_context(nc.sbuf_tensor("qev", [128, 2 * 512], BF16))

    @blk.gpsimd
    def _(gp):
        for dst, src in [(i4[:], i4d), (idf[:], idfd), (on1[0:1, :], on1d),
                         (onf[:], ond2), (onb[:], ond2), (msk[:], mskd),
                         (fh[:], fhd), (epsb[:], epsd), (bg[:], bgd), (inj[:], injd),
                         (extov[:], extd), (rb_sb[0:1, :], rbd),
                         (b1_sb[:], b1d), (b2_sb[0:8, :], b2d)]:
            gp.dma_start(out=dst, in_=src).then_inc(S["dW"], 16)
        for dc in range(NDC):
            gp.dma_start(out=rw_sb[:, dc * 8:dc * 8 + 8],
                         in_=rwd[dc * 128:dc * 128 + 128, :]).then_inc(S["dW"], 16)
            gp.dma_start(out=wkvg_sb[:, dc * 1056:dc * 1056 + 1056],
                         in_=wkvg[dc * 128:dc * 128 + 128, :]).then_inc(S["dW"], 16)
            gp.dma_start(out=wq_sb[:, dc * 512:dc * 512 + 512],
                         in_=wqd[dc * 128:dc * 128 + 128, :]).then_inc(S["dW"], 16)
        for i in range(NT):
            if i >= 2:
                gp.wait_ge(S["dXT"], (i - 1) * 128)
            gp.dma_start(out=x_tm[:, (i % 2) * 1024:(i % 2) * 1024 + 1024],
                         in_=xs[i * 128:i * 128 + 128, :]).then_inc(S["dX"], 16)
    NW0 = 38

    @blk.scalar
    def _(sc):
        for i in range(NT):
            sc.wait_ge(S["dX"], (i + 1) * 16)
            for dc in range(NDC):
                sc.dma_start(out=x_fm[:, dc * TH + i * 128:dc * TH + i * 128 + 128],
                             in_=x_tm[:, (i % 2) * 1024 + dc * 128:(i % 2) * 1024 + dc * 128 + 128],
                             transpose=True).then_inc(S["dXT"], 16)

    @blk.tensor
    def _(t):
        t.wait_ge(S["dW"], NW0 * 16)
        nq = 0
        for i in range(NT):
            t.wait_ge(S["dXT"], (i + 1) * 128)
            t.wait_ge(S["sPre"], i)
            if i >= 2:
                t.wait_ge(S["sEvKV"], (i - 1) * 2)
            for dc in range(NDC):
                lhs = x_fm[:, dc * TH + i * 128:dc * TH + i * 128 + 128]
                t.matmul(PB(0 + i % 2), lhs, wkvg_sb[:, dc * 1056:dc * 1056 + 512],
                         start=(dc == 0), stop=(dc == NDC - 1))
                t.matmul(PB(2 + i % 2), lhs, wkvg_sb[:, dc * 1056 + 512:dc * 1056 + 1024],
                         start=(dc == 0), stop=(dc == NDC - 1))
                mm = t.matmul(PB(5, 0, 32), lhs,
                              wkvg_sb[:, dc * 1056 + 1024:dc * 1056 + 1056],
                              start=(dc == 0), stop=(dc == NDC - 1))
            mm.then_inc(S["sKVG"], 1)
            for ii in ([i - 1] if i >= 1 else []) + ([NT - 1] if i == NT - 1 else []):
                t.wait_ge(S["sDiag"], 2 * ii + 2)
                t.wait_ge(S["sEvKV"], 2 * ii + 2)
                if ii >= 1:
                    t.wait_ge(S["sEvGKV"], ii)
                for hh in range(H):
                    mm = t.matmul(PB(4),
                                  kv_tm[:, (ii % 2) * 1024 + hh * 128:(ii % 2) * 1024 + hh * 128 + 128],
                                  dg[:, hh * 512:hh * 512 + 512],
                                  start=(hh == 0), stop=(hh == H - 1))
                mm.then_inc(S["sGKV"], 1)
            if i >= 5 and (i - 5) % 4 == 0:
                gg = (i - 5) // 4
                for qc in range(4):
                    if nq >= 1:
                        t.wait_ge(S["sEvQ"], nq)
                    for dc in range(NDC):
                        mm = t.matmul(PB(6), wq_sb[:, dc * 512 + qc * 128:dc * 512 + qc * 128 + 128],
                                      x_fm[:, dc * TH + 256 + gg * 512:dc * TH + 256 + gg * 512 + 512],
                                      start=(dc == 0), stop=(dc == NDC - 1))
                    mm.then_inc(S["sQ"], 1)
                    nq += 1

    @blk.vector
    def _(v):
        nq = 0
        for i in range(NT):
            v.wait_ge(S["sKVG"], i + 1)
            v.tensor_tensor(gpre[:, (i % 2) * 32:(i % 2) * 32 + 32],
                            PB(5, 0, 32), bg[:], ALU.add).then_inc(S["sPre"], 1)
            if i >= 2:
                v.wait_ge(S["sGKV"], i - 1)
            v.tensor_copy(kv_tm[:, (i % 2) * 1024:(i % 2) * 1024 + 512],
                          PB(0 + i % 2)).then_inc(S["sEvKV"], 1)
            v.tensor_copy(kv_tm[:, (i % 2) * 1024 + 512:(i % 2) * 1024 + 1024],
                          PB(2 + i % 2)).then_inc(S["sEvKV"], 1)
            v.wait_ge(S["sG"], i + 1)
            if i >= 2:
                v.wait_ge(S["sA"], i - 1)
            ab = 64 + (i % 2) * 8
            for e in range(E):
                ins = v.reduce_sum(stat[:, ab + e:ab + e + 1],
                                   g_tm[:, i * 32 + e * 8:i * 32 + e * 8 + 8],
                                   axis=AX, negate=True)
            ins.then_inc(S["sAr"], 1)
            for ii0 in ([i - 1] if i >= 1 else []) + ([NT - 1] if i == NT - 1 else []):
                v.wait_ge(S["sA"], ii0 + 1)
                ins = v.tensor_copy(a_hi[:, ii0 * 4:ii0 * 4 + 4], a_tm[:, ii0 * 4:ii0 * 4 + 4])
                VS(v, ins)
                ins = v.tensor_copy(a_hif[:, ii0 * 4:ii0 * 4 + 4], a_hi[:, ii0 * 4:ii0 * 4 + 4])
                VS(v, ins)
                v.tensor_sub(a_lo[:, ii0 * 4:ii0 * 4 + 4], a_tm[:, ii0 * 4:ii0 * 4 + 4],
                             a_hif[:, ii0 * 4:ii0 * 4 + 4]).then_inc(S["sAh"], 1)
            if i >= 1:
                v.wait_ge(S["sGKV"], i)
            for hh in range(4):
                for e in range(E):
                    ins = v.tensor_scalar(
                        dg[:, hh * 512 + e * 128:hh * 512 + e * 128 + 128],
                        i4[:, 0:128],
                        g_tm[:, i * 32 + e * 8 + hh:i * 32 + e * 8 + hh + 1],
                        None, ALU.mult)
            ins.then_inc(S["sDiag"], 1)
            for ii in ([i - 1] if i >= 1 else []) + ([NT - 1] if i == NT - 1 else []):
                v.wait_ge(S["sGKV"], ii + 1)
                dst = gkv_fm[:].rearrange("p (e t) -> p e t", e=E)[:, :, ii * 128:ii * 128 + 128]
                v.tensor_copy(dst, PB(4)).then_inc(S["sEvGKV"], 1)
            if i >= 5 and (i - 5) % 4 == 0:
                for qc in range(4):
                    v.wait_ge(S["sQ"], nq + 1)
                    if nq >= 2:
                        v.wait_ge(S["dQo"], (nq - 1) * 16)
                    v.tensor_copy(qev[:, (nq % 2) * 512:(nq % 2) * 512 + 512],
                                  PB(6)).then_inc(S["sEvQ"], 1)
                    nq += 1

    @blk.scalar
    def _(sc):
        for i in range(NT):
            sc.wait_ge(S["sPre"], i + 1)
            sc.activation(g_tm[:, i * 32:i * 32 + 32],
                          gpre[:, (i % 2) * 32:(i % 2) * 32 + 32],
                          AF.Sigmoid, bias=0.0, scale=1.0).then_inc(S["sG"], 1)
            sc.wait_ge(S["sAr"], i + 1)
            sc.activation(a_tm[:, i * 4:i * 4 + 4],
                          stat[:, 64 + (i % 2) * 8:64 + (i % 2) * 8 + 4],
                          AF.Relu, bias=1.0, scale=1.0).then_inc(S["sA"], 1)
            for ii2 in ([i - 1] if i >= 1 else []) + ([NT - 1] if i == NT - 1 else []):
                if ii2 >= 1:
                    sc.wait_ge(S["sGKV"], ii2)
                for hh in range(4, 8):
                    for e in range(E):
                        ins2 = sc.activation(
                            dg[:, hh * 512 + e * 128:hh * 512 + e * 128 + 128],
                            i4[:, 0:128], AF.Copy, bias=0.0,
                            scale=g_tm[:, ii2 * 32 + e * 8 + hh:ii2 * 32 + e * 8 + hh + 1])
                ins2.then_inc(S["sDiag"], 1)

    @blk.sync
    def _(s):
        for k in range(32):
            gg, qc = divmod(k, 4)
            s.wait_ge(S["sEvQ"], k + 1)
            s.dma_start(out=scr_q[qc][0:128, gg * 512:gg * 512 + 512],
                        in_=qev[:, (k % 2) * 512:(k % 2) * 512 + 512]).then_inc(S["dQo"], 16)

    esA1.close()
    esB2 = contextlib.ExitStack()
    kvext = esB2.enter_context(nc.sbuf_tensor("kvext", [128, E * EXTC], BF16))
    vtm = esB2.enter_context(nc.sbuf_tensor("vtm", [128, E * NCH * 80], BF16,
                                            is_transpose_dest=True))
    esB1 = contextlib.ExitStack()
    a_bc = esB1.enter_context(nc.sbuf_tensor("a_bc", [128, TH], F32))
    dga = esB1.enter_context(nc.sbuf_tensor("dga", [128, 512], BF16))
    nkv = esB1.enter_context(nc.sbuf_tensor("nkv", [128, TH], F32))

    @blk.tensor
    def _(t):
        t.wait_ge(S["sEvGKV"], NT)
        for e in range(E):
            for i in range(NT):
                k = e * NT + i
                t.wait_ge(S["sDgA"], k + 1)
                if k >= 2:
                    t.wait_ge(S["sEvA"], k - 1)
                t.matmul(PB(4 + k % 2, 0, 128), onb[:],
                         dga[:, (k % 2) * 256:(k % 2) * 256 + 128],
                         start=True, stop=False)
                t.matmul(PB(4 + k % 2, 0, 128), onb[:],
                         dga[:, (k % 2) * 256 + 128:(k % 2) * 256 + 256],
                         start=False, stop=True).then_inc(S["sBcA"], 1)

    @blk.vector
    def _(v):
        for e in range(E):
            if debug and e == 1:
                v.wait_ge(S["dDbg"], 4 * 16)
            if e == 0:
                v.wait_ge(S["sQ"], 32)
            for i in range(NT):
                k = e * NT + i
                v.wait_ge(S["sAh"], i + 1)
                if k >= 2:
                    v.wait_ge(S["sBcA"], k - 1)
                v.tensor_scalar(dga[:, (k % 2) * 256:(k % 2) * 256 + 128],
                                i4[:, 0:128], a_hif[:, i * 4 + e:i * 4 + e + 1],
                                None, ALU.mult)
                v.tensor_scalar(dga[:, (k % 2) * 256 + 128:(k % 2) * 256 + 256],
                                i4[:, 0:128], a_lo[:, i * 4 + e:i * 4 + e + 1],
                                None, ALU.mult).then_inc(S["sDgA"], 1)
            v.wait_ge(S["sEvA"], (e + 1) * NT)
            acols = a_bc[:, 0:TH:LANE]
            ins = v.tensor_tensor(stat[:, 0:17], inj[:, e * 17:e * 17 + 17], acols, ALU.mult)
            VS(v, ins)
            gcols = gkv_fm[:, e * TH:e * TH + TH:LANE]
            ins = v.tensor_tensor(gcols, gcols, stat[:, 0:17], ALU.add)
            VS(v, ins)
            ins = v.memset(acols, 0.0)
            VS(v, ins)
            v.tensor_tensor_scan(nkv[:], a_bc[:], gkv_fm[:, e * TH:e * TH + TH],
                                 0.0, ALU.mult, ALU.add).then_inc(S["sScan"], 1)
            v.wait_ge(S["sScan"], e + 1)
            v.memset(kvext[:, e * EXTC + 4128:e * EXTC + 4224], 0.0)
            ins = v.tensor_sub(stat[:, 0:32], extov[:, e * 32:e * 32 + 32], nkv[:, 224:256])
            VS(v, ins)
            v.scalar_tensor_tensor(kvext[:, e * EXTC:e * EXTC + 32],
                                   stat[:, 0:32], fh[:, 0:1], nkv[:, 224:256],
                                   ALU.mult, ALU.add)
            v.tensor_copy(kvext[:, e * EXTC + 32:e * EXTC + 4128], nkv[:, 256:TH])
            ins = v.memset(vtm[:, e * NCH * 80 + 64:(e + 1) * NCH * 80:80], 1.0)
            ins.then_inc(S["sExt"], 1)

    @blk.scalar
    def _(sc):
        for e in range(E):
            if e >= 1:
                sc.wait_ge(S["sScan"], e)
            for i in range(NT):
                k = e * NT + i
                sc.wait_ge(S["sBcA"], k + 1)
                sc.activation(a_bc[:, i * 128:i * 128 + 128],
                              PB(4 + k % 2, 0, 128), AF.Copy,
                              bias=0.0, scale=1.0).then_inc(S["sEvA"], 1)


    if debug:
        @blk.gpsimd
        def _(gp):
            gp.wait_ge(S["sG"], NT)
            gp.dma_start(out=dbg_g, in_=g_tm[:]).then_inc(S["dDbg"], 16)
            gp.wait_ge(S["sA"], NT)
            gp.dma_start(out=dbg_a, in_=a_tm[:]).then_inc(S["dDbg"], 16)
            gp.wait_ge(S["sScan"], 1)
            gp.dma_start(out=dbg_abc, in_=a_bc[:]).then_inc(S["dDbg"], 16)
            gp.dma_start(out=dbg_nkv, in_=nkv[:]).then_inc(S["dDbg"], 16)
            gp.wait_ge(S["sScan"], E)
            gp.dma_start(out=dbg_gkv, in_=gkv_fm[:]).then_inc(S["dDbg"], 16)
            gp.wait_ge(S["sExt"], E)
            gp.dma_start(out=dbg_ext, in_=kvext[:]).then_inc(S["dDbg"], 16)

    esB1.close()
    esA2.close()
    esB3 = contextlib.ExitStack()
    qh = esB3.enter_context(nc.sbuf_tensor("qh", [64, 2 * TOWN], BF16))
    pt = esB3.enter_context(nc.sbuf_tensor("pt", [128, E * NCH * 160], BF16))
    dinv = esB3.enter_context(nc.sbuf_tensor("dinv", [1, 1024], F32))
    dinvb = esB3.enter_context(nc.sbuf_tensor("dinvb", [1, 1024], BF16))
    dbc = esB3.enter_context(nc.sbuf_tensor("dbc", [64, 1024], F32))
    ysc = esB3.enter_context(nc.sbuf_tensor("ysc", [64, 1024], BF16))

    def chunk_cols(c):
        return max(c * 128 - 32, 0), min(c * 128 + 126, TOWN - 1)

    @blk.sync
    def _(s):
        for e in range(E):
            s.wait_ge(S["sExt"], e + 1)
            for c in range(NCH):
                s.dma_start(out=vtm[:, (e * NCH + c) * 80:(e * NCH + c) * 80 + 64],
                             in_=kvext[64:128, e * EXTC + c * 128:e * EXTC + c * 128 + 128],
                             transpose=True).then_inc(S["dVT"], 16)
        s.wait_ge(S["dQo"], 32 * 16)
        for hh in range(2):
            s.dma_start(out=qh[0:64, (hh % 2) * TOWN:(hh % 2) * TOWN + TOWN],
                        in_=scr_q[hh // 2][(hh % 2) * 64:(hh % 2) * 64 + 64, :]).then_inc(S["dQi"], 16)

    @blk.tensor
    def _(t):
        t.wait_ge(S["dVT"], E * NCH * 16)
        for hh in range(H):
            t.wait_ge(S["dQi"], (hh + 1) * 16)
            gnext = 0
            for c in range(NCH):
                b0 = (c % 2) * 2
                if c >= 2:
                    t.wait_ge(S["sExp"], hh * NCH + c - 1)
                elif hh >= 1:
                    t.wait_ge(S["sExp"], hh * NCH - (1 if c == 1 else 0))
                lo, hi = chunk_cols(c)
                n = hi - lo + 1
                joff = lo - (c * 128 - 32)
                for e in range(E):
                    off = [0, 160, 320, 0][e]
                    bank = b0 if e < 3 else b0 + 1
                    mm = t.matmul(PB(bank, off + joff, n),
                                  kvext[0:64, e * EXTC + c * 128:e * EXTC + c * 128 + 128],
                                  qh[0:64, (hh % 2) * TOWN + lo:(hh % 2) * TOWN + lo + n],
                                  start=True, stop=True)
                mm.then_inc(S["sSc"], 1)
                while gnext < NG and c == min(4 * gnext + 4, NCH - 1):
                    gg = gnext
                    k = hh * NG + gg
                    t.wait_ge(S["sMkV"], hh * NCH + c + 1)
                    t.wait_ge(S["sMkG"], hh * NCH + c + 1)
                    if k >= 2:
                        t.wait_ge(S["sY"], k - 1)
                    plan = []
                    for e in range(E):
                        for cc in range(4 * gg, min(4 * gg + 5, NCH)):
                            tlo, thi = chunk_cols(cc)
                            lo2, hi2 = max(tlo, gg * 512), min(thi, gg * 512 + 511)
                            if lo2 <= hi2:
                                plan.append((e, cc, lo2, hi2))
                    for idx, (e, cc, lo2, hi2) in enumerate(plan):
                        j0 = lo2 - (cc * 128 - 32)
                        n2 = hi2 - lo2 + 1
                        mm = t.matmul(PB(4 + k % 2, lo2 - gg * 512, n2, 0, 65),
                                      vtm[:, (e * NCH + cc) * 80:(e * NCH + cc) * 80 + 65],
                                      pt[:, (e * NCH + cc) * 160 + j0:(e * NCH + cc) * 160 + j0 + n2],
                                      start=(idx == 0), stop=(idx == len(plan) - 1))
                    mm.then_inc(S["sPV"], 1)
                    gnext += 1
                    if gg >= 1:
                        kk = k - 1
                        t.wait_ge(S["sRcp"], kk + 1)
                        if kk >= 1:
                            t.wait_ge(S["sEvD"], kk)
                        t.matmul(PB(6, 0, 512, 0, 64), on1[0:1, 0:64],
                                 dinvb[0:1, (kk % 2) * 512:(kk % 2) * 512 + 512],
                                 start=True, stop=True).then_inc(S["sBcD"], 1)
            kk = hh * NG + NG - 1
            t.wait_ge(S["sRcp"], kk + 1)
            t.wait_ge(S["sEvD"], kk)
            t.matmul(PB(6, 0, 512, 0, 64), on1[0:1, 0:64],
                     dinvb[0:1, (kk % 2) * 512:(kk % 2) * 512 + 512],
                     start=True, stop=True).then_inc(S["sBcD"], 1)

    @blk.scalar
    def _(sc):
        for hh in range(H):
            for c in range(NCH):
                b0 = (c % 2) * 2
                sc.wait_ge(S["sSc"], hh * NCH + c + 1)
                dst3 = pt[:].rearrange("p (e cj) -> p e cj", e=E)[:, 0:3,
                                                                 c * 160:c * 160 + 160]
                src3 = pall[:, b0 * 512:b0 * 512 + 480].rearrange("p (e j) -> p e j", e=3)
                sc.activation(dst3, src3, AF.Exp, bias=0.0, scale=0.125)
                sc.activation(pt[:, (3 * NCH + c) * 160:(3 * NCH + c) * 160 + 160],
                              PB(b0 + 1, 0, 160), AF.Exp,
                              bias=0.0, scale=0.125).then_inc(S["sExp"], 1)
                if c >= 8 and (c - 8) % 4 == 0 and (c - 8) // 4 < NG - 1:
                    gg = (c - 8) // 4
                    k = hh * NG + gg
                    sc.wait_ge(S["sBcD"], k + 1)
                    sc.activation(dbc[0:64, (k % 2) * 512:(k % 2) * 512 + 512],
                                  PB(6, 0, 512, 0, 64), AF.Copy,
                                  bias=0.0, scale=1.0).then_inc(S["sEvD"], 1)
            for gg in [NG - 2, NG - 1]:
                k = hh * NG + gg
                sc.wait_ge(S["sBcD"], k + 1)
                sc.activation(dbc[0:64, (k % 2) * 512:(k % 2) * 512 + 512],
                              PB(6, 0, 512, 0, 64), AF.Copy,
                              bias=0.0, scale=1.0).then_inc(S["sEvD"], 1)

    @blk.vector
    def _(v):
        for hh in range(H):
            gnext = 0
            for c in range(NCH):
                v.wait_ge(S["sExp"], hh * NCH + c + 1)
                for e in range(2):
                    pc = pt[:, (e * NCH + c) * 160:(e * NCH + c) * 160 + 160]
                    ins = v.tensor_tensor(pc, pc, msk[:], ALU.mult)
                ins.then_inc(S["sMkV"], 1)
                while gnext < NG and c == min(4 * gnext + 4, NCH - 1):
                    gg = gnext
                    k = hh * NG + gg
                    v.wait_ge(S["sPV"], k + 1)
                    ins = v.reciprocal(dinv[0:1, (k % 2) * 512:(k % 2) * 512 + 512],
                                       PB(4 + k % 2, 0, 512, 64, 1))
                    VS(v, ins)
                    v.tensor_copy(dinvb[0:1, (k % 2) * 512:(k % 2) * 512 + 512],
                                  dinv[0:1, (k % 2) * 512:(k % 2) * 512 + 512]).then_inc(S["sRcp"], 1)
                    if gg >= 1:
                        kk = k - 1
                        v.wait_ge(S["sEvD"], kk + 1)
                        if kk >= 2:
                            v.wait_ge(S["dYo"], (kk - 1) * 16)
                        v.scalar_tensor_tensor(
                            ysc[0:64, (kk % 2) * 512:(kk % 2) * 512 + 512],
                            PB(4 + kk % 2, 0, 512, 0, 64), 1.0,
                            dbc[0:64, (kk % 2) * 512:(kk % 2) * 512 + 512],
                            ALU.mult, ALU.mult).then_inc(S["sY"], 1)
                    gnext += 1
            kk = hh * NG + NG - 1
            v.wait_ge(S["sEvD"], kk + 1)
            v.wait_ge(S["dYo"], (kk - 1) * 16)
            v.scalar_tensor_tensor(ysc[0:64, (kk % 2) * 512:(kk % 2) * 512 + 512],
                                   PB(4 + kk % 2, 0, 512, 0, 64), 1.0,
                                   dbc[0:64, (kk % 2) * 512:(kk % 2) * 512 + 512],
                                   ALU.mult, ALU.mult).then_inc(S["sY"], 1)

    @blk.gpsimd
    def _(gp):
        for hh in range(H):
            for c in range(NCH):
                gp.wait_ge(S["sExp"], hh * NCH + c + 1)
                for e in range(2, 4):
                    pc = pt[:, (e * NCH + c) * 160:(e * NCH + c) * 160 + 160]
                    ins = gp.tensor_tensor(pc, pc, msk[:], ALU.mult)
                ins.then_inc(S["sMkG"], 1)

    @blk.sync
    def _(s):
        for hh in range(H):
            for gg in range(NG):
                k = hh * NG + gg
                s.wait_ge(S["sY"], k + 1)
                s.dma_start(out=scr_y[hh // 2][(hh % 2) * 64:(hh % 2) * 64 + 64,
                                               gg * 512:gg * 512 + 512],
                            in_=ysc[0:64, (k % 2) * 512:(k % 2) * 512 + 512]).then_inc(S["dYo"], 16)
            if hh + 2 < H:
                h2 = hh + 2
                s.wait_ge(S["sSc"], (h2 - 1) * NCH)
                s.dma_start(out=qh[0:64, (h2 % 2) * TOWN:(h2 % 2) * TOWN + TOWN],
                            in_=scr_q[h2 // 2][(h2 % 2) * 64:(h2 % 2) * 64 + 64, :]).then_inc(S["dQi"], 16)

    if debug:
        @blk.gpsimd
        def _(gp):
            gp.wait_ge(S["sMkV"], H * NCH)
            gp.wait_ge(S["sMkG"], H * NCH)
            gp.dma_start(out=dbg_pt, in_=pt[:]).then_inc(S["dDbg"], 16)
            gp.wait_ge(S["dDbg"], 7 * 16)

    esB3.close()
    esB2.close()
    esC = contextlib.ExitStack()
    wo_sb = esC.enter_context(nc.sbuf_tensor("wo_sb", [128, 4 * 1024], BF16))
    ybuf = esC.enter_context(nc.sbuf_tensor("ybuf", [128, 2048], BF16))
    attn_fm = esC.enter_context(nc.sbuf_tensor("attn_fm", [128, 8 * 512], BF16))
    attn_tm = esC.enter_context(nc.sbuf_tensor("attn_tm", [128, 6 * 1024], BF16,
                                               is_transpose_dest=True))
    xres = esC.enter_context(nc.sbuf_tensor("xres", [128, 2 * 1024], F32))
    rres = esC.enter_context(nc.sbuf_tensor("rres", [128, 2 * 1024], F32))
    x1_tm = esC.enter_context(nc.sbuf_tensor("x1_tm", [128, 2 * 1024], F32))
    x1b = esC.enter_context(nc.sbuf_tensor("x1b", [128, 2 * 1024], BF16))
    x1t = esC.enter_context(nc.sbuf_tensor("x1t", [128, 2 * 1024], BF16,
                                           is_transpose_dest=True))
    r_fm = esC.enter_context(nc.sbuf_tensor("r_fm", [8, TOWN], BF16))
    x1g = esC.enter_context(nc.sbuf_tensor("x1g", [128, 2 * 4096], BF16))
    h1g = esC.enter_context(nc.sbuf_tensor("h1g", [128, 16 * 512], BF16))
    h1p = esC.enter_context(nc.sbuf_tensor("h1p", [128, 2 * 512], BF16))
    r_bc = esC.enter_context(nc.sbuf_tensor("r_bc", [128, 2 * 2048], BF16))
    dgr = esC.enter_context(nc.sbuf_tensor("dgr", [128, 256], BF16))
    o1ev = esC.enter_context(nc.sbuf_tensor("o1ev", [128, 2 * 512], BF16))
    o1in = esC.enter_context(nc.sbuf_tensor("o1in", [128, 2 * 512], BF16))
    ffn_fm = esC.enter_context(nc.sbuf_tensor("ffn_fm", [128, 2 * 512], BF16))
    ffn_tm = esC.enter_context(nc.sbuf_tensor("ffn_tm", [128, 4 * 1024], BF16,
                                              is_transpose_dest=True))
    esF = contextlib.ExitStack()
    w1h = esF.enter_context(nc.sbuf_tensor("w1h", [128, NDC * 2048], BF16))
    w2h = esF.enter_context(nc.sbuf_tensor("w2h", [128, 16 * 1024], BF16))

    @blk.gpsimd
    def _(gp):
        gp.wait_ge(S["sY"], 64)
        for qc in range(4):
            gp.dma_start(out=wo_sb[:, qc * 1024:qc * 1024 + 1024],
                         in_=wod[qc * 128:qc * 128 + 128, :]).then_inc(S["dW"], 16)
        for half in range(2):
            if half == 1:
                gp.wait_ge(S["sW2"], 64)
            for dc in range(NDC):
                gp.dma_start(out=w1h[:, dc * 2048:dc * 2048 + 2048],
                             in_=w1d[dc * 128:dc * 128 + 128,
                                     half * 2048:half * 2048 + 2048]).then_inc(S["dWF"], 16)
            for fl in range(16):
                gp.dma_start(out=w2h[:, fl * 1024:fl * 1024 + 1024],
                             in_=w2d[(half * 16 + fl) * 128:(half * 16 + fl) * 128 + 128, :]).then_inc(S["dWF"], 16)

    scr_y_r = scr_y.rearrange("q p t -> p q t")
    scr_xf_r = scr_xf.rearrange("c p t -> p c t")

    @blk.sync
    def _(s):
        s.wait_ge(S["dYo"], 64 * 16)
        s.dma_start(out=ybuf[:, 0:2048],
                    in_=scr_y_r[:, :, 0:512]).then_inc(S["dYi"], 16)
        def _p5store(s, tt):
            s.wait_ge(S["sNrm1"], tt + 1)
            s.dma_start(out=scr_x1[tt * 128:tt * 128 + 128, :],
                        in_=x1_tm[:, (tt % 2) * 1024:(tt % 2) * 1024 + 1024]).then_inc(S["dX1s"], 16)
            s.wait_ge(S["dX1T"], (tt + 1) * 128)
            s.dma_start(out=scr_xf_r[:, :, tt * 128:tt * 128 + 128],
                        in_=x1t[:, (tt % 2) * 1024:(tt % 2) * 1024 + 1024]).then_inc(S["dXFs"], 16)
        for gg in range(NG):
            if gg >= 1:
                for s4 in range(4):
                    _p5store(s, (gg - 1) * 4 + s4)
            if gg >= 1:
                s.wait_ge(S["sRes1"], max(0, 4 * gg - 2))
            for dc in range(NDC):
                k = gg * NDC + dc
                s.wait_ge(S["sEvOP"], k + 1)
                for s4 in range(4):
                    tt = gg * 4 + s4
                    s.dma_start(out=attn_tm[:, (tt % 6) * 1024 + dc * 128:(tt % 6) * 1024 + dc * 128 + 128],
                                 in_=attn_fm[:, (k % 8) * 512 + s4 * 128:(k % 8) * 512 + s4 * 128 + 128],
                                 transpose=True).then_inc(S["dXF"], 16)
            for s4 in range(4):
                tt = gg * 4 + s4
                if tt >= 2:
                    s.wait_ge(S["sRes1"], tt - 1)
                s.dma_start(out=xres[:, (tt % 2) * 1024:(tt % 2) * 1024 + 1024],
                            in_=xs[256 + tt * 128:256 + tt * 128 + 128, :]).then_inc(S["dXR"], 16)
            if gg + 1 < NG:
                s.wait_ge(S["sOP"], (gg + 1) * NDC)
                s.dma_start(out=ybuf[:, 0:2048],
                            in_=scr_y_r[:, :, (gg + 1) * 512:(gg + 1) * 512 + 512]).then_inc(S["dYi"], 16)
        for s4 in range(4):
            _p5store(s, (NG - 1) * 4 + s4)
        no1 = 0
        for half in range(2):
            for gg in range(NG):
                hg = half * NG + gg
                s.wait_ge(S["dXFs"], (4 * gg + 4) * 16)
                if hg >= 2:
                    s.wait_ge(S["sH1"], (hg - 1) * 16)
                s.dma_start(out=x1g[:, (gg % 2) * 4096:(gg % 2) * 4096 + 4096],
                            in_=scr_xf_r[:, :, gg * 512:gg * 512 + 512]).then_inc(S["dXG"], 16)
                if half == 1 and gg >= 1:
                    for s4 in range(4):
                        tt = (gg - 1) * 4 + s4
                        if tt >= 2:
                            s.wait_ge(S["sAdd2"], tt - 1)
                        else:
                            s.wait_ge(S["dX1s"], NTO * 16)
                        s.dma_start(out=x1_tm[:, (tt % 2) * 1024:(tt % 2) * 1024 + 1024],
                                    in_=scr_x1[tt * 128:tt * 128 + 128, :]).then_inc(S["dXR"], 16)
                    for s4 in range(4):
                        tt = (gg - 1) * 4 + s4
                        s.wait_ge(S["sNrm2"], tt + 1)
                        s.dma_start(out=y[tt * 128:tt * 128 + 128, :],
                                    in_=xres[:, (tt % 2) * 1024:(tt % 2) * 1024 + 1024]).then_inc(S["dY"], 16)
                def _ffnTd(s, gg, dc):
                    ko = gg * NDC + dc
                    s.wait_ge(S["sEvW2"], 64 + ko + 1)
                    for s4 in range(4):
                        tt = gg * 4 + s4
                        s.dma_start(out=ffn_tm[:, (tt % 4) * 1024 + dc * 128:(tt % 4) * 1024 + dc * 128 + 128],
                                    in_=ffn_fm[:, (ko % 2) * 512 + s4 * 128:(ko % 2) * 512 + s4 * 128 + 128],
                                    transpose=True).then_inc(S["dFT"], 16)
                for dc in range(NDC):
                    k2 = gg * NDC + dc
                    if half == 0:
                        s.wait_ge(S["sEvW2"], k2 + 1)
                        s.dma_start(out=scr_o1[dc][0:128, gg * 512:gg * 512 + 512],
                                    in_=o1ev[:, (k2 % 2) * 512:(k2 % 2) * 512 + 512]).then_inc(S["dO1o"], 16)
                    else:
                        if no1 >= 2:
                            s.wait_ge(S["sEvW2"], 64 + no1 - 1)
                        s.dma_start(out=o1in[:, (k2 % 2) * 512:(k2 % 2) * 512 + 512],
                                    in_=scr_o1[dc][0:128, gg * 512:gg * 512 + 512]).then_inc(S["dO1i"], 16)
                        no1 += 1
                        if dc == 2 and gg >= 1:
                            s.wait_ge(S["sAdd2"], gg * 4)
                        if dc >= 2:
                            _ffnTd(s, gg, dc - 2)
                if half == 1:
                    _ffnTd(s, gg, NDC - 2)
                    _ffnTd(s, gg, NDC - 1)
        for tt in range((NG - 1) * 4, NTO):
            s.wait_ge(S["sAdd2"], tt - 1)
            s.dma_start(out=x1_tm[:, (tt % 2) * 1024:(tt % 2) * 1024 + 1024],
                        in_=scr_x1[tt * 128:tt * 128 + 128, :]).then_inc(S["dXR"], 16)
        for tt in range((NG - 1) * 4, NTO):
            s.wait_ge(S["sNrm2"], tt + 1)
            s.dma_start(out=y[tt * 128:tt * 128 + 128, :],
                        in_=xres[:, (tt % 2) * 1024:(tt % 2) * 1024 + 1024]).then_inc(S["dY"], 16)
        s.wait_ge(S["dY"], NTO * 16)
        if debug:
            s.dma_start(out=dbg_r, in_=r_tm[:]).then_inc(S["dDbg"], 16)
            s.wait_ge(S["dDbg"], 8 * 16)

    @blk.tensor
    def _(t):
        if debug:
            t.wait_ge(S["dDbg"], 7 * 16)
        t.wait_ge(S["dW"], (NW0 + 4) * 16)
        t.wait_ge(S["sExp"], H * NCH)
        for gg in range(NG):
            t.wait_ge(S["dYi"], (gg + 1) * 16)
            for dc in range(NDC):
                k = gg * NDC + dc
                if k >= 2:
                    t.wait_ge(S["sEvOP"], k - 1)
                for qc in range(4):
                    mm = t.matmul(PB(k % 2), wo_sb[:, qc * 1024 + dc * 128:qc * 1024 + dc * 128 + 128],
                                  ybuf[:, qc * 512:qc * 512 + 512],
                                  start=(qc == 0), stop=(qc == 3))
                mm.then_inc(S["sOP"], 1)
        t.wait_ge(S["sEvD"], 64)
        t.wait_ge(S["sEvOP"], 64)
        for half in range(2):
            for gg in range(NG):
                hg = half * NG + gg
                t.wait_ge(S["dXG"], (hg + 1) * 16)
                if half == 0:
                    for s4 in range(4):
                        tt = gg * 4 + s4
                        if tt >= 1:
                            t.wait_ge(S["sEvRt"], tt)
                        for dc in range(NDC):
                            t.matmul(PB(6, 0, 8),
                                     x1g[:, (gg % 2) * 4096 + dc * 512 + s4 * 128:(gg % 2) * 4096 + dc * 512 + s4 * 128 + 128],
                                     rw_sb[:, dc * 8:dc * 8 + 8],
                                     start=(dc == 0), stop=False)
                        t.matmul(PB(6, 0, 8), on1[0:1, 0:128], rb_sb[0:1, :],
                                 start=False, stop=True).then_inc(S["sRt"], 1)
                    for s4 in range(4):
                        tt = gg * 4 + s4
                        t.wait_ge(S["sTop"], tt + 1)
                        if tt >= 1:
                            t.wait_ge(S["sEvRT"], tt)
                        t.transpose(pbf[0:8, 0:128], rcast[:, (tt % 2) * 8:(tt % 2) * 8 + 8],
                                    i4[:, 0:128]).then_inc(S["sRT"], 1)
                for el in range(4):
                    m = hg * 4 + el
                    if m >= 2:
                        t.wait_ge(S["sEvR"], m - 1)
                    for s4 in range(4):
                        j = m * 4 + s4
                        t.wait_ge(S["sDgR"], j + 1)
                        t.matmul(PB(0 + m % 2, s4 * 128, 128), onb[:],
                                 dgr[:, (j % 2) * 128:(j % 2) * 128 + 128],
                                 start=True, stop=True).then_inc(S["sBcR"], 1)
                for fl in range(16):
                    kf = hg * 16 + fl
                    t.wait_ge(S["dWF"], (half * 24 + 8) * 16)
                    if kf >= 2:
                        t.wait_ge(S["sGel"], kf - 1)
                    for dc in range(NDC):
                        mm = t.matmul(PB(2 + fl % 2),
                                      w1h[:, dc * 2048 + fl * 128:dc * 2048 + fl * 128 + 128],
                                      x1g[:, (gg % 2) * 4096 + dc * 512:(gg % 2) * 4096 + dc * 512 + 512],
                                      start=(dc == 0), stop=(dc == NDC - 1))
                    mm.then_inc(S["sH1"], 1)
                t.wait_ge(S["dWF"], (half + 1) * 24 * 16)
                t.wait_ge(S["sScl"], hg * 16 + 16)
                for dc in range(NDC):
                    k2 = hg * NDC + dc
                    if k2 >= 2:
                        t.wait_ge(S["sEvW2"], k2 - 1)
                    for fl in range(16):
                        mm = t.matmul(PB(4 + k2 % 2),
                                      w2h[:, fl * 1024 + dc * 128:fl * 1024 + dc * 128 + 128],
                                      h1g[:, fl * 512:fl * 512 + 512],
                                      start=(fl == 0), stop=(half == 0 and fl == 15))
                    if half == 1:
                        t.wait_ge(S["sEvRT"], NTO)
                        mm = t.matmul(PB(4 + k2 % 2), b2_sb[0:8, dc * 128:dc * 128 + 128],
                                      r_fm[0:8, gg * 512:gg * 512 + 512],
                                      start=False, stop=True)
                    mm.then_inc(S["sW2"], 1)

    @blk.scalar
    def _(sc):
        def _p5act(sc, tt):
            sb0 = (tt % 2) * 24
            sc.wait_ge(S["sBN1"], tt + 1)
            sc.activation(stat[:, sb0 + 14:sb0 + 15], stat[:, sb0 + 1:sb0 + 2],
                          AF.Sqrt, bias=epsb[:, 0:1], scale=1.0).then_inc(S["sSq1"], 1)
            sc.wait_ge(S["sRst1"], tt + 1)
            if tt >= 2:
                sc.wait_ge(S["dX1s"], (tt - 1) * 16)
            sc.activation(x1_tm[:, (tt % 2) * 1024:(tt % 2) * 1024 + 1024],
                          rres[:, (tt % 2) * 1024:(tt % 2) * 1024 + 1024],
                          AF.Identity, bias=stat[:, sb0 + 16:sb0 + 17],
                          scale=stat[:, sb0 + 15:sb0 + 16]).then_inc(S["sNrm1"], 1)
            sc.wait_ge(S["sX1b"], tt + 1)
            if tt >= 2:
                sc.wait_ge(S["dXFs"], (tt - 1) * 16)
            for dc in range(NDC):
                sc.dma_start(out=x1t[:, (tt % 2) * 1024 + dc * 128:(tt % 2) * 1024 + dc * 128 + 128],
                             in_=x1b[:, (tt % 2) * 1024 + dc * 128:(tt % 2) * 1024 + dc * 128 + 128],
                             transpose=True).then_inc(S["dX1T"], 16)
        for gg in range(NG):
            if gg >= 1:
                for s4 in range(4):
                    _p5act(sc, (gg - 1) * 4 + s4)
        for s4 in range(4):
            _p5act(sc, (NG - 1) * 4 + s4)
        def _sqrt2(sc, tt):
            sb0 = 48 + (tt % 2) * 24
            sc.wait_ge(S["sBN2"], tt + 1)
            sc.activation(stat[:, sb0 + 14:sb0 + 15], stat[:, sb0 + 1:sb0 + 2],
                          AF.Sqrt, bias=epsb[:, 0:1], scale=1.0).then_inc(S["sSq2"], 1)
        def _nrm2(sc, tt):
            sb0 = 48 + (tt % 2) * 24
            sc.wait_ge(S["sRst2"], tt + 1)
            if tt < 2:
                sc.wait_ge(S["sRes1"], NTO)
            else:
                sc.wait_ge(S["dY"], (tt - 1) * 16)
            sc.activation(xres[:, (tt % 2) * 1024:(tt % 2) * 1024 + 1024],
                          rres[:, (tt % 2) * 1024:(tt % 2) * 1024 + 1024],
                          AF.Identity, bias=stat[:, sb0 + 16:sb0 + 17],
                          scale=stat[:, sb0 + 15:sb0 + 16]).then_inc(S["sNrm2"], 1)
        for half in range(2):
            for gg in range(NG):
                hg = half * NG + gg
                if half == 1 and gg >= 1:
                    for s4 in range(4):
                        _sqrt2(sc, (gg - 1) * 4 + s4)
                        _nrm2(sc, (gg - 1) * 4 + s4)
                if half == 0:
                    for s4 in range(4):
                        tt = gg * 4 + s4
                        sc.wait_ge(S["sRtd"], tt + 1)
                        sc.activation(rstat[:, (tt % 4) * 32 + 3:(tt % 4) * 32 + 4],
                                      rstat[:, (tt % 4) * 32 + 2:(tt % 4) * 32 + 3],
                                      AF.Sigmoid, bias=0.0, scale=1.0).then_inc(S["sSig"], 1)
                for el in range(4):
                    m = hg * 4 + el
                    sc.wait_ge(S["sBcR"], (m + 1) * 4)
                    if hg >= 2:
                        sc.wait_ge(S["sScl"], (hg - 1) * 16)
                    sc.activation(r_bc[:, (hg % 2) * 2048 + el * 512:(hg % 2) * 2048 + el * 512 + 512],
                                  PB(0 + m % 2), AF.Copy, bias=0.0,
                                  scale=1.0).then_inc(S["sEvR"], 1)
                for fl in range(16):
                    kf = hg * 16 + fl
                    sc.wait_ge(S["sH1"], kf + 1)
                    if kf >= 2:
                        sc.wait_ge(S["sScl"], kf - 1)
                    sc.activation(h1p[:, (kf % 2) * 512:(kf % 2) * 512 + 512],
                                  PB(2 + fl % 2), AF.Gelu,
                                  bias=b1_sb[:, half * 16 + fl:half * 16 + fl + 1],
                                  scale=1.0).then_inc(S["sGel"], 1)
        for s4 in range(4):
            _sqrt2(sc, (NG - 1) * 4 + s4)
            _nrm2(sc, (NG - 1) * 4 + s4)

    @blk.vector
    def _(v):
        def _p5tile(v, tt):
            gg = tt // 4
            if True:
                sb0 = (tt % 2) * 24
                v.wait_ge(S["dXF"], (gg + 1) * 32 * 16)
                v.wait_ge(S["dXR"], (tt + 1) * 16)
                if tt >= 2:
                    v.wait_ge(S["sNrm1"], tt - 1)
                v.tensor_tensor(rres[:, (tt % 2) * 1024:(tt % 2) * 1024 + 1024],
                                xres[:, (tt % 2) * 1024:(tt % 2) * 1024 + 1024],
                                attn_tm[:, (tt % 6) * 1024:(tt % 6) * 1024 + 1024],
                                ALU.add).then_inc(S["sRes1"], 1)
                v.wait_ge(S["sRes1"], tt + 1)
                v.bn_stats(stat[:, sb0 + 2:sb0 + 8],
                           rres[:, (tt % 2) * 1024:(tt % 2) * 1024 + 512])
                ins = v.bn_stats(stat[:, sb0 + 8:sb0 + 14],
                                 rres[:, (tt % 2) * 1024 + 512:(tt % 2) * 1024 + 1024])
                VS(v, ins)
                v.bn_aggr(stat[:, sb0:sb0 + 2], stat[:, sb0 + 2:sb0 + 14]).then_inc(S["sBN1"], 1)
                v.wait_ge(S["sSq1"], tt + 1)
                ins = v.reciprocal(stat[:, sb0 + 15:sb0 + 16], stat[:, sb0 + 14:sb0 + 15])
                VS(v, ins)
                v.tensor_scalar(stat[:, sb0 + 16:sb0 + 17], stat[:, sb0:sb0 + 1],
                                stat[:, sb0 + 15:sb0 + 16], -1.0,
                                ALU.mult, ALU.mult).then_inc(S["sRst1"], 1)
                v.wait_ge(S["sNrm1"], tt + 1)
                if tt >= 2:
                    v.wait_ge(S["dX1T"], (tt - 1) * 128)
                v.tensor_copy(x1b[:, (tt % 2) * 1024:(tt % 2) * 1024 + 1024],
                              x1_tm[:, (tt % 2) * 1024:(tt % 2) * 1024 + 1024]).then_inc(S["sX1b"], 1)
        for gg in range(NG):
            if gg >= 1:
                for s4 in range(4):
                    _p5tile(v, (gg - 1) * 4 + s4)
            for dc in range(NDC):
                k = gg * NDC + dc
                v.wait_ge(S["sOP"], k + 1)
                if k >= 8:
                    v.wait_ge(S["dXF"], (k - 7) * 64)
                v.tensor_copy(attn_fm[:, (k % 8) * 512:(k % 8) * 512 + 512],
                              PB(k % 2)).then_inc(S["sEvOP"], 1)
        for s4 in range(4):
            _p5tile(v, (NG - 1) * 4 + s4)
        for s4 in range(4):
            tt = (NG - 1) * 4 + s4
            v.wait_ge(S["sNrm1"], tt + 1)
            v.wait_ge(S["dX1T"], (tt - 1) * 128)
            v.tensor_copy(x1b[:, (tt % 2) * 1024:(tt % 2) * 1024 + 1024],
                          x1_tm[:, (tt % 2) * 1024:(tt % 2) * 1024 + 1024]).then_inc(S["sX1b"], 1)
        def _p7tile(v, tt):
            sb0 = 48 + (tt % 2) * 24
            gg7 = tt // 4
            v.wait_ge(S["dFT"], (gg7 + 1) * 32 * 16)
            v.wait_ge(S["dXR"], (NTO + tt + 1) * 16)
            if tt >= 2:
                v.wait_ge(S["sNrm2"], tt - 1)
            else:
                v.wait_ge(S["sNrm1"], NTO)
            v.tensor_tensor(rres[:, (tt % 2) * 1024:(tt % 2) * 1024 + 1024],
                            x1_tm[:, (tt % 2) * 1024:(tt % 2) * 1024 + 1024],
                            ffn_tm[:, (tt % 4) * 1024:(tt % 4) * 1024 + 1024],
                            ALU.add).then_inc(S["sAdd2"], 1)
            v.wait_ge(S["sAdd2"], tt + 1)
            v.bn_stats(stat[:, sb0 + 2:sb0 + 8],
                       rres[:, (tt % 2) * 1024:(tt % 2) * 1024 + 512])
            ins = v.bn_stats(stat[:, sb0 + 8:sb0 + 14],
                             rres[:, (tt % 2) * 1024 + 512:(tt % 2) * 1024 + 1024])
            VS(v, ins)
            v.bn_aggr(stat[:, sb0:sb0 + 2], stat[:, sb0 + 2:sb0 + 14]).then_inc(S["sBN2"], 1)
            v.wait_ge(S["sSq2"], tt + 1)
            ins = v.reciprocal(stat[:, sb0 + 15:sb0 + 16], stat[:, sb0 + 14:sb0 + 15])
            VS(v, ins)
            v.tensor_scalar(stat[:, sb0 + 16:sb0 + 17], stat[:, sb0:sb0 + 1],
                            stat[:, sb0 + 15:sb0 + 16], -1.0,
                            ALU.mult, ALU.mult).then_inc(S["sRst2"], 1)
        for half in range(2):
            for gg in range(NG):
                hg = half * NG + gg
                if half == 1 and gg >= 1:
                    for s4 in range(4):
                        _p7tile(v, (gg - 1) * 4 + s4)
                if half == 0:
                    for s4 in range(4):
                        tt = gg * 4 + s4
                        rs = (tt % 4) * 32
                        v.wait_ge(S["sRt"], tt + 1)
                        if tt >= 4:
                            v.wait_ge(S["sTop"], tt - 3)
                        v.tensor_copy(lbuf[:, (tt % 2) * 8:(tt % 2) * 8 + 8],
                                      PB(6, 0, 8)).then_inc(S["sEvRt"], 1)
                        v.wait_ge(S["sEvRt"], tt + 1)
                        L = lbuf[:, (tt % 2) * 8:(tt % 2) * 8 + 8]
                        ins = v.reduce_max(rstat[:, rs:rs + 1], L, axis=AX)
                        VS(v, ins)
                        ins = v.tensor_scalar(rstat[:, rs + 8:rs + 16], L,
                                              rstat[:, rs:rs + 1], None, ALU.is_ge)
                        VS(v, ins)
                        ins = v.scalar_tensor_tensor(rstat[:, rs + 16:rs + 24],
                                                     rstat[:, rs + 8:rs + 16], -1e30, L,
                                                     ALU.mult, ALU.add)
                        VS(v, ins)
                        ins = v.reduce_max(rstat[:, rs + 1:rs + 2],
                                           rstat[:, rs + 16:rs + 24], axis=AX)
                        VS(v, ins)
                        v.tensor_sub(rstat[:, rs + 2:rs + 3], rstat[:, rs:rs + 1],
                                     rstat[:, rs + 1:rs + 2]).then_inc(S["sRtd"], 1)
                    for s4 in range(4):
                        tt = gg * 4 + s4
                        rs = (tt % 4) * 32
                        v.wait_ge(S["sSig"], tt + 1)
                        v.tensor_scalar(rstat[:, rs + 4:rs + 5], rstat[:, rs + 3:rs + 4],
                                        -1.0, 1.0, ALU.mult, ALU.add)
                        ins = v.tensor_scalar(rstat[:, rs + 24:rs + 32],
                                              rstat[:, rs + 16:rs + 24],
                                              rstat[:, rs + 1:rs + 2], None, ALU.is_ge)
                        VS(v, ins)
                        ins = v.tensor_scalar(rstat[:, rs + 16:rs + 24],
                                              rstat[:, rs + 8:rs + 16],
                                              rstat[:, rs + 3:rs + 4], None, ALU.mult)
                        VS(v, ins)
                        ins = v.scalar_tensor_tensor(r_tm[:, tt * 8:tt * 8 + 8],
                                                     rstat[:, rs + 24:rs + 32],
                                                     rstat[:, rs + 4:rs + 5],
                                                     rstat[:, rs + 16:rs + 24],
                                                     ALU.mult, ALU.add)
                        VS(v, ins)
                        v.tensor_copy(rcast[:, (tt % 2) * 8:(tt % 2) * 8 + 8],
                                      r_tm[:, tt * 8:tt * 8 + 8]).then_inc(S["sTop"], 1)
                        v.wait_ge(S["sRT"], tt + 1)
                        v.tensor_copy(r_fm[0:8, tt * 128:tt * 128 + 128],
                                      pbf[0:8, 0:128]).then_inc(S["sEvRT"], 1)
                for el in range(4):
                    m = hg * 4 + el
                    for s4 in range(4):
                        j = m * 4 + s4
                        tt = gg * 4 + s4
                        if j >= 2:
                            v.wait_ge(S["sBcR"], j - 1)
                        v.tensor_scalar(dgr[:, (j % 2) * 128:(j % 2) * 128 + 128],
                                        i4[:, 0:128],
                                        r_tm[:, tt * 8 + half * 4 + el:tt * 8 + half * 4 + el + 1],
                                        None, ALU.mult).then_inc(S["sDgR"], 1)
                for fl in range(16):
                    kf = hg * 16 + fl
                    el = fl // 4
                    v.wait_ge(S["sGel"], kf + 1)
                    v.wait_ge(S["sEvR"], hg * 4 + el + 1)
                    v.tensor_tensor(h1g[:, fl * 512:fl * 512 + 512],
                                    h1p[:, (kf % 2) * 512:(kf % 2) * 512 + 512],
                                    r_bc[:, (hg % 2) * 2048 + el * 512:(hg % 2) * 2048 + el * 512 + 512],
                                    ALU.mult).then_inc(S["sScl"], 1)
                for dc in range(NDC):
                    k2 = hg * NDC + dc
                    v.wait_ge(S["sW2"], k2 + 1)
                    if half == 0:
                        if k2 >= 2:
                            v.wait_ge(S["dO1o"], (k2 - 1) * 16)
                        v.tensor_copy(o1ev[:, (k2 % 2) * 512:(k2 % 2) * 512 + 512],
                                      PB(4 + k2 % 2)).then_inc(S["sEvW2"], 1)
                    else:
                        ko = gg * NDC + dc
                        v.wait_ge(S["dO1i"], (ko + 1) * 16)
                        if ko >= 2:
                            v.wait_ge(S["dFT"], (ko - 1) * 64)
                        v.tensor_tensor(ffn_fm[:, (ko % 2) * 512:(ko % 2) * 512 + 512],
                                        PB(4 + k2 % 2),
                                        o1in[:, (ko % 2) * 512:(ko % 2) * 512 + 512],
                                        ALU.add).then_inc(S["sEvW2"], 1)
        for tt in range((NG - 1) * 4, NTO):
            _p7tile(v, tt)

    esF.close()
    esC.close()
    return nc, es


_CACHE = {}


def _host_arrays(inputs):
    X = np.asarray(inputs["L0_attn_X"])
    w_G = np.asarray(inputs["L0_attn_w_G"])
    b_G = np.asarray(inputs["L0_attn_b_G"])
    w_K = np.asarray(inputs["L0_attn_w_K"])
    w_V = np.asarray(inputs["L0_attn_w_V"])
    w_Q = np.asarray(inputs["L0_attn_w_Q"])
    w_O = np.asarray(inputs["L0_attn_w_O"])
    iK = np.asarray(inputs["L0_attn_init_K_rec"])
    iV = np.asarray(inputs["L0_attn_init_V_rec"])
    rw = np.asarray(inputs["L0_ffn_router_w"])
    rb = np.asarray(inputs["L0_ffn_router_b"])
    W1 = np.asarray(inputs["L0_ffn_W1"])
    b1 = np.asarray(inputs["L0_ffn_b1"])
    W2 = np.asarray(inputs["L0_ffn_W2"])
    b2 = np.asarray(inputs["L0_ffn_b2"])

    f = np.float32
    wkvg = np.empty((1024, 1056), f)
    for h in range(H):
        wkvg[:, h * 128:h * 128 + 64] = w_K[h].T
        wkvg[:, h * 128 + 64:h * 128 + 128] = w_V[h].T
    for e in range(E):
        for h in range(H):
            wkvg[:, 1024 + e * 8 + h] = w_G[h, e]
    wq = w_Q.transpose(2, 0, 1).reshape(1024, 512).astype(f)
    w1c = np.ascontiguousarray(W1.transpose(1, 0, 2).reshape(1024, 4096), f)
    w2c = np.ascontiguousarray(W2.reshape(4096, 1024), f)
    b1h = np.ascontiguousarray(b1.reshape(4096).reshape(32, 128).T, f)
    bgrow = np.empty(32, f)
    for e in range(E):
        for h in range(H):
            bgrow[e * 8 + h] = b_G[h, e]
    bgd = np.ascontiguousarray(np.broadcast_to(bgrow, (128, 32)), f)
    mskd = np.zeros((128, 160), f)
    for p in range(128):
        mskd[p, p:p + 32] = 1.0
    i4d = np.ascontiguousarray(np.tile(np.eye(128, dtype=f), (1, 4)))
    idfd = np.eye(128, dtype=f)
    on1d = np.ones((1, 128), f)
    ond2 = np.ones((128, 128), f)
    extd = np.empty((128, 128), f)
    for e in range(E):
        extd[0:64, e * 32:e * 32 + 32] = iK[e].T
        extd[64:128, e * 32:e * 32 + 32] = iV[e].T

    common = dict(wkvg=np.ascontiguousarray(wkvg), wqd=np.ascontiguousarray(wq),
                  wod=np.ascontiguousarray(w_O.astype(f)), w1d=w1c, w2d=w2c,
                  b1d=b1h, b2d=np.ascontiguousarray(b2.astype(f)),
                  rwd=np.ascontiguousarray(rw.astype(f)),
                  rbd=np.ascontiguousarray(rb.reshape(1, 8).astype(f)),
                  extd=extd, bgd=bgd, mskd=mskd, i4d=i4d, idfd=idfd,
                  on1d=on1d, ond2=ond2, epsd=np.full((128, 1), 1e-5, f))

    in_maps = []
    for c in range(8):
        n, half = divmod(c, 2)
        t0 = half * TOWN
        halo = X[n, t0 - 256:t0] if half == 1 else X[n, 0:256]
        xs_c = np.ascontiguousarray(np.concatenate([halo, X[n, t0:t0 + TOWN]], 0), f)
        lanes = ([15] + list(range(16, 32))) if half == 1 else ([0] + list(range(0, 16)))
        injd = np.empty((128, 68), f)
        for e in range(E):
            for li, lg in enumerate(lanes):
                injd[0:64, e * 17 + li] = iK[e, lg]
                injd[64:128, e * 17 + li] = iV[e, lg]
        fhd = np.full((128, 1), 1.0 if half == 0 else 0.0, f)
        m = dict(common)
        m.update(xs=xs_c, injd=injd, fhd=fhd)
        in_maps.append(m)
    return in_maps


def kernel(**inputs):
    if "nc" not in _CACHE:
        _CACHE["nc"] = build_nc()
    nc = _CACHE["nc"][0]
    in_maps = _host_arrays(inputs)
    res = run_bass_kernel_spmd(nc, in_maps, core_ids=list(range(8)))
    out = np.empty((4, 8192, 1024), np.float32)
    for c in range(8):
        n, half = divmod(c, 2)
        out[n, half * TOWN:(half + 1) * TOWN] = res.results[c]["y"]
    return out


# version 10 (final)
# speedup vs baseline: 1.1615x; 1.0261x over previous
import contextlib
import numpy as np
import ml_dtypes
import concourse.bass as bass
import concourse.mybir as mybir
from concourse.bass_utils import run_bass_kernel_spmd

F32 = mybir.dt.float32
BF16 = mybir.dt.bfloat16
AF = mybir.ActivationFunctionType
ALU = mybir.AluOpType
AX = mybir.AxisListType.X
BF = ml_dtypes.bfloat16

TH, NT = 4352, 34
TOWN, NTO, NG = 4096, 32, 8
D, NDC = 1024, 8
H, E = 8, 4
LANE, NLANE = 256, 17
EXTC, NCH = 4224, 33
DFF = 4096


def build_nc(debug=False):
    nc = bass.Bass("TRN2", target_bir_lowering=False)

    xs = nc.dram_tensor("xs", [TH, D], F32, kind="ExternalInput").ap()
    wkvg = nc.dram_tensor("wkvg", [D, 1056], F32, kind="ExternalInput").ap()
    wqd = nc.dram_tensor("wqd", [D, 512], F32, kind="ExternalInput").ap()
    wod = nc.dram_tensor("wod", [512, D], F32, kind="ExternalInput").ap()
    w1d = nc.dram_tensor("w1d", [D, DFF], F32, kind="ExternalInput").ap()
    w2d = nc.dram_tensor("w2d", [DFF, D], F32, kind="ExternalInput").ap()
    b1d = nc.dram_tensor("b1d", [128, 32], F32, kind="ExternalInput").ap()
    b2d = nc.dram_tensor("b2d", [8, D], F32, kind="ExternalInput").ap()
    rwd = nc.dram_tensor("rwd", [D, 8], F32, kind="ExternalInput").ap()
    rbd = nc.dram_tensor("rbd", [1, 8], F32, kind="ExternalInput").ap()
    injd = nc.dram_tensor("injd", [128, 68], F32, kind="ExternalInput").ap()
    extd = nc.dram_tensor("extd", [128, 128], F32, kind="ExternalInput").ap()
    fhd = nc.dram_tensor("fhd", [128, 1], F32, kind="ExternalInput").ap()
    bgd = nc.dram_tensor("bgd", [128, 32], F32, kind="ExternalInput").ap()
    mskd = nc.dram_tensor("mskd", [128, 160], F32, kind="ExternalInput").ap()
    i4d = nc.dram_tensor("i4d", [128, 512], F32, kind="ExternalInput").ap()
    idfd = nc.dram_tensor("idfd", [128, 128], F32, kind="ExternalInput").ap()
    on1d = nc.dram_tensor("on1d", [1, 128], F32, kind="ExternalInput").ap()
    epsd = nc.dram_tensor("epsd", [128, 1], F32, kind="ExternalInput").ap()
    ond2 = nc.dram_tensor("ond2", [128, 128], F32, kind="ExternalInput").ap()

    KI = "ExternalOutput" if debug else "Internal"
    scr_q = nc.dram_tensor("scr_q", [4, 128, TOWN], BF16, kind=KI).ap()
    scr_y = nc.dram_tensor("scr_y", [4, 128, TOWN], BF16, kind=KI).ap()
    scr_x1 = nc.dram_tensor("scr_x1", [TOWN, D], F32, kind=KI).ap()
    scr_xf = nc.dram_tensor("scr_xf", [NDC, 128, TOWN], BF16, kind=KI).ap()
    scr_o1 = nc.dram_tensor("scr_o1", [NDC, 128, TOWN], BF16, kind=KI).ap()
    if debug:
        dbg_g = nc.dram_tensor("dbg_g", [128, NT * 32], F32, kind="ExternalOutput").ap()
        dbg_a = nc.dram_tensor("dbg_a", [128, NT * 4], F32, kind="ExternalOutput").ap()
        dbg_ext = nc.dram_tensor("dbg_ext", [128, E * EXTC], F32, kind="ExternalOutput").ap()
        dbg_pt = nc.dram_tensor("dbg_pt", [128, E * NCH * 160], F32, kind="ExternalOutput").ap()
        dbg_r = nc.dram_tensor("dbg_r", [128, NTO * 8], F32, kind="ExternalOutput").ap()
        dbg_gkv = nc.dram_tensor("dbg_gkv", [128, E * TH], F32, kind="ExternalOutput").ap()
        dbg_abc = nc.dram_tensor("dbg_abc", [128, TH], F32, kind="ExternalOutput").ap()
        dbg_nkv = nc.dram_tensor("dbg_nkv", [128, TH], F32, kind="ExternalOutput").ap()
    y = nc.dram_tensor("y", [TOWN, D], F32, kind="ExternalOutput").ap()

    es = contextlib.ExitStack()
    semnames = [
        "dW", "dX", "dXT", "dQo", "dQi", "dVT", "dYo", "dYi", "dXR", "dX1s",
        "dX1T", "dXFs", "dXG", "dO1o", "dO1i", "dY", "dWF", "dXF", "dFT",
        "sKVG", "sPre", "sG", "sDiag", "sGKV", "sEvGKV", "sEvKV",
        "sA", "sQ", "sEvQ", "sDgA", "sBcA", "sEvA", "sScan", "sExt",
        "sSc", "sExp", "sMkV", "sMkG", "sPV", "sRcp", "sBcD", "sEvD", "sY",
        "sOP", "sEvOP", "sRes1", "sBN1", "sSq1", "sRst1", "sNrm1", "sX1b",
        "sRt", "sEvRt", "sRtd", "sSig", "sTop", "sRT", "sEvRT",
        "sDgR", "sBcR", "sEvR", "sH1", "sGel", "sScl", "sW2", "sEvW2",
        "sAdd2", "sBN2", "sSq2", "sRst2", "sNrm2", "dDbg", "sVs", "sAr", "sAh",
    ]
    S = {n: es.enter_context(nc.semaphore(n)) for n in semnames}
    _vs = {"n": 0}
    def VS(v, ins):
        _vs["n"] += 1
        ins.then_inc(S["sVs"], 1)
        v.wait_ge(S["sVs"], _vs["n"])

    i4 = es.enter_context(nc.sbuf_tensor("i4", [128, 512], BF16))
    idf = es.enter_context(nc.sbuf_tensor("idf", [128, 128], F32))
    on1 = es.enter_context(nc.sbuf_tensor("on1", [1, 128], BF16))
    onf = es.enter_context(nc.sbuf_tensor("onf", [128, 128], F32))
    onb = es.enter_context(nc.sbuf_tensor("onb", [128, 128], BF16))
    msk = es.enter_context(nc.sbuf_tensor("msk", [128, 160], BF16))
    fh = es.enter_context(nc.sbuf_tensor("fh", [128, 1], F32))
    epsb = es.enter_context(nc.sbuf_tensor("epsb", [128, 1], F32))
    bg = es.enter_context(nc.sbuf_tensor("bg", [128, 32], F32))
    inj = es.enter_context(nc.sbuf_tensor("inj", [128, 68], F32))
    extov = es.enter_context(nc.sbuf_tensor("extov", [128, 128], F32))
    rw_sb = es.enter_context(nc.sbuf_tensor("rw_sb", [128, 64], BF16))
    rb_sb = es.enter_context(nc.sbuf_tensor("rb_sb", [1, 8], BF16))
    b1_sb = es.enter_context(nc.sbuf_tensor("b1_sb", [128, 32], F32))
    b2_sb = es.enter_context(nc.sbuf_tensor("b2_sb", [8, 1024], BF16))
    g_tm = es.enter_context(nc.sbuf_tensor("g_tm", [128, NT * 32], F32))
    a_tm = es.enter_context(nc.sbuf_tensor("a_tm", [128, NT * 4], F32))
    a_hi = es.enter_context(nc.sbuf_tensor("a_hi", [128, NT * 4], BF16))
    a_lo = es.enter_context(nc.sbuf_tensor("a_lo", [128, NT * 4], F32))
    a_hif = es.enter_context(nc.sbuf_tensor("a_hif", [128, NT * 4], F32))
    stat = es.enter_context(nc.sbuf_tensor("stat", [128, 96], F32))
    r_tm = es.enter_context(nc.sbuf_tensor("r_tm", [128, NTO * 8], F32))
    rstat = es.enter_context(nc.sbuf_tensor("rstat", [128, 128], F32))
    lbuf = es.enter_context(nc.sbuf_tensor("lbuf", [128, 16], F32))
    rcast = es.enter_context(nc.sbuf_tensor("rcast", [128, 16], BF16))

    pall = es.enter_context(nc.psum_tensor("pall", [128, 3584], F32))
    pbf = es.enter_context(nc.psum_tensor("pbf", [128, 192], BF16))

    def PB(k, off=0, w=512, p0=0, pn=128):
        return pall[p0:p0 + pn, k * 512 + off:k * 512 + off + w]

    blk = es.enter_context(nc.Block())

    esA2 = contextlib.ExitStack()
    gkv_fm = esA2.enter_context(nc.sbuf_tensor("gkv_fm", [128, E * TH], BF16, side="right"))
    esA1 = contextlib.ExitStack()
    x_tm = esA1.enter_context(nc.sbuf_tensor("x_tm", [128, 2 * 1024], BF16))
    x_fm = esA1.enter_context(nc.sbuf_tensor("x_fm", [128, NDC * TH], BF16,
                                             is_transpose_dest=True))
    wkvg_sb = esA1.enter_context(nc.sbuf_tensor("wkvg_sb", [128, NDC * 1056], BF16))
    wq_sb = esA1.enter_context(nc.sbuf_tensor("wq_sb", [128, NDC * 512], BF16))
    kv_tm = esA1.enter_context(nc.sbuf_tensor("kv_tm", [128, 2 * 1024], BF16))
    dg = esA1.enter_context(nc.sbuf_tensor("dg", [128, 4096], BF16))
    gpre = esA1.enter_context(nc.sbuf_tensor("gpre", [128, 64], F32))
    qev = esA1.enter_context(nc.sbuf_tensor("qev", [128, 2 * 512], BF16))

    @blk.gpsimd
    def _(gp):
        for i in range(2):
            gp.dma_start(out=x_tm[:, (i % 2) * 1024:(i % 2) * 1024 + 1024],
                         in_=xs[i * 128:i * 128 + 128, :]).then_inc(S["dX"], 16)
        for dst, src in [(i4[:], i4d), (idf[:], idfd), (on1[0:1, :], on1d),
                         (onf[:], ond2), (onb[:], ond2), (msk[:], mskd),
                         (fh[:], fhd), (epsb[:], epsd), (bg[:], bgd), (inj[:], injd),
                         (extov[:], extd), (rb_sb[0:1, :], rbd),
                         (b1_sb[:], b1d), (b2_sb[0:8, :], b2d)]:
            gp.dma_start(out=dst, in_=src).then_inc(S["dW"], 16)
        for dc in range(NDC):
            gp.dma_start(out=rw_sb[:, dc * 8:dc * 8 + 8],
                         in_=rwd[dc * 128:dc * 128 + 128, :]).then_inc(S["dW"], 16)
            gp.dma_start(out=wkvg_sb[:, dc * 1056:dc * 1056 + 1056],
                         in_=wkvg[dc * 128:dc * 128 + 128, :]).then_inc(S["dW"], 16)
            gp.dma_start(out=wq_sb[:, dc * 512:dc * 512 + 512],
                         in_=wqd[dc * 128:dc * 128 + 128, :]).then_inc(S["dW"], 16)
        for i in range(2, NT):
            if i >= 2:
                gp.wait_ge(S["dXT"], (i - 1) * 128)
            gp.dma_start(out=x_tm[:, (i % 2) * 1024:(i % 2) * 1024 + 1024],
                         in_=xs[i * 128:i * 128 + 128, :]).then_inc(S["dX"], 16)
    NW0 = 38

    @blk.sync
    def _(s):
        for i in range(NT):
            s.wait_ge(S["dX"], (i + 1) * 16)
            for dc in range(NDC):
                s.dma_start(out=x_fm[:, dc * TH + i * 128:dc * TH + i * 128 + 128],
                            in_=x_tm[:, (i % 2) * 1024 + dc * 128:(i % 2) * 1024 + dc * 128 + 128],
                            transpose=True).then_inc(S["dXT"], 16)

    @blk.tensor
    def _(t):
        t.wait_ge(S["dW"], NW0 * 16)
        nq = 0
        for i in range(NT):
            t.wait_ge(S["dXT"], (i + 1) * 128)
            t.wait_ge(S["sPre"], i)
            if i >= 2:
                t.wait_ge(S["sEvKV"], (i - 1) * 2)
            for dc in range(NDC):
                lhs = x_fm[:, dc * TH + i * 128:dc * TH + i * 128 + 128]
                t.matmul(PB(0 + i % 2), lhs, wkvg_sb[:, dc * 1056:dc * 1056 + 512],
                         start=(dc == 0), stop=(dc == NDC - 1))
                t.matmul(PB(2 + i % 2), lhs, wkvg_sb[:, dc * 1056 + 512:dc * 1056 + 1024],
                         start=(dc == 0), stop=(dc == NDC - 1))
                mm = t.matmul(PB(5, 0, 32), lhs,
                              wkvg_sb[:, dc * 1056 + 1024:dc * 1056 + 1056],
                              start=(dc == 0), stop=(dc == NDC - 1))
            mm.then_inc(S["sKVG"], 1)
            for ii in ([i - 1] if i >= 1 else []) + ([NT - 1] if i == NT - 1 else []):
                t.wait_ge(S["sDiag"], 2 * ii + 2)
                t.wait_ge(S["sEvKV"], 2 * ii + 2)
                if ii >= 1:
                    t.wait_ge(S["sEvGKV"], ii)
                for hh in range(H):
                    mm = t.matmul(PB(4),
                                  kv_tm[:, (ii % 2) * 1024 + hh * 128:(ii % 2) * 1024 + hh * 128 + 128],
                                  dg[:, hh * 512:hh * 512 + 512],
                                  start=(hh == 0), stop=(hh == H - 1))
                mm.then_inc(S["sGKV"], 1)
            if i >= 5 and (i - 5) % 4 == 0:
                gg = (i - 5) // 4
                for qc in range(4):
                    if nq >= 1:
                        t.wait_ge(S["sEvQ"], nq)
                    for dc in range(NDC):
                        mm = t.matmul(PB(6), wq_sb[:, dc * 512 + qc * 128:dc * 512 + qc * 128 + 128],
                                      x_fm[:, dc * TH + 256 + gg * 512:dc * TH + 256 + gg * 512 + 512],
                                      start=(dc == 0), stop=(dc == NDC - 1))
                    mm.then_inc(S["sQ"], 1)
                    nq += 1

    @blk.vector
    def _(v):
        nq = 0
        for i in range(NT):
            v.wait_ge(S["sKVG"], i + 1)
            v.tensor_tensor(gpre[:, (i % 2) * 32:(i % 2) * 32 + 32],
                            PB(5, 0, 32), bg[:], ALU.add).then_inc(S["sPre"], 1)
            if i >= 2:
                v.wait_ge(S["sGKV"], i - 1)
            v.tensor_copy(kv_tm[:, (i % 2) * 1024:(i % 2) * 1024 + 512],
                          PB(0 + i % 2)).then_inc(S["sEvKV"], 1)
            v.tensor_copy(kv_tm[:, (i % 2) * 1024 + 512:(i % 2) * 1024 + 1024],
                          PB(2 + i % 2)).then_inc(S["sEvKV"], 1)
            v.wait_ge(S["sG"], i + 1)
            if i >= 2:
                v.wait_ge(S["sA"], i - 1)
            ab = 64 + (i % 2) * 8
            for e in range(E):
                ins = v.reduce_sum(stat[:, ab + e:ab + e + 1],
                                   g_tm[:, i * 32 + e * 8:i * 32 + e * 8 + 8],
                                   axis=AX, negate=True)
            ins.then_inc(S["sAr"], 1)
            for ii0 in ([i - 1] if i >= 1 else []) + ([NT - 1] if i == NT - 1 else []):
                v.wait_ge(S["sA"], ii0 + 1)
                ins = v.tensor_copy(a_hi[:, ii0 * 4:ii0 * 4 + 4], a_tm[:, ii0 * 4:ii0 * 4 + 4])
                VS(v, ins)
                ins = v.tensor_copy(a_hif[:, ii0 * 4:ii0 * 4 + 4], a_hi[:, ii0 * 4:ii0 * 4 + 4])
                VS(v, ins)
                v.tensor_sub(a_lo[:, ii0 * 4:ii0 * 4 + 4], a_tm[:, ii0 * 4:ii0 * 4 + 4],
                             a_hif[:, ii0 * 4:ii0 * 4 + 4]).then_inc(S["sAh"], 1)
            if i >= 1:
                v.wait_ge(S["sGKV"], i)
            for hh in range(4):
                for e in range(E):
                    ins = v.tensor_scalar(
                        dg[:, hh * 512 + e * 128:hh * 512 + e * 128 + 128],
                        i4[:, 0:128],
                        g_tm[:, i * 32 + e * 8 + hh:i * 32 + e * 8 + hh + 1],
                        None, ALU.mult)
            ins.then_inc(S["sDiag"], 1)
            for ii in ([i - 1] if i >= 1 else []) + ([NT - 1] if i == NT - 1 else []):
                v.wait_ge(S["sGKV"], ii + 1)
                dst = gkv_fm[:].rearrange("p (e t) -> p e t", e=E)[:, :, ii * 128:ii * 128 + 128]
                v.tensor_copy(dst, PB(4)).then_inc(S["sEvGKV"], 1)
            if i >= 5 and (i - 5) % 4 == 0:
                for qc in range(4):
                    v.wait_ge(S["sQ"], nq + 1)
                    if nq >= 2:
                        v.wait_ge(S["dQo"], (nq - 1) * 16)
                    v.tensor_copy(qev[:, (nq % 2) * 512:(nq % 2) * 512 + 512],
                                  PB(6)).then_inc(S["sEvQ"], 1)
                    nq += 1

    @blk.scalar
    def _(sc):
        for i in range(NT):
            sc.wait_ge(S["sPre"], i + 1)
            sc.activation(g_tm[:, i * 32:i * 32 + 32],
                          gpre[:, (i % 2) * 32:(i % 2) * 32 + 32],
                          AF.Sigmoid, bias=0.0, scale=1.0).then_inc(S["sG"], 1)
            sc.wait_ge(S["sAr"], i + 1)
            sc.activation(a_tm[:, i * 4:i * 4 + 4],
                          stat[:, 64 + (i % 2) * 8:64 + (i % 2) * 8 + 4],
                          AF.Relu, bias=1.0, scale=1.0).then_inc(S["sA"], 1)
            for ii2 in ([i - 1] if i >= 1 else []) + ([NT - 1] if i == NT - 1 else []):
                if ii2 >= 1:
                    sc.wait_ge(S["sGKV"], ii2)
                for hh in range(4, 8):
                    for e in range(E):
                        ins2 = sc.activation(
                            dg[:, hh * 512 + e * 128:hh * 512 + e * 128 + 128],
                            i4[:, 0:128], AF.Copy, bias=0.0,
                            scale=g_tm[:, ii2 * 32 + e * 8 + hh:ii2 * 32 + e * 8 + hh + 1])
                ins2.then_inc(S["sDiag"], 1)

    @blk.sync
    def _(s):
        for k in range(32):
            gg, qc = divmod(k, 4)
            s.wait_ge(S["sEvQ"], k + 1)
            s.dma_start(out=scr_q[qc][0:128, gg * 512:gg * 512 + 512],
                        in_=qev[:, (k % 2) * 512:(k % 2) * 512 + 512]).then_inc(S["dQo"], 16)

    esA1.close()
    esB2 = contextlib.ExitStack()
    kvext = esB2.enter_context(nc.sbuf_tensor("kvext", [128, E * EXTC], BF16))
    vtm = esB2.enter_context(nc.sbuf_tensor("vtm", [128, E * NCH * 80], BF16,
                                            is_transpose_dest=True))
    esB1 = contextlib.ExitStack()
    a_bc = esB1.enter_context(nc.sbuf_tensor("a_bc", [128, TH], F32))
    dga = esB1.enter_context(nc.sbuf_tensor("dga", [128, 512], BF16))
    nkv = esB1.enter_context(nc.sbuf_tensor("nkv", [128, TH], F32))

    @blk.tensor
    def _(t):
        t.wait_ge(S["sEvGKV"], NT)
        for e in range(E):
            for i in range(NT):
                k = e * NT + i
                t.wait_ge(S["sDgA"], k + 1)
                if k >= 2:
                    t.wait_ge(S["sEvA"], k - 1)
                t.matmul(PB(4 + k % 2, 0, 128), onb[:],
                         dga[:, (k % 2) * 256:(k % 2) * 256 + 128],
                         start=True, stop=False)
                t.matmul(PB(4 + k % 2, 0, 128), onb[:],
                         dga[:, (k % 2) * 256 + 128:(k % 2) * 256 + 256],
                         start=False, stop=True).then_inc(S["sBcA"], 1)

    @blk.vector
    def _(v):
        for e in range(E):
            if debug and e == 1:
                v.wait_ge(S["dDbg"], 4 * 16)
            if e == 0:
                v.wait_ge(S["sQ"], 32)
            for i in range(NT):
                k = e * NT + i
                v.wait_ge(S["sAh"], i + 1)
                if k >= 2:
                    v.wait_ge(S["sBcA"], k - 1)
                v.tensor_scalar(dga[:, (k % 2) * 256:(k % 2) * 256 + 128],
                                i4[:, 0:128], a_hif[:, i * 4 + e:i * 4 + e + 1],
                                None, ALU.mult)
                v.tensor_scalar(dga[:, (k % 2) * 256 + 128:(k % 2) * 256 + 256],
                                i4[:, 0:128], a_lo[:, i * 4 + e:i * 4 + e + 1],
                                None, ALU.mult).then_inc(S["sDgA"], 1)
            v.wait_ge(S["sEvA"], (e + 1) * NT)
            acols = a_bc[:, 0:TH:LANE]
            ins = v.tensor_tensor(stat[:, 0:17], inj[:, e * 17:e * 17 + 17], acols, ALU.mult)
            VS(v, ins)
            gcols = gkv_fm[:, e * TH:e * TH + TH:LANE]
            ins = v.tensor_tensor(gcols, gcols, stat[:, 0:17], ALU.add)
            VS(v, ins)
            ins = v.memset(acols, 0.0)
            VS(v, ins)
            v.tensor_tensor_scan(nkv[:], a_bc[:], gkv_fm[:, e * TH:e * TH + TH],
                                 0.0, ALU.mult, ALU.add).then_inc(S["sScan"], 1)
            v.wait_ge(S["sScan"], e + 1)
            v.memset(kvext[:, e * EXTC + 4128:e * EXTC + 4224], 0.0)
            ins = v.tensor_sub(stat[:, 0:32], extov[:, e * 32:e * 32 + 32], nkv[:, 224:256])
            VS(v, ins)
            v.scalar_tensor_tensor(kvext[:, e * EXTC:e * EXTC + 32],
                                   stat[:, 0:32], fh[:, 0:1], nkv[:, 224:256],
                                   ALU.mult, ALU.add)
            v.tensor_copy(kvext[:, e * EXTC + 32:e * EXTC + 4128], nkv[:, 256:TH])
            ins = v.memset(vtm[:, e * NCH * 80 + 64:(e + 1) * NCH * 80:80], 1.0)
            ins.then_inc(S["sExt"], 1)

    @blk.scalar
    def _(sc):
        for e in range(E):
            if e >= 1:
                sc.wait_ge(S["sScan"], e)
            for i in range(NT):
                k = e * NT + i
                sc.wait_ge(S["sBcA"], k + 1)
                sc.activation(a_bc[:, i * 128:i * 128 + 128],
                              PB(4 + k % 2, 0, 128), AF.Copy,
                              bias=0.0, scale=1.0).then_inc(S["sEvA"], 1)


    if debug:
        @blk.gpsimd
        def _(gp):
            gp.wait_ge(S["sG"], NT)
            gp.dma_start(out=dbg_g, in_=g_tm[:]).then_inc(S["dDbg"], 16)
            gp.wait_ge(S["sA"], NT)
            gp.dma_start(out=dbg_a, in_=a_tm[:]).then_inc(S["dDbg"], 16)
            gp.wait_ge(S["sScan"], 1)
            gp.dma_start(out=dbg_abc, in_=a_bc[:]).then_inc(S["dDbg"], 16)
            gp.dma_start(out=dbg_nkv, in_=nkv[:]).then_inc(S["dDbg"], 16)
            gp.wait_ge(S["sScan"], E)
            gp.dma_start(out=dbg_gkv, in_=gkv_fm[:]).then_inc(S["dDbg"], 16)
            gp.wait_ge(S["sExt"], E)
            gp.dma_start(out=dbg_ext, in_=kvext[:]).then_inc(S["dDbg"], 16)

    esB1.close()
    esA2.close()
    esB3 = contextlib.ExitStack()
    qh = esB3.enter_context(nc.sbuf_tensor("qh", [64, 2 * TOWN], BF16))
    pt = esB3.enter_context(nc.sbuf_tensor("pt", [128, E * NCH * 160], BF16))
    dinv = esB3.enter_context(nc.sbuf_tensor("dinv", [1, 1024], F32))
    dinvb = esB3.enter_context(nc.sbuf_tensor("dinvb", [1, 1024], BF16))
    dbc = esB3.enter_context(nc.sbuf_tensor("dbc", [64, 1024], F32))
    ysc = esB3.enter_context(nc.sbuf_tensor("ysc", [64, 1024], BF16))

    def chunk_cols(c):
        return max(c * 128 - 32, 0), min(c * 128 + 126, TOWN - 1)

    @blk.sync
    def _(s):
        for e in range(E):
            s.wait_ge(S["sExt"], e + 1)
            for c in range(NCH):
                s.dma_start(out=vtm[:, (e * NCH + c) * 80:(e * NCH + c) * 80 + 64],
                             in_=kvext[64:128, e * EXTC + c * 128:e * EXTC + c * 128 + 128],
                             transpose=True).then_inc(S["dVT"], 16)
        s.wait_ge(S["dQo"], 32 * 16)
        for hh in range(2):
            s.dma_start(out=qh[0:64, (hh % 2) * TOWN:(hh % 2) * TOWN + TOWN],
                        in_=scr_q[hh // 2][(hh % 2) * 64:(hh % 2) * 64 + 64, :]).then_inc(S["dQi"], 16)

    @blk.tensor
    def _(t):
        t.wait_ge(S["dVT"], E * NCH * 16)
        for hh in range(H):
            t.wait_ge(S["dQi"], (hh + 1) * 16)
            gnext = 0
            for c in range(NCH):
                b0 = (c % 2) * 2
                if c >= 2:
                    t.wait_ge(S["sExp"], hh * NCH + c - 1)
                elif hh >= 1:
                    t.wait_ge(S["sExp"], hh * NCH - (1 if c == 1 else 0))
                lo, hi = chunk_cols(c)
                n = hi - lo + 1
                joff = lo - (c * 128 - 32)
                for e in range(E):
                    off = [0, 160, 320, 0][e]
                    bank = b0 if e < 3 else b0 + 1
                    mm = t.matmul(PB(bank, off + joff, n),
                                  kvext[0:64, e * EXTC + c * 128:e * EXTC + c * 128 + 128],
                                  qh[0:64, (hh % 2) * TOWN + lo:(hh % 2) * TOWN + lo + n],
                                  start=True, stop=True)
                mm.then_inc(S["sSc"], 1)
                while gnext < NG and c == min(4 * gnext + 4, NCH - 1):
                    gg = gnext
                    k = hh * NG + gg
                    t.wait_ge(S["sMkV"], hh * NCH + c + 1)
                    t.wait_ge(S["sMkG"], hh * NCH + c + 1)
                    if k >= 2:
                        t.wait_ge(S["sY"], k - 1)
                    plan = []
                    for e in range(E):
                        for cc in range(4 * gg, min(4 * gg + 5, NCH)):
                            tlo, thi = chunk_cols(cc)
                            lo2, hi2 = max(tlo, gg * 512), min(thi, gg * 512 + 511)
                            if lo2 <= hi2:
                                plan.append((e, cc, lo2, hi2))
                    for idx, (e, cc, lo2, hi2) in enumerate(plan):
                        j0 = lo2 - (cc * 128 - 32)
                        n2 = hi2 - lo2 + 1
                        mm = t.matmul(PB(4 + k % 2, lo2 - gg * 512, n2, 0, 65),
                                      vtm[:, (e * NCH + cc) * 80:(e * NCH + cc) * 80 + 65],
                                      pt[:, (e * NCH + cc) * 160 + j0:(e * NCH + cc) * 160 + j0 + n2],
                                      start=(idx == 0), stop=(idx == len(plan) - 1))
                    mm.then_inc(S["sPV"], 1)
                    gnext += 1
                    if gg >= 1:
                        kk = k - 1
                        t.wait_ge(S["sRcp"], kk + 1)
                        if kk >= 1:
                            t.wait_ge(S["sEvD"], kk)
                        t.matmul(PB(6, 0, 512, 0, 64), on1[0:1, 0:64],
                                 dinvb[0:1, (kk % 2) * 512:(kk % 2) * 512 + 512],
                                 start=True, stop=True).then_inc(S["sBcD"], 1)
            kk = hh * NG + NG - 1
            t.wait_ge(S["sRcp"], kk + 1)
            t.wait_ge(S["sEvD"], kk)
            t.matmul(PB(6, 0, 512, 0, 64), on1[0:1, 0:64],
                     dinvb[0:1, (kk % 2) * 512:(kk % 2) * 512 + 512],
                     start=True, stop=True).then_inc(S["sBcD"], 1)

    @blk.scalar
    def _(sc):
        for hh in range(H):
            for c in range(NCH):
                b0 = (c % 2) * 2
                sc.wait_ge(S["sSc"], hh * NCH + c + 1)
                dst3 = pt[:].rearrange("p (e cj) -> p e cj", e=E)[:, 0:3,
                                                                 c * 160:c * 160 + 160]
                src3 = pall[:, b0 * 512:b0 * 512 + 480].rearrange("p (e j) -> p e j", e=3)
                sc.activation(dst3, src3, AF.Exp, bias=0.0, scale=0.125)
                sc.activation(pt[:, (3 * NCH + c) * 160:(3 * NCH + c) * 160 + 160],
                              PB(b0 + 1, 0, 160), AF.Exp,
                              bias=0.0, scale=0.125).then_inc(S["sExp"], 1)
                if c >= 8 and (c - 8) % 4 == 0 and (c - 8) // 4 < NG - 1:
                    gg = (c - 8) // 4
                    k = hh * NG + gg
                    sc.wait_ge(S["sBcD"], k + 1)
                    sc.activation(dbc[0:64, (k % 2) * 512:(k % 2) * 512 + 512],
                                  PB(6, 0, 512, 0, 64), AF.Copy,
                                  bias=0.0, scale=1.0).then_inc(S["sEvD"], 1)
            for gg in [NG - 2, NG - 1]:
                k = hh * NG + gg
                sc.wait_ge(S["sBcD"], k + 1)
                sc.activation(dbc[0:64, (k % 2) * 512:(k % 2) * 512 + 512],
                              PB(6, 0, 512, 0, 64), AF.Copy,
                              bias=0.0, scale=1.0).then_inc(S["sEvD"], 1)

    @blk.vector
    def _(v):
        for hh in range(H):
            gnext = 0
            for c in range(NCH):
                v.wait_ge(S["sExp"], hh * NCH + c + 1)
                for e in range(2):
                    pc = pt[:, (e * NCH + c) * 160:(e * NCH + c) * 160 + 160]
                    ins = v.tensor_tensor(pc, pc, msk[:], ALU.mult)
                ins.then_inc(S["sMkV"], 1)
                while gnext < NG and c == min(4 * gnext + 4, NCH - 1):
                    gg = gnext
                    k = hh * NG + gg
                    v.wait_ge(S["sPV"], k + 1)
                    ins = v.reciprocal(dinv[0:1, (k % 2) * 512:(k % 2) * 512 + 512],
                                       PB(4 + k % 2, 0, 512, 64, 1))
                    VS(v, ins)
                    v.tensor_copy(dinvb[0:1, (k % 2) * 512:(k % 2) * 512 + 512],
                                  dinv[0:1, (k % 2) * 512:(k % 2) * 512 + 512]).then_inc(S["sRcp"], 1)
                    if gg >= 1:
                        kk = k - 1
                        v.wait_ge(S["sEvD"], kk + 1)
                        if kk >= 2:
                            v.wait_ge(S["dYo"], (kk - 1) * 16)
                        v.scalar_tensor_tensor(
                            ysc[0:64, (kk % 2) * 512:(kk % 2) * 512 + 512],
                            PB(4 + kk % 2, 0, 512, 0, 64), 1.0,
                            dbc[0:64, (kk % 2) * 512:(kk % 2) * 512 + 512],
                            ALU.mult, ALU.mult).then_inc(S["sY"], 1)
                    gnext += 1
            kk = hh * NG + NG - 1
            v.wait_ge(S["sEvD"], kk + 1)
            v.wait_ge(S["dYo"], (kk - 1) * 16)
            v.scalar_tensor_tensor(ysc[0:64, (kk % 2) * 512:(kk % 2) * 512 + 512],
                                   PB(4 + kk % 2, 0, 512, 0, 64), 1.0,
                                   dbc[0:64, (kk % 2) * 512:(kk % 2) * 512 + 512],
                                   ALU.mult, ALU.mult).then_inc(S["sY"], 1)

    @blk.gpsimd
    def _(gp):
        for hh in range(H):
            for c in range(NCH):
                gp.wait_ge(S["sExp"], hh * NCH + c + 1)
                for e in range(2, 4):
                    pc = pt[:, (e * NCH + c) * 160:(e * NCH + c) * 160 + 160]
                    ins = gp.tensor_tensor(pc, pc, msk[:], ALU.mult)
                ins.then_inc(S["sMkG"], 1)

    @blk.sync
    def _(s):
        for hh in range(H):
            for gg in range(NG):
                k = hh * NG + gg
                s.wait_ge(S["sY"], k + 1)
                s.dma_start(out=scr_y[hh // 2][(hh % 2) * 64:(hh % 2) * 64 + 64,
                                               gg * 512:gg * 512 + 512],
                            in_=ysc[0:64, (k % 2) * 512:(k % 2) * 512 + 512]).then_inc(S["dYo"], 16)
            if hh + 2 < H:
                h2 = hh + 2
                s.wait_ge(S["sSc"], (h2 - 1) * NCH)
                s.dma_start(out=qh[0:64, (h2 % 2) * TOWN:(h2 % 2) * TOWN + TOWN],
                            in_=scr_q[h2 // 2][(h2 % 2) * 64:(h2 % 2) * 64 + 64, :]).then_inc(S["dQi"], 16)

    if debug:
        @blk.gpsimd
        def _(gp):
            gp.wait_ge(S["sMkV"], H * NCH)
            gp.wait_ge(S["sMkG"], H * NCH)
            gp.dma_start(out=dbg_pt, in_=pt[:]).then_inc(S["dDbg"], 16)
            gp.wait_ge(S["dDbg"], 7 * 16)

    esB3.close()
    esB2.close()
    esC = contextlib.ExitStack()
    wo_sb = esC.enter_context(nc.sbuf_tensor("wo_sb", [128, 4 * 1024], BF16))
    ybuf = esC.enter_context(nc.sbuf_tensor("ybuf", [128, 2048], BF16))
    attn_fm = esC.enter_context(nc.sbuf_tensor("attn_fm", [128, 8 * 512], BF16))
    attn_tm = esC.enter_context(nc.sbuf_tensor("attn_tm", [128, 6 * 1024], BF16,
                                               is_transpose_dest=True))
    xres = esC.enter_context(nc.sbuf_tensor("xres", [128, 2 * 1024], F32))
    rres = esC.enter_context(nc.sbuf_tensor("rres", [128, 2 * 1024], F32))
    x1_tm = esC.enter_context(nc.sbuf_tensor("x1_tm", [128, 2 * 1024], F32))
    x1b = esC.enter_context(nc.sbuf_tensor("x1b", [128, 2 * 1024], BF16))
    x1t = esC.enter_context(nc.sbuf_tensor("x1t", [128, 2 * 1024], BF16,
                                           is_transpose_dest=True))
    r_fm = esC.enter_context(nc.sbuf_tensor("r_fm", [8, TOWN], BF16))
    x1g = esC.enter_context(nc.sbuf_tensor("x1g", [128, 2 * 4096], BF16))
    h1g = esC.enter_context(nc.sbuf_tensor("h1g", [128, 16 * 512], BF16))
    h1p = esC.enter_context(nc.sbuf_tensor("h1p", [128, 2 * 512], BF16))
    r_bc = esC.enter_context(nc.sbuf_tensor("r_bc", [128, 2 * 2048], BF16))
    dgr = esC.enter_context(nc.sbuf_tensor("dgr", [128, 256], BF16))
    o1ev = esC.enter_context(nc.sbuf_tensor("o1ev", [128, 2 * 512], BF16))
    o1in = esC.enter_context(nc.sbuf_tensor("o1in", [128, 2 * 512], BF16))
    ffn_fm = esC.enter_context(nc.sbuf_tensor("ffn_fm", [128, 2 * 512], BF16))
    ffn_tm = esC.enter_context(nc.sbuf_tensor("ffn_tm", [128, 4 * 1024], BF16,
                                              is_transpose_dest=True))
    esF = contextlib.ExitStack()
    w1h = esF.enter_context(nc.sbuf_tensor("w1h", [128, NDC * 2048], BF16))
    w2h = esF.enter_context(nc.sbuf_tensor("w2h", [128, 16 * 1024], BF16))

    @blk.gpsimd
    def _(gp):
        gp.wait_ge(S["sY"], 64)
        for qc in range(4):
            gp.dma_start(out=wo_sb[:, qc * 1024:qc * 1024 + 1024],
                         in_=wod[qc * 128:qc * 128 + 128, :]).then_inc(S["dW"], 16)
        for half in range(2):
            if half == 1:
                gp.wait_ge(S["sW2"], 64)
            for dc in range(NDC):
                gp.dma_start(out=w1h[:, dc * 2048:dc * 2048 + 2048],
                             in_=w1d[dc * 128:dc * 128 + 128,
                                     half * 2048:half * 2048 + 2048]).then_inc(S["dWF"], 16)
            for fl in range(16):
                gp.dma_start(out=w2h[:, fl * 1024:fl * 1024 + 1024],
                             in_=w2d[(half * 16 + fl) * 128:(half * 16 + fl) * 128 + 128, :]).then_inc(S["dWF"], 16)

    scr_y_r = scr_y.rearrange("q p t -> p q t")
    scr_xf_r = scr_xf.rearrange("c p t -> p c t")

    @blk.sync
    def _(s):
        s.wait_ge(S["dYo"], 64 * 16)
        s.dma_start(out=ybuf[:, 0:2048],
                    in_=scr_y_r[:, :, 0:512]).then_inc(S["dYi"], 16)
        def _p5store(s, tt):
            s.wait_ge(S["sNrm1"], tt + 1)
            s.dma_start(out=scr_x1[tt * 128:tt * 128 + 128, :],
                        in_=x1_tm[:, (tt % 2) * 1024:(tt % 2) * 1024 + 1024]).then_inc(S["dX1s"], 16)
            s.wait_ge(S["dX1T"], (tt + 1) * 128)
            s.dma_start(out=scr_xf_r[:, :, tt * 128:tt * 128 + 128],
                        in_=x1t[:, (tt % 2) * 1024:(tt % 2) * 1024 + 1024]).then_inc(S["dXFs"], 16)
        for gg in range(NG):
            if gg >= 1:
                for s4 in range(4):
                    _p5store(s, (gg - 1) * 4 + s4)
            if gg >= 1:
                s.wait_ge(S["sRes1"], max(0, 4 * gg - 2))
            for dc in range(NDC):
                k = gg * NDC + dc
                s.wait_ge(S["sEvOP"], k + 1)
                for s4 in range(4):
                    tt = gg * 4 + s4
                    s.dma_start(out=attn_tm[:, (tt % 6) * 1024 + dc * 128:(tt % 6) * 1024 + dc * 128 + 128],
                                 in_=attn_fm[:, (k % 8) * 512 + s4 * 128:(k % 8) * 512 + s4 * 128 + 128],
                                 transpose=True).then_inc(S["dXF"], 16)
            for s4 in range(4):
                tt = gg * 4 + s4
                if tt >= 2:
                    s.wait_ge(S["sRes1"], tt - 1)
                s.dma_start(out=xres[:, (tt % 2) * 1024:(tt % 2) * 1024 + 1024],
                            in_=xs[256 + tt * 128:256 + tt * 128 + 128, :]).then_inc(S["dXR"], 16)
            if gg + 1 < NG:
                s.wait_ge(S["sOP"], (gg + 1) * NDC)
                s.dma_start(out=ybuf[:, 0:2048],
                            in_=scr_y_r[:, :, (gg + 1) * 512:(gg + 1) * 512 + 512]).then_inc(S["dYi"], 16)
        for s4 in range(4):
            _p5store(s, (NG - 1) * 4 + s4)
        no1 = 0
        for half in range(2):
            for gg in range(NG):
                hg = half * NG + gg
                s.wait_ge(S["dXFs"], (4 * gg + 4) * 16)
                if hg >= 2:
                    s.wait_ge(S["sH1"], (hg - 1) * 16)
                s.dma_start(out=x1g[:, (gg % 2) * 4096:(gg % 2) * 4096 + 4096],
                            in_=scr_xf_r[:, :, gg * 512:gg * 512 + 512]).then_inc(S["dXG"], 16)
                if half == 1 and gg >= 1:
                    for s4 in range(4):
                        tt = (gg - 1) * 4 + s4
                        if tt >= 2:
                            s.wait_ge(S["sAdd2"], tt - 1)
                        else:
                            s.wait_ge(S["dX1s"], NTO * 16)
                        s.dma_start(out=x1_tm[:, (tt % 2) * 1024:(tt % 2) * 1024 + 1024],
                                    in_=scr_x1[tt * 128:tt * 128 + 128, :]).then_inc(S["dXR"], 16)
                    for s4 in range(4):
                        tt = (gg - 1) * 4 + s4
                        s.wait_ge(S["sNrm2"], tt + 1)
                        s.dma_start(out=y[tt * 128:tt * 128 + 128, :],
                                    in_=xres[:, (tt % 2) * 1024:(tt % 2) * 1024 + 1024]).then_inc(S["dY"], 16)
                def _ffnTd(s, gg, dc):
                    ko = gg * NDC + dc
                    s.wait_ge(S["sEvW2"], 64 + ko + 1)
                    for s4 in range(4):
                        tt = gg * 4 + s4
                        s.dma_start(out=ffn_tm[:, (tt % 4) * 1024 + dc * 128:(tt % 4) * 1024 + dc * 128 + 128],
                                    in_=ffn_fm[:, (ko % 2) * 512 + s4 * 128:(ko % 2) * 512 + s4 * 128 + 128],
                                    transpose=True).then_inc(S["dFT"], 16)
                for dc in range(NDC):
                    k2 = gg * NDC + dc
                    if half == 0:
                        s.wait_ge(S["sEvW2"], k2 + 1)
                        s.dma_start(out=scr_o1[dc][0:128, gg * 512:gg * 512 + 512],
                                    in_=o1ev[:, (k2 % 2) * 512:(k2 % 2) * 512 + 512]).then_inc(S["dO1o"], 16)
                    else:
                        if no1 >= 2:
                            s.wait_ge(S["sEvW2"], 64 + no1 - 1)
                        s.dma_start(out=o1in[:, (k2 % 2) * 512:(k2 % 2) * 512 + 512],
                                    in_=scr_o1[dc][0:128, gg * 512:gg * 512 + 512]).then_inc(S["dO1i"], 16)
                        no1 += 1
                        if dc == 2 and gg >= 1:
                            s.wait_ge(S["sAdd2"], gg * 4)
                        if dc >= 2:
                            _ffnTd(s, gg, dc - 2)
                if half == 1:
                    _ffnTd(s, gg, NDC - 2)
                    _ffnTd(s, gg, NDC - 1)
        for tt in range((NG - 1) * 4, NTO):
            s.wait_ge(S["sAdd2"], tt - 1)
            s.dma_start(out=x1_tm[:, (tt % 2) * 1024:(tt % 2) * 1024 + 1024],
                        in_=scr_x1[tt * 128:tt * 128 + 128, :]).then_inc(S["dXR"], 16)
        for tt in range((NG - 1) * 4, NTO):
            s.wait_ge(S["sNrm2"], tt + 1)
            s.dma_start(out=y[tt * 128:tt * 128 + 128, :],
                        in_=xres[:, (tt % 2) * 1024:(tt % 2) * 1024 + 1024]).then_inc(S["dY"], 16)
        s.wait_ge(S["dY"], NTO * 16)
        if debug:
            s.dma_start(out=dbg_r, in_=r_tm[:]).then_inc(S["dDbg"], 16)
            s.wait_ge(S["dDbg"], 8 * 16)

    @blk.tensor
    def _(t):
        if debug:
            t.wait_ge(S["dDbg"], 7 * 16)
        t.wait_ge(S["dW"], (NW0 + 4) * 16)
        t.wait_ge(S["sExp"], H * NCH)
        for gg in range(NG):
            t.wait_ge(S["dYi"], (gg + 1) * 16)
            for dc in range(NDC):
                k = gg * NDC + dc
                if k >= 2:
                    t.wait_ge(S["sEvOP"], k - 1)
                for qc in range(4):
                    mm = t.matmul(PB(k % 2), wo_sb[:, qc * 1024 + dc * 128:qc * 1024 + dc * 128 + 128],
                                  ybuf[:, qc * 512:qc * 512 + 512],
                                  start=(qc == 0), stop=(qc == 3))
                mm.then_inc(S["sOP"], 1)
        t.wait_ge(S["sEvD"], 64)
        t.wait_ge(S["sEvOP"], 64)
        for half in range(2):
            for gg in range(NG):
                hg = half * NG + gg
                t.wait_ge(S["dXG"], (hg + 1) * 16)
                if half == 0:
                    for s4 in range(4):
                        tt = gg * 4 + s4
                        if tt >= 1:
                            t.wait_ge(S["sEvRt"], tt)
                        for dc in range(NDC):
                            t.matmul(PB(6, 0, 8),
                                     x1g[:, (gg % 2) * 4096 + dc * 512 + s4 * 128:(gg % 2) * 4096 + dc * 512 + s4 * 128 + 128],
                                     rw_sb[:, dc * 8:dc * 8 + 8],
                                     start=(dc == 0), stop=False)
                        t.matmul(PB(6, 0, 8), on1[0:1, 0:128], rb_sb[0:1, :],
                                 start=False, stop=True).then_inc(S["sRt"], 1)
                    for s4 in range(4):
                        tt = gg * 4 + s4
                        t.wait_ge(S["sTop"], tt + 1)
                        if tt >= 1:
                            t.wait_ge(S["sEvRT"], tt)
                        t.transpose(pbf[0:8, 0:128], rcast[:, (tt % 2) * 8:(tt % 2) * 8 + 8],
                                    i4[:, 0:128]).then_inc(S["sRT"], 1)
                for el in range(4):
                    m = hg * 4 + el
                    if m >= 2:
                        t.wait_ge(S["sEvR"], m - 1)
                    for s4 in range(4):
                        j = m * 4 + s4
                        t.wait_ge(S["sDgR"], j + 1)
                        t.matmul(PB(0 + m % 2, s4 * 128, 128), onb[:],
                                 dgr[:, (j % 2) * 128:(j % 2) * 128 + 128],
                                 start=True, stop=True).then_inc(S["sBcR"], 1)
                for fl in range(16):
                    kf = hg * 16 + fl
                    t.wait_ge(S["dWF"], (half * 24 + 8) * 16)
                    if kf >= 2:
                        t.wait_ge(S["sGel"], kf - 1)
                    for dc in range(NDC):
                        mm = t.matmul(PB(2 + fl % 2),
                                      w1h[:, dc * 2048 + fl * 128:dc * 2048 + fl * 128 + 128],
                                      x1g[:, (gg % 2) * 4096 + dc * 512:(gg % 2) * 4096 + dc * 512 + 512],
                                      start=(dc == 0), stop=(dc == NDC - 1))
                    mm.then_inc(S["sH1"], 1)
                t.wait_ge(S["dWF"], (half + 1) * 24 * 16)
                t.wait_ge(S["sScl"], hg * 16 + 16)
                for dc in range(NDC):
                    k2 = hg * NDC + dc
                    if k2 >= 2:
                        t.wait_ge(S["sEvW2"], k2 - 1)
                    for fl in range(16):
                        mm = t.matmul(PB(4 + k2 % 2),
                                      w2h[:, fl * 1024 + dc * 128:fl * 1024 + dc * 128 + 128],
                                      h1g[:, fl * 512:fl * 512 + 512],
                                      start=(fl == 0), stop=(half == 0 and fl == 15))
                    if half == 1:
                        t.wait_ge(S["sEvRT"], NTO)
                        mm = t.matmul(PB(4 + k2 % 2), b2_sb[0:8, dc * 128:dc * 128 + 128],
                                      r_fm[0:8, gg * 512:gg * 512 + 512],
                                      start=False, stop=True)
                    mm.then_inc(S["sW2"], 1)

    @blk.scalar
    def _(sc):
        def _p5act(sc, tt):
            sb0 = (tt % 2) * 24
            sc.wait_ge(S["sBN1"], tt + 1)
            sc.activation(stat[:, sb0 + 14:sb0 + 15], stat[:, sb0 + 1:sb0 + 2],
                          AF.Sqrt, bias=epsb[:, 0:1], scale=1.0).then_inc(S["sSq1"], 1)
            sc.wait_ge(S["sRst1"], tt + 1)
            if tt >= 2:
                sc.wait_ge(S["dX1s"], (tt - 1) * 16)
            sc.activation(x1_tm[:, (tt % 2) * 1024:(tt % 2) * 1024 + 1024],
                          rres[:, (tt % 2) * 1024:(tt % 2) * 1024 + 1024],
                          AF.Identity, bias=stat[:, sb0 + 16:sb0 + 17],
                          scale=stat[:, sb0 + 15:sb0 + 16]).then_inc(S["sNrm1"], 1)
            sc.wait_ge(S["sX1b"], tt + 1)
            if tt >= 2:
                sc.wait_ge(S["dXFs"], (tt - 1) * 16)
            for dc in range(NDC):
                sc.dma_start(out=x1t[:, (tt % 2) * 1024 + dc * 128:(tt % 2) * 1024 + dc * 128 + 128],
                             in_=x1b[:, (tt % 2) * 1024 + dc * 128:(tt % 2) * 1024 + dc * 128 + 128],
                             transpose=True).then_inc(S["dX1T"], 16)
        for gg in range(NG):
            if gg >= 1:
                for s4 in range(4):
                    _p5act(sc, (gg - 1) * 4 + s4)
        for s4 in range(4):
            _p5act(sc, (NG - 1) * 4 + s4)
        def _sqrt2(sc, tt):
            sb0 = 48 + (tt % 2) * 24
            sc.wait_ge(S["sBN2"], tt + 1)
            sc.activation(stat[:, sb0 + 14:sb0 + 15], stat[:, sb0 + 1:sb0 + 2],
                          AF.Sqrt, bias=epsb[:, 0:1], scale=1.0).then_inc(S["sSq2"], 1)
        def _nrm2(sc, tt):
            sb0 = 48 + (tt % 2) * 24
            sc.wait_ge(S["sRst2"], tt + 1)
            if tt < 2:
                sc.wait_ge(S["sRes1"], NTO)
            else:
                sc.wait_ge(S["dY"], (tt - 1) * 16)
            sc.activation(xres[:, (tt % 2) * 1024:(tt % 2) * 1024 + 1024],
                          rres[:, (tt % 2) * 1024:(tt % 2) * 1024 + 1024],
                          AF.Identity, bias=stat[:, sb0 + 16:sb0 + 17],
                          scale=stat[:, sb0 + 15:sb0 + 16]).then_inc(S["sNrm2"], 1)
        for half in range(2):
            for gg in range(NG):
                hg = half * NG + gg
                if half == 1 and gg >= 1:
                    for s4 in range(4):
                        _sqrt2(sc, (gg - 1) * 4 + s4)
                        _nrm2(sc, (gg - 1) * 4 + s4)
                if half == 0:
                    for s4 in range(4):
                        tt = gg * 4 + s4
                        sc.wait_ge(S["sRtd"], tt + 1)
                        sc.activation(rstat[:, (tt % 4) * 32 + 3:(tt % 4) * 32 + 4],
                                      rstat[:, (tt % 4) * 32 + 2:(tt % 4) * 32 + 3],
                                      AF.Sigmoid, bias=0.0, scale=1.0).then_inc(S["sSig"], 1)
                for el in range(4):
                    m = hg * 4 + el
                    sc.wait_ge(S["sBcR"], (m + 1) * 4)
                    if hg >= 2:
                        sc.wait_ge(S["sScl"], (hg - 1) * 16)
                    sc.activation(r_bc[:, (hg % 2) * 2048 + el * 512:(hg % 2) * 2048 + el * 512 + 512],
                                  PB(0 + m % 2), AF.Copy, bias=0.0,
                                  scale=1.0).then_inc(S["sEvR"], 1)
                for fl in range(16):
                    kf = hg * 16 + fl
                    sc.wait_ge(S["sH1"], kf + 1)
                    if kf >= 2:
                        sc.wait_ge(S["sScl"], kf - 1)
                    sc.activation(h1p[:, (kf % 2) * 512:(kf % 2) * 512 + 512],
                                  PB(2 + fl % 2), AF.Gelu,
                                  bias=b1_sb[:, half * 16 + fl:half * 16 + fl + 1],
                                  scale=1.0).then_inc(S["sGel"], 1)
        for s4 in range(4):
            _sqrt2(sc, (NG - 1) * 4 + s4)
            _nrm2(sc, (NG - 1) * 4 + s4)

    @blk.vector
    def _(v):
        def _p5tile(v, tt):
            gg = tt // 4
            if True:
                sb0 = (tt % 2) * 24
                v.wait_ge(S["dXF"], (gg + 1) * 32 * 16)
                v.wait_ge(S["dXR"], (tt + 1) * 16)
                if tt >= 2:
                    v.wait_ge(S["sNrm1"], tt - 1)
                v.tensor_tensor(rres[:, (tt % 2) * 1024:(tt % 2) * 1024 + 1024],
                                xres[:, (tt % 2) * 1024:(tt % 2) * 1024 + 1024],
                                attn_tm[:, (tt % 6) * 1024:(tt % 6) * 1024 + 1024],
                                ALU.add).then_inc(S["sRes1"], 1)
                v.wait_ge(S["sRes1"], tt + 1)
                v.bn_stats(stat[:, sb0 + 2:sb0 + 8],
                           rres[:, (tt % 2) * 1024:(tt % 2) * 1024 + 512])
                ins = v.bn_stats(stat[:, sb0 + 8:sb0 + 14],
                                 rres[:, (tt % 2) * 1024 + 512:(tt % 2) * 1024 + 1024])
                VS(v, ins)
                v.bn_aggr(stat[:, sb0:sb0 + 2], stat[:, sb0 + 2:sb0 + 14]).then_inc(S["sBN1"], 1)
                v.wait_ge(S["sSq1"], tt + 1)
                ins = v.reciprocal(stat[:, sb0 + 15:sb0 + 16], stat[:, sb0 + 14:sb0 + 15])
                VS(v, ins)
                v.tensor_scalar(stat[:, sb0 + 16:sb0 + 17], stat[:, sb0:sb0 + 1],
                                stat[:, sb0 + 15:sb0 + 16], -1.0,
                                ALU.mult, ALU.mult).then_inc(S["sRst1"], 1)
                v.wait_ge(S["sNrm1"], tt + 1)
                if tt >= 2:
                    v.wait_ge(S["dX1T"], (tt - 1) * 128)
                v.tensor_copy(x1b[:, (tt % 2) * 1024:(tt % 2) * 1024 + 1024],
                              x1_tm[:, (tt % 2) * 1024:(tt % 2) * 1024 + 1024]).then_inc(S["sX1b"], 1)
        for gg in range(NG):
            if gg >= 1:
                for s4 in range(4):
                    _p5tile(v, (gg - 1) * 4 + s4)
            for dc in range(NDC):
                k = gg * NDC + dc
                v.wait_ge(S["sOP"], k + 1)
                if k >= 8:
                    v.wait_ge(S["dXF"], (k - 7) * 64)
                v.tensor_copy(attn_fm[:, (k % 8) * 512:(k % 8) * 512 + 512],
                              PB(k % 2)).then_inc(S["sEvOP"], 1)
        for s4 in range(4):
            _p5tile(v, (NG - 1) * 4 + s4)
        for s4 in range(4):
            tt = (NG - 1) * 4 + s4
            v.wait_ge(S["sNrm1"], tt + 1)
            v.wait_ge(S["dX1T"], (tt - 1) * 128)
            v.tensor_copy(x1b[:, (tt % 2) * 1024:(tt % 2) * 1024 + 1024],
                          x1_tm[:, (tt % 2) * 1024:(tt % 2) * 1024 + 1024]).then_inc(S["sX1b"], 1)
        def _p7tile(v, tt):
            sb0 = 48 + (tt % 2) * 24
            gg7 = tt // 4
            v.wait_ge(S["dFT"], (gg7 + 1) * 32 * 16)
            v.wait_ge(S["dXR"], (NTO + tt + 1) * 16)
            if tt >= 2:
                v.wait_ge(S["sNrm2"], tt - 1)
            else:
                v.wait_ge(S["sNrm1"], NTO)
            v.tensor_tensor(rres[:, (tt % 2) * 1024:(tt % 2) * 1024 + 1024],
                            x1_tm[:, (tt % 2) * 1024:(tt % 2) * 1024 + 1024],
                            ffn_tm[:, (tt % 4) * 1024:(tt % 4) * 1024 + 1024],
                            ALU.add).then_inc(S["sAdd2"], 1)
            v.wait_ge(S["sAdd2"], tt + 1)
            v.bn_stats(stat[:, sb0 + 2:sb0 + 8],
                       rres[:, (tt % 2) * 1024:(tt % 2) * 1024 + 512])
            ins = v.bn_stats(stat[:, sb0 + 8:sb0 + 14],
                             rres[:, (tt % 2) * 1024 + 512:(tt % 2) * 1024 + 1024])
            VS(v, ins)
            v.bn_aggr(stat[:, sb0:sb0 + 2], stat[:, sb0 + 2:sb0 + 14]).then_inc(S["sBN2"], 1)
            v.wait_ge(S["sSq2"], tt + 1)
            ins = v.reciprocal(stat[:, sb0 + 15:sb0 + 16], stat[:, sb0 + 14:sb0 + 15])
            VS(v, ins)
            v.tensor_scalar(stat[:, sb0 + 16:sb0 + 17], stat[:, sb0:sb0 + 1],
                            stat[:, sb0 + 15:sb0 + 16], -1.0,
                            ALU.mult, ALU.mult).then_inc(S["sRst2"], 1)
        for half in range(2):
            for gg in range(NG):
                hg = half * NG + gg
                if half == 1 and gg >= 1:
                    for s4 in range(4):
                        _p7tile(v, (gg - 1) * 4 + s4)
                if half == 0:
                    for s4 in range(4):
                        tt = gg * 4 + s4
                        rs = (tt % 4) * 32
                        v.wait_ge(S["sRt"], tt + 1)
                        if tt >= 4:
                            v.wait_ge(S["sTop"], tt - 3)
                        v.tensor_copy(lbuf[:, (tt % 2) * 8:(tt % 2) * 8 + 8],
                                      PB(6, 0, 8)).then_inc(S["sEvRt"], 1)
                        v.wait_ge(S["sEvRt"], tt + 1)
                        L = lbuf[:, (tt % 2) * 8:(tt % 2) * 8 + 8]
                        ins = v.reduce_max(rstat[:, rs:rs + 1], L, axis=AX)
                        VS(v, ins)
                        ins = v.tensor_scalar(rstat[:, rs + 8:rs + 16], L,
                                              rstat[:, rs:rs + 1], None, ALU.is_ge)
                        VS(v, ins)
                        ins = v.scalar_tensor_tensor(rstat[:, rs + 16:rs + 24],
                                                     rstat[:, rs + 8:rs + 16], -1e30, L,
                                                     ALU.mult, ALU.add)
                        VS(v, ins)
                        ins = v.reduce_max(rstat[:, rs + 1:rs + 2],
                                           rstat[:, rs + 16:rs + 24], axis=AX)
                        VS(v, ins)
                        v.tensor_sub(rstat[:, rs + 2:rs + 3], rstat[:, rs:rs + 1],
                                     rstat[:, rs + 1:rs + 2]).then_inc(S["sRtd"], 1)
                    for s4 in range(4):
                        tt = gg * 4 + s4
                        rs = (tt % 4) * 32
                        v.wait_ge(S["sSig"], tt + 1)
                        v.tensor_scalar(rstat[:, rs + 4:rs + 5], rstat[:, rs + 3:rs + 4],
                                        -1.0, 1.0, ALU.mult, ALU.add)
                        ins = v.tensor_scalar(rstat[:, rs + 24:rs + 32],
                                              rstat[:, rs + 16:rs + 24],
                                              rstat[:, rs + 1:rs + 2], None, ALU.is_ge)
                        VS(v, ins)
                        ins = v.tensor_scalar(rstat[:, rs + 16:rs + 24],
                                              rstat[:, rs + 8:rs + 16],
                                              rstat[:, rs + 3:rs + 4], None, ALU.mult)
                        VS(v, ins)
                        ins = v.scalar_tensor_tensor(r_tm[:, tt * 8:tt * 8 + 8],
                                                     rstat[:, rs + 24:rs + 32],
                                                     rstat[:, rs + 4:rs + 5],
                                                     rstat[:, rs + 16:rs + 24],
                                                     ALU.mult, ALU.add)
                        VS(v, ins)
                        v.tensor_copy(rcast[:, (tt % 2) * 8:(tt % 2) * 8 + 8],
                                      r_tm[:, tt * 8:tt * 8 + 8]).then_inc(S["sTop"], 1)
                        v.wait_ge(S["sRT"], tt + 1)
                        v.tensor_copy(r_fm[0:8, tt * 128:tt * 128 + 128],
                                      pbf[0:8, 0:128]).then_inc(S["sEvRT"], 1)
                for el in range(4):
                    m = hg * 4 + el
                    for s4 in range(4):
                        j = m * 4 + s4
                        tt = gg * 4 + s4
                        if j >= 2:
                            v.wait_ge(S["sBcR"], j - 1)
                        v.tensor_scalar(dgr[:, (j % 2) * 128:(j % 2) * 128 + 128],
                                        i4[:, 0:128],
                                        r_tm[:, tt * 8 + half * 4 + el:tt * 8 + half * 4 + el + 1],
                                        None, ALU.mult).then_inc(S["sDgR"], 1)
                for fl in range(16):
                    kf = hg * 16 + fl
                    el = fl // 4
                    v.wait_ge(S["sGel"], kf + 1)
                    v.wait_ge(S["sEvR"], hg * 4 + el + 1)
                    v.tensor_tensor(h1g[:, fl * 512:fl * 512 + 512],
                                    h1p[:, (kf % 2) * 512:(kf % 2) * 512 + 512],
                                    r_bc[:, (hg % 2) * 2048 + el * 512:(hg % 2) * 2048 + el * 512 + 512],
                                    ALU.mult).then_inc(S["sScl"], 1)
                for dc in range(NDC):
                    k2 = hg * NDC + dc
                    v.wait_ge(S["sW2"], k2 + 1)
                    if half == 0:
                        if k2 >= 2:
                            v.wait_ge(S["dO1o"], (k2 - 1) * 16)
                        v.tensor_copy(o1ev[:, (k2 % 2) * 512:(k2 % 2) * 512 + 512],
                                      PB(4 + k2 % 2)).then_inc(S["sEvW2"], 1)
                    else:
                        ko = gg * NDC + dc
                        v.wait_ge(S["dO1i"], (ko + 1) * 16)
                        if ko >= 2:
                            v.wait_ge(S["dFT"], (ko - 1) * 64)
                        v.tensor_tensor(ffn_fm[:, (ko % 2) * 512:(ko % 2) * 512 + 512],
                                        PB(4 + k2 % 2),
                                        o1in[:, (ko % 2) * 512:(ko % 2) * 512 + 512],
                                        ALU.add).then_inc(S["sEvW2"], 1)
        for tt in range((NG - 1) * 4, NTO):
            _p7tile(v, tt)

    esF.close()
    esC.close()
    return nc, es


_CACHE = {}


def _host_arrays(inputs):
    X = np.asarray(inputs["L0_attn_X"])
    w_G = np.asarray(inputs["L0_attn_w_G"])
    b_G = np.asarray(inputs["L0_attn_b_G"])
    w_K = np.asarray(inputs["L0_attn_w_K"])
    w_V = np.asarray(inputs["L0_attn_w_V"])
    w_Q = np.asarray(inputs["L0_attn_w_Q"])
    w_O = np.asarray(inputs["L0_attn_w_O"])
    iK = np.asarray(inputs["L0_attn_init_K_rec"])
    iV = np.asarray(inputs["L0_attn_init_V_rec"])
    rw = np.asarray(inputs["L0_ffn_router_w"])
    rb = np.asarray(inputs["L0_ffn_router_b"])
    W1 = np.asarray(inputs["L0_ffn_W1"])
    b1 = np.asarray(inputs["L0_ffn_b1"])
    W2 = np.asarray(inputs["L0_ffn_W2"])
    b2 = np.asarray(inputs["L0_ffn_b2"])

    f = np.float32
    wkvg = np.empty((1024, 1056), f)
    for h in range(H):
        wkvg[:, h * 128:h * 128 + 64] = w_K[h].T
        wkvg[:, h * 128 + 64:h * 128 + 128] = w_V[h].T
    for e in range(E):
        for h in range(H):
            wkvg[:, 1024 + e * 8 + h] = w_G[h, e]
    wq = w_Q.transpose(2, 0, 1).reshape(1024, 512).astype(f)
    w1c = np.ascontiguousarray(W1.transpose(1, 0, 2).reshape(1024, 4096), f)
    w2c = np.ascontiguousarray(W2.reshape(4096, 1024), f)
    b1h = np.ascontiguousarray(b1.reshape(4096).reshape(32, 128).T, f)
    bgrow = np.empty(32, f)
    for e in range(E):
        for h in range(H):
            bgrow[e * 8 + h] = b_G[h, e]
    bgd = np.ascontiguousarray(np.broadcast_to(bgrow, (128, 32)), f)
    mskd = np.zeros((128, 160), f)
    for p in range(128):
        mskd[p, p:p + 32] = 1.0
    i4d = np.ascontiguousarray(np.tile(np.eye(128, dtype=f), (1, 4)))
    idfd = np.eye(128, dtype=f)
    on1d = np.ones((1, 128), f)
    ond2 = np.ones((128, 128), f)
    extd = np.empty((128, 128), f)
    for e in range(E):
        extd[0:64, e * 32:e * 32 + 32] = iK[e].T
        extd[64:128, e * 32:e * 32 + 32] = iV[e].T

    common = dict(wkvg=np.ascontiguousarray(wkvg), wqd=np.ascontiguousarray(wq),
                  wod=np.ascontiguousarray(w_O.astype(f)), w1d=w1c, w2d=w2c,
                  b1d=b1h, b2d=np.ascontiguousarray(b2.astype(f)),
                  rwd=np.ascontiguousarray(rw.astype(f)),
                  rbd=np.ascontiguousarray(rb.reshape(1, 8).astype(f)),
                  extd=extd, bgd=bgd, mskd=mskd, i4d=i4d, idfd=idfd,
                  on1d=on1d, ond2=ond2, epsd=np.full((128, 1), 1e-5, f))

    in_maps = []
    for c in range(8):
        n, half = divmod(c, 2)
        t0 = half * TOWN
        halo = X[n, t0 - 256:t0] if half == 1 else X[n, 0:256]
        xs_c = np.ascontiguousarray(np.concatenate([halo, X[n, t0:t0 + TOWN]], 0), f)
        lanes = ([15] + list(range(16, 32))) if half == 1 else ([0] + list(range(0, 16)))
        injd = np.empty((128, 68), f)
        for e in range(E):
            for li, lg in enumerate(lanes):
                injd[0:64, e * 17 + li] = iK[e, lg]
                injd[64:128, e * 17 + li] = iV[e, lg]
        fhd = np.full((128, 1), 1.0 if half == 0 else 0.0, f)
        m = dict(common)
        m.update(xs=xs_c, injd=injd, fhd=fhd)
        in_maps.append(m)
    return in_maps


def kernel(**inputs):
    if "nc" not in _CACHE:
        _CACHE["nc"] = build_nc()
    nc = _CACHE["nc"][0]
    in_maps = _host_arrays(inputs)
    res = run_bass_kernel_spmd(nc, in_maps, core_ids=list(range(8)))
    out = np.empty((4, 8192, 1024), np.float32)
    for c in range(8):
        n, half = divmod(c, 2)
        out[n, half * TOWN:(half + 1) * TOWN] = res.results[c]["y"]
    return out
